# version 05
# baseline (speedup 1.0000x reference)
import numpy as np
from ml_dtypes import bfloat16
import concourse.bass as bass
import concourse.mybir as mybir

F32 = mybir.dt.float32
BF16 = mybir.dt.bfloat16
U8 = mybir.dt.uint8
AF = mybir.ActivationFunctionType
ALU = mybir.AluOpType
AX = mybir.AxisListType

T = 2048
D = 1024
NCH = 8
TB = 512
NTB = T // TB
EPS = 1e-5
NEG = -1e30


class Sem:
    __slots__ = ("name", "n", "h")
    def __init__(self, name):
        self.name, self.n, self.h = name, 0, None


class Tile:
    __slots__ = ("kb", "boff", "rows", "cols", "dt")
    def __init__(self, kb, boff, rows, cols, dt):
        self.kb, self.boff, self.rows, self.cols, self.dt = kb, boff, rows, cols, dt

    def ap(self):
        esz = 2 if self.dt == BF16 else 4
        a = self.kb.arena_ap[:, self.boff:self.boff + self.cols * esz].bitcast(self.dt)
        if self.rows != 128:
            a = a[0:self.rows, :]
        return a

    def __getitem__(self, key):
        return self.ap()[key]


class KB:
    ENGINES = ("sync", "tensor", "vector", "scalar", "gpsimd")

    def __init__(self, nc, arena_bytes=211456):
        import contextlib
        self.nc = nc
        self.streams = {e: [] for e in self.ENGINES}
        self.sems = []
        self.arena_bytes = arena_bytes
        self.arena_off = 0
        self.arena_hwm = 0
        self._stack = contextlib.ExitStack()
        arena_t = self._stack.enter_context(nc.sbuf_tensor("ARENA", [128, arena_bytes], U8))
        self.arena_ap = arena_t.ap()
        ps_t = self._stack.enter_context(nc.psum_tensor("PS", [128, 4096], F32))
        self.ps = ps_t.ap()
        self.LD = self.sem("LD")
        self.WL = self.sem("WL")
        self.SD = self.sem("SD")
        self.PE = self.sem("PE")
        self.DV = self.sem("DV")
        self.AC = self.sem("AC")
        self.GP = self.sem("GP")
        self.consts = {}
        self.bank_guards = {}
        self.sealed = False
        self._pool = {"gp": [self.sem(f"SG{i}") for i in range(8)],
                      "gpw": [self.sem(f"SW{i}") for i in range(4)],
                      "sync": [self.sem(f"SY{i}") for i in range(6)],
                      "syncs": [self.sem(f"SS{i}") for i in range(8)],
                      "syncw": [self.sem(f"SV{i}") for i in range(4)]}
        self._pool_i = {"gp": 0, "gpw": 0, "sync": 0, "syncs": 0, "syncw": 0}
        self.precast = {}
        self._sem_q = {}
        self.dram = {}
        self.inputs = {}

    def sem(self, name):
        s = Sem(name)
        s.h = self._stack.enter_context(self.nc.semaphore(name))
        self.sems.append(s)
        return s

    def alloc(self, nbytes, align=4):
        self.arena_off = -(-self.arena_off // align) * align
        r = self.arena_off
        self.arena_off += nbytes
        self.arena_hwm = max(self.arena_hwm, self.arena_off)
        assert self.arena_off <= self.arena_bytes, \
            f"SBUF arena overflow: {self.arena_off} > {self.arena_bytes}"
        return r

    def mark(self):
        return self.arena_off

    def reset(self, m):
        self.arena_off = m

    def tile(self, rows, cols, dt, align=None):
        esz = 2 if dt == BF16 else 4
        boff = self.alloc(cols * esz, align or esz)
        return Tile(self, boff, rows, cols, dt)

    def emit(self, eng, fn):
        self.streams[eng].append(fn)

    def wait(self, eng, sem, val=None):
        v = sem.n if val is None else val
        assert v <= sem.n, f"wait on future value of {sem.name}: {v} > {sem.n}"
        if v <= 0:
            return
        self.emit(eng, lambda E, s=sem, v=v: E.wait_ge(s.h, v))

    def drain(self, eng):
        self.emit(eng, lambda E: E.drain())

    def op(self, eng, builder, inc=None, inck=1):
        if inc is not None:
            inc.n += inck
            def f(E, b=builder, s=inc, k=inck):
                b(E).then_inc(s.h, k)
        else:
            def f(E, b=builder):
                b(E)
        self.emit(eng, f)
        return inc.n if inc is not None else None

    def take_sems(self, n, q="gp"):
        p = self._pool[q]
        i = self._pool_i[q]
        r = [p[(i + k) % len(p)] for k in range(n)]
        self._pool_i[q] = (i + n) % len(p)
        return r

    def dma(self, eng, out_ap, in_ap, inc=None, accum=False):
        if inc is None:
            assert not self.sealed, "post-prologue DMA needs an explicit stream sem"
            inc = self.LD
            assert eng in ("sync", "gpsimd")
            if eng == "gpsimd":
                inc = self.WL
        if accum:
            def b(E, o=out_ap, i=in_ap):
                return E.dma_start(out=(o() if callable(o) else o),
                                   in_=(i() if callable(i) else i),
                                   accum_op=ALU.add)
        else:
            def b(E, o=out_ap, i=in_ap):
                return E.dma_start(out=(o() if callable(o) else o),
                                   in_=(i() if callable(i) else i))
        self.op(eng, b, inc=inc, inck=16)
        return inc.n

    def ps_view(self, rows=128, rowoff=0, col0=0, cols=512, dt=F32):
        a = self.ps[:, col0:col0 + cols]
        if dt != F32:
            a = a.bitcast(dt)
        return a[rowoff:rowoff + rows, :]

    def bank(self, b, rows=128, cols=512, rowoff=0, coloff=0, dt=F32):
        return self.ps_view(rows, rowoff, 512 * b + coloff, cols, dt)

    def use_bank(self, b, eng):
        for sem, cnt in self.bank_guards.get(b, ()):
            self.wait(eng, sem, cnt)
        self.bank_guards[b] = []

    def guard_bank(self, b, sem):
        self.bank_guards.setdefault(b, []).append((sem, sem.n))

    def fence_pe(self):
        self.wait("tensor", self.DV)
        self.wait("tensor", self.AC)
        self.wait("tensor", self.GP)

    def phase_fence(self):
        for e in self.ENGINES:
            for s in (self.PE, self.DV, self.AC, self.GP):
                self.wait(e, s)
        self.bank_guards = {}

    def barrier_loads(self):
        for e in self.ENGINES:
            self.wait(e, self.LD)
            self.wait(e, self.WL)
        self.sealed = True

    def const(self, name, arr=None):
        if name in self.consts:
            return self.consts[name][0]
        if arr is None:
            arr = _hostc()[name]
        dt = BF16 if arr.dtype == bfloat16 else F32
        t = self.tile(arr.shape[0], arr.shape[1], dt)
        self.consts[name] = (t, arr)
        assert not self.sealed, f"const {name} touched after prologue; preload it"
        d = self.nc.dram_tensor(name, list(arr.shape), dt, kind="ExternalInput").ap()
        self.inputs[name] = arr
        self.dma("sync", t.ap(), d)
        return t

    def preload_consts(self, names):
        for n in names:
            self.const(n)

    def load_consts(self):
        pass

    def const_dram(self, name):
        key = "dram_" + name
        if key in self.dram:
            return self.dram[key].ap()
        arr = _hostc()[name]
        dt = BF16 if arr.dtype == bfloat16 else F32
        h = self.nc.dram_tensor(name, list(arr.shape), dt, kind="ExternalInput")
        self.dram[key] = h
        self.inputs[name] = arr
        return h.ap()

    def pconst(self, name):
        t = getattr(self, "phase_consts", {}).get(name)
        return t if t is not None else self.const(name)

    def load_pconsts(self, names, eng="sync"):
        if not hasattr(self, "phase_consts"):
            self.phase_consts = {}
        sem = self.take_sems(1, "sync")[0]
        for n in names:
            arr = _hostc()[n]
            dt = BF16 if arr.dtype == bfloat16 else F32
            t = self.tile(arr.shape[0], arr.shape[1], dt)
            self.dma("sync", t.ap(), self.const_dram(n), inc=sem)
            self.phase_consts[n] = t
        for e in self.ENGINES:
            self.wait(e, sem, sem.n)

    def clear_pconsts(self, names):
        for n in names:
            self.phase_consts.pop(n, None)

    def precast_weights(self, d, names, shapes):
        psem = self.sem("PCAST")
        for n in names:
            ap16 = self.dtmp(n + "_bf16", shapes[n], BF16)
            self.dma("gpsimd", ap16, d[n], inc=psem)
            self.emit("gpsimd", lambda E, s=psem, v=psem.n: E.wait_ge(s.h, v))
            self.precast[n] = (psem, psem.n)
            d[n] = ap16

    def wait_precast(self, eng, name):
        g = self.precast.get(name)
        if g:
            self.wait(eng, g[0], g[1])

    def din(self, name, shape, dt=F32):
        h = self.nc.dram_tensor(name, list(shape), dt, kind="ExternalInput")
        self.dram[name] = h
        return h.ap()

    def dout(self, name, shape, dt=F32):
        h = self.nc.dram_tensor(name, list(shape), dt, kind="ExternalOutput")
        self.dram[name] = h
        return h.ap()

    def dtmp(self, name, shape, dt=F32):
        h = self.nc.dram_tensor(name, list(shape), dt)
        self.dram[name] = h
        return h.ap()

    def finalize(self):
        nc = self.nc
        block = self._stack.enter_context(nc.Block())
        for eng in self.ENGINES:
            ops = self.streams[eng]
            if ops:
                getattr(block, eng)(lambda E, ops=ops: [f(E) for f in ops])
        self._stack.close()
        return nc


def host_constants():
    c = {}
    c["c_idn"] = np.eye(128, dtype=bfloat16)
    tri = (np.arange(128)[:, None] > np.arange(128)[None, :])
    c["c_tri"] = tri.astype(bfloat16)
    c["c_negI"] = (NEG * np.eye(128)).astype(bfloat16)
    c["c_ones_col"] = np.ones((128, 1), dtype=bfloat16)
    c["c_ones_row"] = np.ones((1, 128), dtype=bfloat16)
    c["c_ones_rowT"] = np.ones((1, T), dtype=bfloat16)
    DH, WIN = 64, 256
    half = np.arange(0, DH, 2, dtype=np.float64)
    inv_freq = 1.0 / 10000.0 ** (half / DH)
    t = np.arange(T, dtype=np.float64)
    freqs = np.concatenate([t[:, None] * inv_freq[None, :]] * 2, -1)
    base_scale = (half + 0.4 * DH) / (1.4 * DH)
    power = (t - T // 2) / (WIN // 2)
    scale = np.concatenate([base_scale[None, :] ** power[:, None]] * 2, -1)
    cos, sin = np.cos(freqs), np.sin(freqs)
    def s2(S):
        S2 = np.empty_like(S)
        S2[0:32] = S[32:64]
        S2[32:64] = -S[0:32]
        return S2
    def dup(A):
        return np.concatenate([A, A], axis=0)
    c["c_rotCp"] = dup((cos * scale).T).astype(bfloat16)
    c["c_rotS2p"] = dup(s2((sin * scale).T)).astype(bfloat16)
    c["c_rotCm"] = dup((cos / scale).T).astype(bfloat16)
    c["c_rotS2m"] = dup(s2((sin / scale).T)).astype(bfloat16)
    i = np.arange(128)[:, None]
    j = np.arange(512)[None, :]
    c["c_delta"] = ((j - i).astype(np.float32) * np.ones((128, 512))).astype(np.float32)
    dc, mc = [], []
    for a in range(4):
        m = (j - 128 * a) > i
        dte = (j - i).astype(np.float32) * np.ones((128, 512))
        dte[np.broadcast_to(m, dte.shape)] = -3e11
        dc.append(dte.astype(np.float32))
        mc.append(np.where(m, NEG, 0.0).astype(np.float32) * np.ones((128, 512), np.float32))
    c["c_deltac"] = np.concatenate(dc, axis=1)
    c["c_causal"] = np.concatenate(mc, axis=1)
    lm = []
    for first in range(2):
        for hh in range(2):
            ia = 128 * hh + i
            if first:
                att = (j >= 256) & ((j - 256) <= ia)
            else:
                att = (j - 256 <= ia) & (ia <= j)
            lm.append(np.where(att, 0.0, NEG).astype(np.float32))
    c["c_lmask"] = np.concatenate(lm, axis=1)
    c["c_iota"] = np.arange(T, dtype=np.float32)[None, :]
    c["c_eps"] = np.full((1, 1), EPS, dtype=np.float32)
    c["c_tiny"] = np.full((128, 1), 1e-28, dtype=np.float32)
    c["c_one"] = np.full((1, 1), 1.0, dtype=np.float32)
    c["c_tri2"] = (np.arange(128)[:, None] < np.arange(128)[None, :]).astype(bfloat16)
    blk = (np.arange(128)[:, None] // 64 == np.arange(128)[None, :] // 64)
    c["c_blk2"] = blk.astype(bfloat16)
    c["c_ones512"] = np.ones((1, 512), dtype=bfloat16)
    c["c_ones_rowD"] = np.ones((1, D), dtype=bfloat16)
    return c


HOSTC = None
def _hostc():
    global HOSTC
    if HOSTC is None:
        HOSTC = host_constants()
    return HOSTC


def load_row_bf16(kb, dram_vec_ap, n, sem=None):
    t = kb.tile(1, n, BF16)
    kb.dma("gpsimd", t.ap(), dram_vec_ap.rearrange("(o n) -> o n", o=1) if len(dram_vec_ap.shape) == 1 else dram_vec_ap, inc=sem)
    return t

def load_row_f32(kb, dram_vec_ap, n, sem=None):
    t = kb.tile(1, n, F32)
    kb.dma("sync", t.ap(), dram_vec_ap.rearrange("(o n) -> o n", o=1) if len(dram_vec_ap.shape) == 1 else dram_vec_ap, inc=sem)
    return t

def load_cols_f32(kb, dram_vec_ap, n, sem=None):
    ncols = -(-n // 128)
    t = kb.tile(128, ncols, F32)
    for c in range(ncols):
        r = min(128, n - c * 128)
        kb.dma("sync", t.ap()[0:r, c:c + 1],
               dram_vec_ap[c * 128:c * 128 + r].rearrange("(p o) -> p o", o=1), inc=sem)
    return t


def emit_ln(kb, src, srcb, dst, grow, bcol, src_is_bf16=False, tag=""):
    if src is None:
        src = srcb
    _m = kb.mark()
    mu = kb.tile(1, T, F32); mu2 = kb.tile(1, T, F32)
    sr = kb.tile(1, T, F32); rr = kb.tile(1, T, F32)
    vr = mu2
    mb = mu2
    rrb = kb.tile(1, T, BF16); mbb = kb.tile(1, T, BF16)
    sq_pp = [kb.tile(128, TB, BF16) for _ in range(2)]
    xb_pp = [kb.tile(128, TB, BF16) for _ in range(2)] if srcb is None else None
    rg_sb = [kb.tile(128, TB, F32) for _ in range(2)]
    tmp_pp = [kb.tile(128, TB, F32) for _ in range(2)]
    onescol = kb.const("c_ones_col")

    kb.fence_pe()
    pe0 = kb.PE.n
    for tb in range(NTB):
        bk = 4 + tb
        sum_ps = kb.bank(bk, rows=1, rowoff=0)
        sq_ps = kb.bank(bk, rows=1, rowoff=32)
        for c in range(NCH):
            pp = c % 2
            k = tb * NCH + c
            if k >= 2:
                kb.wait("scalar", kb.PE, pe0 + k - 1)
            kb.op("scalar", lambda E, s=src(c), d=sq_pp[pp], tb=tb:
                  E.activation(d.ap(), s[:, tb * TB:(tb + 1) * TB], AF.Square),
                  inc=kb.AC)
            if srcb is None:
                if k >= 2:
                    kb.wait("gpsimd", kb.PE, pe0 + k - 1)
                kb.op("gpsimd", lambda E, s=src(c), d=xb_pp[pp], tb=tb:
                      E.tensor_copy(d.ap(), s[:, tb * TB:(tb + 1) * TB]), inc=kb.GP)
                xb = xb_pp[pp].ap()
            else:
                xb = srcb(c)[:, tb * TB:(tb + 1) * TB]
            kb.wait("tensor", kb.AC)
            if srcb is None:
                kb.wait("tensor", kb.GP)
            kb.op("tensor", lambda E, xb=xb, sp=sum_ps, st=(c == 0), en=(c == NCH - 1):
                  E.matmul(sp, onescol.ap(), xb, start=st, stop=en))
            kb.op("tensor", lambda E, d=sq_pp[pp], qp=sq_ps, st=(c == 0), en=(c == NCH - 1):
                  E.matmul(qp, onescol.ap(), d.ap(), start=st, stop=en), inc=kb.PE)

    sum_all = kb.ps_view(rows=1, rowoff=0, col0=512 * 4, cols=2048)
    sq_all = kb.ps_view(rows=1, rowoff=32, col0=512 * 4, cols=2048)
    kb.wait("vector", kb.PE)
    kb.op("vector", lambda E: E.tensor_scalar_mul(mu.ap(), sum_all, 1.0 / D))
    kb.drain("vector")
    kb.op("vector", lambda E: E.tensor_tensor(mu2.ap(), mu.ap(), mu.ap(), ALU.mult))
    kb.drain("vector")
    kb.op("vector", lambda E: E.scalar_tensor_tensor(vr.ap(), sq_all, 1.0 / D, mu2.ap(),
                                                     ALU.mult, ALU.subtract), inc=kb.DV)
    kb.wait("scalar", kb.DV)
    epst = kb.const("c_eps")
    kb.op("scalar", lambda E: E.activation(sr.ap(), vr.ap(), AF.Sqrt, bias=epst.ap()),
          inc=kb.AC)
    kb.wait("vector", kb.AC)
    kb.op("vector", lambda E: E.reciprocal(rr.ap(), sr.ap()))
    kb.drain("vector")
    kb.op("vector", lambda E: E.scalar_tensor_tensor(mb.ap(), mu.ap(), -1.0, rr.ap(),
                                                     ALU.mult, ALU.mult))
    kb.drain("vector")
    kb.op("vector", lambda E: E.tensor_copy(rrb.ap(), rr.ap()))
    kb.drain("vector")
    kb.op("vector", lambda E: E.tensor_copy(mbb.ap(), mb.ap()), inc=kb.DV)
    dv_rows = kb.DV.n

    ac0, gp0, dv0 = kb.AC.n, kb.GP.n, kb.DV.n
    pe1 = kb.PE.n
    for c in range(NCH):
        for tb in range(NTB):
            k = c * NTB + tb
            pp = k % 2
            rg_ps = kb.bank(pp)
            mb_ps = kb.bank(2 + pp)
            kb.wait("tensor", kb.DV, dv_rows)
            if k >= 2:
                kb.wait("tensor", kb.AC, ac0 + (k - 1))
                kb.wait("tensor", kb.DV, dv0 + (k - 1))
            kb.op("tensor", lambda E, c=c, tb=tb, d=rg_ps:
                  E.matmul(d, grow.ap()[0:1, c * 128:(c + 1) * 128],
                           rrb.ap()[0:1, tb * TB:(tb + 1) * TB], start=True, stop=True))
            kb.op("tensor", lambda E, c=c, tb=tb, d=mb_ps:
                  E.matmul(d, grow.ap()[0:1, c * 128:(c + 1) * 128],
                           mbb.ap()[0:1, tb * TB:(tb + 1) * TB], start=True, stop=True),
                  inc=kb.PE)
            kb.wait("scalar", kb.PE)
            if k >= 2:
                kb.wait("scalar", kb.GP, gp0 + (k - 1))
            kb.op("scalar", lambda E, pp=pp, s=rg_ps:
                  E.activation(rg_sb[pp].ap(), s, AF.Copy), inc=kb.AC)
            kb.wait("gpsimd", kb.AC)
            if k >= 2:
                kb.wait("gpsimd", kb.DV, dv0 + (k - 1))
            kb.op("gpsimd", lambda E, c=c, tb=tb, pp=pp:
                  E.tensor_tensor(tmp_pp[pp].ap(), src(c)[:, tb * TB:(tb + 1) * TB],
                                  rg_sb[pp].ap(), ALU.mult), inc=kb.GP)
            kb.wait("vector", kb.GP)
            b_ap = bcol.ap()[:, c:c + 1] if bcol is not None else 0.0
            kb.op("vector", lambda E, c=c, tb=tb, pp=pp, mp=mb_ps, b=b_ap:
                  E.scalar_tensor_tensor(dst(c)[:, tb * TB:(tb + 1) * TB], mp, b,
                                         tmp_pp[pp].ap(), ALU.add, ALU.add), inc=kb.DV)
    kb.reset(_m)


def emit_proj_fm(kb, wdram, din, dout, srcb, evac, evac_sem, aug=None,
                 mw=512, banks=(0, 1, 2, 3, 4, 5, 6, 7), ntb=NTB, pre=None, tag=""):
    if pre:
        kb.wait_precast("sync", pre)
    nkc = -(-din // 128)
    nmb = -(-dout // mw)
    _m = kb.mark()
    wbuf = [kb.tile(128, nkc * mw, BF16) for _ in range(2)]
    wsem = kb.take_sems(2, "syncw")
    evac_counts = []
    nbank = len(banks)
    use = 0
    pe_last_for_buf = [0, 0]
    for mb_i in range(nmb):
        m0 = mb_i * mw
        mwc = min(mw, dout - m0)
        bufp = mb_i % 2
        if mb_i >= 2:
            kb.wait("sync", kb.PE, pe_last_for_buf[bufp])
        else:
            kb.wait("sync", kb.PE)
        nkfull = din // 128
        if nkfull:
            kb.dma("sync",
                   wbuf[bufp].ap()[:, 0:nkfull * mwc]
                   .rearrange("p (kc m) -> p kc m", kc=nkfull),
                   wdram[0:nkfull * 128, m0:m0 + mwc]
                   .rearrange("(kc kp) m -> kp kc m", kp=128), inc=wsem[bufp])
        if nkfull * 128 != din:
            rem = din - nkfull * 128
            kb.dma("sync", wbuf[bufp].ap()[0:rem, nkfull * mwc:(nkfull + 1) * mwc],
                   wdram[nkfull * 128:din, m0:m0 + mwc], inc=wsem[bufp])
        wl_ready = wsem[bufp].n
        nmc = -(-mwc // 128)
        for mc in range(nmc):
            mm0 = mc * 128
            mrows = min(128, mwc - mm0)
            mi = (m0 + mm0) // 128
            for tb in range(ntb):
                bk = banks[use % nbank]
                ps = kb.bank(bk, rows=mrows)
                kb.use_bank(bk, "tensor")
                if use >= nbank:
                    kb.wait("tensor", evac_sem, evac_counts[use - nbank])
                if tb == 0 and mc == 0:
                    kb.wait("tensor", wsem[bufp], wl_ready)
                last_k = nkc - 1 if aug is None else None
                for kc in range(nkc):
                    krows = min(128, din - kc * 128)
                    if kc * 128 + 128 <= din:
                        w_ap = wbuf[bufp].ap()[:, kc * mwc + mm0:kc * mwc + mm0 + mrows]
                    else:
                        w_ap = wbuf[bufp].ap()[0:krows, (din // 128) * mwc + mm0:
                                               (din // 128) * mwc + mm0 + mrows]
                    st = (kc == 0)
                    en = (kc == nkc - 1) and aug is None
                    kb.op("tensor", lambda E, w=w_ap, s=srcb(kc), ps=ps, tb=tb, st=st, en=en, kr=krows:
                          E.matmul(ps, w, s[0:kr, tb * TB:(tb + 1) * TB], start=st, stop=en),
                          inc=(kb.PE if en else None))
                if aug is not None:
                    kb.op("tensor", lambda E, ps=ps, tb=tb, mi=mi, a=aug:
                          E.matmul(ps, a[0](mi), a[1](mi, tb), start=False, stop=True),
                          inc=kb.PE)
                pe_last_for_buf[bufp] = kb.PE.n
                evac(mi, tb, ps, mrows)
                kb.guard_bank(bk, evac_sem)
                evac_counts.append(evac_sem.n)
                use += 1
    kb.reset(_m)


"""Macaron megakernel emitters: streams, L0_ffn (local-attn transformer)."""
import numpy as np
from ml_dtypes import bfloat16

GT = T + 2


class St:
    def __init__(self, kb):
        self.kb = kb
        self.z = [kb.tile(128, GT, F32) for _ in range(NCH)]
        self.zb = [kb.tile(128, GT, BF16) for _ in range(NCH)]
        for c in range(NCH):
            kb.op("vector", lambda E, t=self.z[c]: E.memset(t.ap()[:, 0:1], 0.0))
            kb.op("vector", lambda E, t=self.zb[c]: E.memset(t.ap()[:, 0:1], 0.0),
                  inc=(kb.DV if c == NCH - 1 else None))

    def zs(self, c, shifted=False):
        off = 0 if (shifted and c >= 4) else 1
        return self.z[c].ap()[:, off:off + T]

    def zbs(self, c, shifted=False):
        off = 0 if (shifted and c >= 4) else 1
        return self.zb[c].ap()[:, off:off + T]


def refresh_zb(kb, st, chunks=None):
    for c in (chunks if chunks is not None else range(NCH)):
        if c % 2 == 0:
            kb.wait("vector", kb.DV)
            kb.op("vector", lambda E, c=c: E.tensor_copy(st.zbs(c), st.zs(c)),
                  inc=kb.DV)
        else:
            kb.wait("scalar", kb.DV)
            kb.wait("scalar", kb.AC)
            kb.op("scalar", lambda E, c=c:
                  E.activation(st.zbs(c), st.zs(c), AF.Copy), inc=kb.AC)


class FfnParams:
    def __init__(self, kb, d, li):
        self.wqkv = d["L0_ffn_Wqkv"][li]
        self.wo = d["L0_ffn_Wo"][li]
        self.w1 = d["L0_ffn_W1"][li]
        self.w2 = d["L0_ffn_W2"][li]
        ps = kb.take_sems(1, "sync")[0]
        pg = kb.take_sems(1, "gp")[0]
        self.qs8 = kb.tile(128, 1, F32)
        self.ks = kb.tile(128, 1, F32)
        for half in range(2):
            kb.dma("sync", self.qs8.ap()[64 * half:64 * half + 64, :],
                   d["L0_ffn_q_scale"][li].rearrange("(p o) -> p o", o=1), inc=ps)
            kb.dma("sync", self.ks.ap()[64 * half:64 * half + 64, :],
                   d["L0_ffn_k_scale"][li].rearrange("(p o) -> p o", o=1), inc=ps)
        self.lng = load_row_bf16(kb, d["L0_ffn_ln_g"][li], D, sem=pg)
        self.lnb = load_cols_f32(kb, d["L0_ffn_ln_b"][li], D, sem=ps)
        self.b1 = load_row_bf16(kb, d["L0_ffn_b1"][li], 5460, sem=pg)
        self.b2 = load_cols_f32(kb, d["L0_ffn_b2"][li], D, sem=ps)
        kb.wait("vector", ps, ps.n)
        kb.op("vector", lambda E: E.tensor_scalar_mul(self.qs8.ap(), self.qs8.ap(), 8.0))
        kb.drain("vector")
        kb.op("vector", lambda E: E.engine_nop(), inc=kb.DV)
        for e in ("tensor", "gpsimd", "scalar", "vector"):
            kb.wait(e, ps, ps.n)
            kb.wait(e, pg, pg.n)
            kb.wait(e, kb.DV)


class FfnScratch:
    def __init__(self, kb):
        self.out_all = [kb.tile(128, T, BF16) for _ in range(NCH)]
        self.wq_b = [kb.tile(128, 8 * 128, BF16) for _ in range(2)]
        self.wk_b = [kb.tile(128, 8 * 128, BF16) for _ in range(2)]
        self.wv_b = [kb.tile(128, 8 * 128, BF16) for _ in range(2)]
        self.qrot = kb.tile(128, T, BF16)
        self.krot = kb.tile(128, T, BF16)
        self.sq_pp = [kb.tile(128, TB, BF16) for _ in range(2)]
        self.rn_pp = [kb.tile(128, TB, BF16) for _ in range(2)]
        self.l2_pp = [kb.tile(128, TB, F32)] * 2
        self.nb_pp = [kb.tile(128, 1, F32) for _ in range(2)]
        self.sm_pp = [kb.tile(128, 1, F32) for _ in range(4)]
        self.rs_pp = [kb.tile(128, 1, F32) for _ in range(2)]
        m = kb.mark()
        self.u = kb.tile(128, T, BF16)
        self.s = kb.tile(128, T, BF16)
        hw1 = kb.arena_off
        kb.reset(m)
        self.v_tm = [kb.tile(128, 128, BF16) for _ in range(16)]
        self.P_pp = [kb.tile(128, 512, BF16) for _ in range(4)]
        self.P2 = kb.tile(128, 512, BF16)
        self.PT = [kb.tile(128, 256, BF16) for _ in range(4)]
        hw2 = kb.arena_off
        kb.arena_off = max(hw1, hw2)
        kb.arena_hwm = max(kb.arena_hwm, kb.arena_off)
        self.wsem = kb.take_sems(2, "syncs")
        self.rsem = kb.take_sems(2)
        self.gd = {}

    def gwait(self, kb, eng, key):
        g = self.gd.get(key)
        if g:
            kb.wait(eng, g[0], g[1])

    def gset(self, kb, key, sem):
        self.gd[key] = (sem, sem.n)


class GluScratch:
    def __init__(self, kb):
        self.r_pp = [kb.tile(128, 1024, BF16) for _ in range(2)]
        self.e1_pp = [kb.tile(128, TB, BF16) for _ in range(2)]
        self.q5_pp = [kb.tile(128, TB, BF16) for _ in range(2)]
        self.wg_pp = [kb.tile(128, 8 * 128, BF16) for _ in range(2)]
        self.wa_pp = [kb.tile(128, 8 * 128, BF16) for _ in range(2)]
        self.h = [kb.tile(128, 1024, BF16) for _ in range(22)]
        self.gd = {}

    def gwait(self, kb, eng, key):
        g = self.gd.get(key)
        if g:
            kb.wait(eng, g[0], g[1])

    def gset(self, kb, key, sem):
        self.gd[key] = (sem, sem.n)


def stream_pair_weights(kb, sc, p, pair, pp):
    s = sc.wsem[pp]
    kb.wait_precast("sync", "L0_ffn_Wqkv")
    for name, buf, base in (("q", sc.wq_b[pp], 0), ("k", sc.wk_b[pp], D),
                            ("v", sc.wv_b[pp], 2 * D)):
        sc.gwait(kb, "sync", ("wb", name, pp))
        kb.dma("sync",
               buf.ap().rearrange("p (kc m) -> p kc m", kc=NCH),
               p.wqkv[:, base + pair * 128: base + (pair + 1) * 128]
               .rearrange("(kc kp) m -> kp kc m", kp=128), inc=s)
    return (s, s.n)


def qk_prep(kb, st, sc, p, pair, which, pp):
    wb = (sc.wq_b if which == "q" else sc.wk_b)[pp]
    dst = sc.qrot if which == "q" else sc.krot
    Ctab = kb.pconst("c_rotCp" if which == "q" else "c_rotCm")
    Stab = kb.pconst("c_rotS2p" if which == "q" else "c_rotS2m")
    blk2 = kb.const("c_blk2")
    tiny = kb.const("c_tiny")
    scl = p.qs8 if which == "q" else p.ks
    sc.gwait(kb, "vector", ("u",))
    kb.wait("vector", kb.DV)

    def mm_stage(tb):
        ps = kb.bank(tb)
        kb.use_bank(tb, "tensor")
        for kc in range(NCH):
            kb.op("tensor", lambda E, kc=kc, tb=tb, ps=ps:
                  E.matmul(ps, wb.ap()[:, kc * 128:(kc + 1) * 128],
                           st.zbs(kc, shifted=True)[:, tb * TB:(tb + 1) * TB],
                           start=(kc == 0), stop=(kc == NCH - 1)),
                  inc=(kb.PE if kc == NCH - 1 else None))

    def norm_stage(tb):
        ps = kb.bank(tb)
        nb_ = 4 + tb
        sc.gwait(kb, "scalar", ("sq", tb % 2))
        kb.wait("scalar", kb.PE)
        kb.op("scalar", lambda E, ps=ps, d=sc.sq_pp[tb % 2]:
              E.activation(d.ap(), ps, AF.Square), inc=kb.AC)
        nps = kb.bank(nb_)
        kb.use_bank(nb_, "tensor")
        kb.wait("tensor", kb.AC)
        kb.op("tensor", lambda E, nps=nps, d=sc.sq_pp[tb % 2]:
              E.matmul(nps, blk2.ap(), d.ap(), start=True, stop=True), inc=kb.PE)
        sc.gset(kb, ("sq", tb % 2), kb.PE)
        sc.gwait(kb, "scalar", ("l2", tb % 2))
        kb.wait("scalar", kb.PE)
        kb.op("scalar", lambda E, nps=nps, d=sc.l2_pp[tb % 2]:
              E.activation(d.ap(), nps, AF.Ln, bias=tiny.ap()), inc=kb.AC)
        kb.guard_bank(nb_, kb.AC)
        sc.gwait(kb, "scalar", ("rn", tb % 2))
        kb.drain("scalar")
        kb.op("scalar", lambda E, d=sc.rn_pp[tb % 2], s=sc.l2_pp[tb % 2]:
              E.activation(d.ap(), s.ap(), AF.Exp, scale=-0.5), inc=kb.AC)
        sc.gset(kb, ("l2", tb % 2), kb.AC)
        kb.wait("vector", kb.AC)
        kb.op("vector", lambda E, ps=ps, tb=tb, s=scl, r=sc.rn_pp[tb % 2]:
              E.scalar_tensor_tensor(sc.u.ap()[:, tb * TB:(tb + 1) * TB], ps, s.ap(),
                                     r.ap(), ALU.mult, ALU.mult), inc=kb.DV)
        kb.guard_bank(tb, kb.DV)
        sc.gset(kb, ("rn", tb % 2), kb.DV)

    mm_stage(0)
    for tb in range(NTB):
        if tb + 1 < NTB:
            mm_stage(tb + 1)
        norm_stage(tb)
    sc.gwait(kb, "gpsimd", ("rot", which))
    kb.wait("gpsimd", kb.DV)
    kb.op("gpsimd", lambda E: E.tensor_tensor(dst.ap()[:, 0:T], sc.u.ap(),
                                              Ctab.ap()[:, 0:T], ALU.mult), inc=kb.GP)
    sc.gset(kb, ("u",), kb.GP)
    sc.gwait(kb, "vector", ("s",))
    kb.wait("vector", kb.DV)
    kb.op("vector", lambda E: E.tensor_tensor(sc.s.ap(), sc.u.ap(),
                                              Stab.ap()[:, 0:T], ALU.mult), inc=kb.DV)
    kb.wait("gpsimd", kb.GP)
    kb.wait("gpsimd", kb.DV)
    rs = sc.rsem[0 if which == "q" else 1]
    for blk in range(4):
        src_b = (blk // 2) * 2 + (1 - blk % 2)
        kb.dma("gpsimd", dst.ap()[blk * 32:(blk + 1) * 32, 0:T],
               sc.s.ap()[src_b * 32:(src_b + 1) * 32, :], inc=rs, accum=True)
    sc.gset(kb, ("s",), rs)
    sc.gd[("rotrdy", which)] = (rs, rs.n)


def v_prep(kb, st, sc, pair, pp):
    wb = sc.wv_b[pp]
    sc.gwait(kb, "scalar", ("rotrdy", "q"))
    sc.gwait(kb, "scalar", ("rotrdy", "k"))
    for tt in range(16):
        b = tt % 2
        ps = kb.bank(b, rows=128, cols=128)
        kb.use_bank(b, "tensor")
        for kc in range(NCH):
            kb.op("tensor", lambda E, kc=kc, tt=tt, ps=ps:
                  E.matmul(ps, st.zbs(kc, shifted=True)[:, tt * 128:(tt + 1) * 128],
                           wb.ap()[:, kc * 128:(kc + 1) * 128],
                           start=(kc == 0), stop=(kc == NCH - 1)),
                  inc=(kb.PE if kc == NCH - 1 else None))
        sc.gwait(kb, "scalar", ("vtm",)) if tt == 0 else None
        kb.wait("scalar", kb.PE)
        kb.op("scalar", lambda E, ps=ps, d=sc.v_tm[tt]:
              E.activation(d.ap(), ps, AF.Copy), inc=kb.AC)
        kb.guard_bank(b, kb.AC)


DBG = "full"

def local_attn_pair(kb, st, sc, pair):
    idn = kb.const("c_idn")
    tri = kb.const("c_tri")
    tri2 = kb.const("c_tri2")
    negI = kb.const("c_negI")
    out = sc.out_all[pair]

    def stage_A(it):
        w, hl, sidx = it["w"], it["hl"], it["sidx"]
        s2 = sidx % 2
        for hh in range(2):
            q0 = w * 256 + hh * 128
            mychunks = [(kc, kind) for kc, kind in
                        zip(range(2 * w + hh - 2, 2 * w + hh + 1),
                            ("tri2", "full", "tri")) if kc >= 0]
            it["chunks"][hh] = mychunks
            ncol = 128 * len(mychunks)
            it["ncol"][hh] = ncol
            sb = hh
            kb.use_bank(sb, "tensor")
            if w == 0 and hl == 0 and hh == 0:
                sc.gwait(kb, "tensor", ("rotrdy", "q"))
                sc.gwait(kb, "tensor", ("rotrdy", "k"))
                kb.wait("tensor", kb.AC)
            mms = []
            for idx, (kc, kind) in enumerate(mychunks):
                mms.append(("s", idx, kc))
            for idx, (kc, kind) in enumerate(mychunks):
                if kind != "full":
                    mms.append(("m", idx, kind))
            for n_, m in enumerate(mms):
                st_, en_ = (n_ == 0), (n_ == len(mms) - 1)
                if m[0] == "s":
                    _, idx, kc = m
                    kb.op("tensor", lambda E, idx=idx, kc=kc, hl=hl, q0=q0, sb=sb,
                          st_=st_, en_=en_:
                          E.matmul(kb.bank(sb, cols=128, coloff=idx * 128),
                                   sc.qrot.ap()[hl * 64:(hl + 1) * 64, q0:q0 + 128],
                                   sc.krot.ap()[hl * 64:(hl + 1) * 64,
                                                kc * 128:(kc + 1) * 128],
                                   start=st_, stop=en_),
                          inc=(kb.PE if en_ else None))
                else:
                    _, idx, kind = m
                    tm = tri if kind == "tri" else tri2
                    kb.op("tensor", lambda E, idx=idx, tm=tm, sb=sb, st_=st_, en_=en_:
                          E.matmul(kb.bank(sb, cols=128, coloff=idx * 128),
                                   tm.ap(), negI.ap(), start=st_, stop=en_),
                          inc=(kb.PE if en_ else None))
            S = kb.bank(sb, cols=ncol)
            ps_ = 2 * s2 + hh
            sc.gwait(kb, "vector", ("nb", hh))
            kb.wait("vector", kb.PE)
            kb.op("vector", lambda E, S=S, d=sc.nb_pp[hh]:
                  E.reduce_max(d.ap(), S, axis=AX.X, negate=True), inc=kb.DV)
            sc.gwait(kb, "scalar", ("P", ps_))
            sc.gwait(kb, "scalar", ("sm", ps_))
            kb.wait("scalar", kb.DV)
            kb.op("scalar", lambda E, S=S, ps_=ps_, hh=hh, ncol=ncol:
                  E.activation(sc.P_pp[ps_].ap()[:, 0:ncol], S, AF.Exp,
                               bias=sc.nb_pp[hh].ap(),
                               accum_out=sc.sm_pp[ps_].ap()), inc=kb.AC)
            kb.guard_bank(sb, kb.AC)
            sc.gset(kb, ("nb", hh), kb.AC)
            it["acexp"][hh] = kb.AC.n

    def stage_BC(it):
        w, hl, sidx = it["w"], it["hl"], it["sidx"]
        s2 = sidx % 2
        kcs = [kc for kc in range(2 * w - 2, 2 * w + 2) if kc >= 0]
        for hh in range(2):
            mychunks = it["chunks"][hh]
            ncol = it["ncol"][hh]
            ps_ = 2 * s2 + hh
            sc.gwait(kb, "vector", ("rs", hh))
            kb.wait("vector", kb.AC, it["acexp"][hh])
            kb.op("vector", lambda E, hh=hh, ps_=ps_:
                  E.reciprocal(sc.rs_pp[hh].ap(), sc.sm_pp[ps_].ap()), inc=kb.DV)
            sc.gset(kb, ("sm", ps_), kb.DV)
            sc.gwait(kb, "vector", ("P2",))
            kb.wait("vector", kb.DV)
            kb.op("vector", lambda E, hh=hh, ps_=ps_, ncol=ncol:
                  E.tensor_scalar_mul(sc.P2.ap()[:, 0:ncol],
                                      sc.P_pp[ps_].ap()[:, 0:ncol],
                                      sc.rs_pp[hh].ap()), inc=kb.DV)
            sc.gset(kb, ("rs", hh), kb.DV)
            sc.gset(kb, ("P", ps_), kb.DV)
            dvp2 = kb.DV.n
            tb_ = 2 + hh
            kb.use_bank(tb_, "tensor")
            kb.wait("tensor", kb.DV, dvp2)
            for idx, (kc, kind) in enumerate(mychunks):
                kb.op("tensor", lambda E, idx=idx, tb_=tb_:
                      E.transpose(kb.ps_view(128, 0, 512 * tb_ + idx * 64, 64, BF16),
                                  sc.P2.ap()[:, idx * 128:(idx + 1) * 128],
                                  idn.ap()),
                      inc=(kb.PE if idx == len(mychunks) - 1 else None))
            sc.gset(kb, ("P2",), kb.PE)
            eng = "vector" if hh == 0 else "scalar"
            for idx, (kc, kind) in enumerate(mychunks):
                slot = kc - (2 * w - 2)
                sc.gwait(kb, eng, ("PT", slot, hh))
                kb.wait(eng, kb.PE)
                src_ap = kb.ps_view(128, 0, 512 * tb_ + idx * 64, 64, BF16)
                dst_ap = sc.PT[slot].ap()[:, hh * 128:(hh + 1) * 128]
                if eng == "vector":
                    kb.op("vector", lambda E, s=src_ap, d=dst_ap:
                          E.tensor_copy(d, s), inc=kb.DV)
                    kb.guard_bank(tb_, kb.DV)
                else:
                    kb.op("scalar", lambda E, s=src_ap, d=dst_ap:
                          E.activation(d, s, AF.Copy), inc=kb.AC)
                    kb.guard_bank(tb_, kb.AC)
        ob = 4 + sidx % 2
        kb.use_bank(ob, "tensor")
        kb.wait("tensor", kb.DV)
        kb.wait("tensor", kb.AC)
        if w == 0:
            contribs = {0: [0], 1: [0, 1]}
        else:
            contribs = {0: [kc for kc in kcs if kc <= 2 * w],
                        1: [kc for kc in kcs if kc >= 2 * w - 1]}
        nmm = sum(len(v) for v in contribs.values())
        n_ = 0
        for r in (0, 1):
            lst = contribs[r]
            for i, kc in enumerate(lst):
                slot = kc - (2 * w - 2)
                n_ += 1
                kb.op("tensor", lambda E, kc=kc, hl=hl, slot=slot, r=r, ob=ob,
                      st_=(n_ == 1), en_=(n_ == nmm):
                      E.matmul(kb.ps_view(64, 0, 512 * ob + r * 128, 128),
                               sc.v_tm[kc].ap()[:, hl * 64:(hl + 1) * 64],
                               sc.PT[slot].ap()[:, r * 128:(r + 1) * 128],
                               start=st_, stop=en_),
                      inc=(kb.PE if n_ == nmm else None))
        for kc in kcs:
            slot = kc - (2 * w - 2)
            for hh in range(2):
                sc.gset(kb, ("PT", slot, hh), kb.PE)
        kb.wait("scalar", kb.PE)
        kb.op("scalar", lambda E, hl=hl, w=w, ob=ob:
              E.activation(out.ap()[hl * 64:(hl + 1) * 64, w * 256:(w + 1) * 256],
                           kb.ps_view(64, 0, 512 * ob, 256), AF.Copy), inc=kb.AC)
        kb.guard_bank(ob, kb.AC)

    prev = None
    sidx = 0
    for w in range(8):
        for hl in range(2):
            it = {"w": w, "hl": hl, "sidx": sidx, "chunks": {}, "ncol": {},
                  "acexp": {}}
            sidx += 1
            stage_A(it)
            if prev is not None:
                stage_BC(prev)
            prev = it
    stage_BC(prev)
    sc.gset(kb, ("rot", "q"), kb.PE)
    sc.gset(kb, ("rot", "k"), kb.PE)
    sc.gset(kb, ("vtm",), kb.PE)


def wo_residual(kb, st, sc, p):
    def evac(mi, tb, ps, mrows):
        kb.wait("vector", kb.PE)
        kb.op("vector", lambda E, mi=mi, tb=tb, ps=ps:
              E.scalar_tensor_tensor(st.zs(mi)[:, tb * TB:(tb + 1) * TB], ps, 0.0,
                                     st.zs(mi)[:, tb * TB:(tb + 1) * TB],
                                     ALU.add, ALU.add), inc=kb.DV)
    emit_proj_fm(kb, p.wo, D, D, srcb=lambda c: sc.out_all[c].ap()[:, 0:T],
                 evac=evac, evac_sem=kb.DV, mw=256, banks=(0, 1, 2, 3),
                 pre="L0_ffn_Wo")


def glu_w2(kb, st, p):
    sc = GluScratch(kb)
    gsem = kb.take_sems(2, "syncs")
    asem = kb.take_sems(2, "syncs")
    kb.wait_precast("sync", "L0_ffn_W1")
    ones512 = kb.const("c_ones512")
    DI = 2730
    njc = 22
    wg_pp, wa_pp = sc.wg_pp, sc.wa_pp
    for half in range(2):
        hoff = half * 1024

        def stream_j(j):
            m0 = j * 128
            mrows = min(128, DI - m0)
            pp = j % 2
            sc.gwait(kb, "sync", ("wg", pp))
            kb.dma("sync", wg_pp[pp].ap()[:, 0:NCH * mrows]
                   .rearrange("p (kc m) -> p kc m", kc=NCH),
                   p.w1[:, DI + m0:DI + m0 + mrows]
                   .rearrange("(kc kp) m -> kp kc m", kp=128), inc=gsem[pp])
            sc.gwait(kb, "sync", ("wa", pp))
            kb.dma("sync", wa_pp[pp].ap()[:, 0:NCH * mrows]
                   .rearrange("p (kc m) -> p kc m", kc=NCH),
                   p.w1[:, m0:m0 + mrows]
                   .rearrange("(kc kp) m -> kp kc m", kp=128), inc=asem[pp])
            return (gsem[pp].n, asem[pp].n)

        def stage_g(j, wl):
            m0 = j * 128
            mrows = min(128, DI - m0)
            pp = j % 2
            for tb in range(2):
                t0 = hoff + tb * TB
                b = tb % 2
                ps = kb.bank(b, rows=mrows)
                kb.use_bank(b, "tensor")
                if tb == 0:
                    kb.wait("tensor", gsem[pp], wl[0])
                for kc in range(NCH):
                    kb.op("tensor", lambda E, kc=kc, t0=t0, ps=ps, pp=pp, mr=mrows:
                          E.matmul(ps, wg_pp[pp].ap()[:, kc * mr:(kc + 1) * mr],
                                   st.zbs(kc)[:, t0:t0 + TB],
                                   start=(kc == 0), stop=False))
                kb.op("tensor", lambda E, ps=ps, m0=m0, mr=mrows:
                      E.matmul(ps, p.b1.ap()[0:1, DI + m0:DI + m0 + mr],
                               ones512.ap(), start=False, stop=True), inc=kb.PE)
                sc.gwait(kb, "scalar", ("e1", tb % 2))
                kb.wait("scalar", kb.PE)
                kb.op("scalar", lambda E, ps=ps, d=sc.e1_pp[tb % 2], mr=mrows:
                      E.activation(d.ap()[0:mr, :], ps, AF.Erf,
                                   scale=0.7071067811865476), inc=kb.AC)
                sc.gwait(kb, "scalar", ("q5", tb % 2))
                kb.op("scalar", lambda E, ps=ps, d=sc.q5_pp[tb % 2], mr=mrows:
                      E.activation(d.ap()[0:mr, :], ps, AF.Copy, scale=0.5),
                      inc=kb.AC)
                kb.guard_bank(b, kb.AC)
                sc.gwait(kb, "vector", ("r", pp)) if tb == 0 else None
                kb.wait("vector", kb.AC)
                kb.op("vector", lambda E, tb=tb, pp=pp, mr=mrows:
                      E.scalar_tensor_tensor(
                          sc.r_pp[pp].ap()[0:mr, tb * TB:(tb + 1) * TB],
                          sc.e1_pp[tb % 2].ap()[0:mr, :], 1.0,
                          sc.q5_pp[tb % 2].ap()[0:mr, :],
                          ALU.add, ALU.mult), inc=kb.DV)
                sc.gset(kb, ("e1", tb % 2), kb.DV)
                sc.gset(kb, ("q5", tb % 2), kb.DV)

        def stage_a(j, wl):
            m0 = j * 128
            mrows = min(128, DI - m0)
            pp = j % 2
            for tb in range(2):
                t0 = hoff + tb * TB
                b = 2 + (2 * j + tb) % 4
                ps = kb.bank(b, rows=mrows)
                kb.use_bank(b, "tensor")
                if tb == 0:
                    kb.wait("tensor", asem[pp], wl[1])
                for kc in range(NCH):
                    kb.op("tensor", lambda E, kc=kc, t0=t0, ps=ps, pp=pp, mr=mrows:
                          E.matmul(ps, wa_pp[pp].ap()[:, kc * mr:(kc + 1) * mr],
                                   st.zbs(kc)[:, t0:t0 + TB],
                                   start=(kc == 0), stop=False))
                kb.op("tensor", lambda E, ps=ps, m0=m0, mr=mrows:
                      E.matmul(ps, p.b1.ap()[0:1, m0:m0 + mr],
                               ones512.ap(), start=False, stop=True), inc=kb.PE)
                kb.wait("vector", kb.PE)
                if tb == 0:
                    kb.wait("vector", kb.DV)
                kb.op("vector", lambda E, ps=ps, j=j, tb=tb, pp=pp, mr=mrows:
                      E.tensor_tensor(sc.h[j].ap()[0:mr, tb * TB:(tb + 1) * TB],
                                      ps, sc.r_pp[pp].ap()[0:mr,
                                                           tb * TB:(tb + 1) * TB],
                                      ALU.mult), inc=kb.DV)
                kb.guard_bank(b, kb.DV)
                if tb == 1:
                    sc.gset(kb, ("r", pp), kb.DV)
            sc.gset(kb, ("wg", pp), kb.PE)
            sc.gset(kb, ("wa", pp), kb.PE)

        wls = {}
        wls[0] = stream_j(0)
        stage_g(0, wls[0])
        for j in range(njc):
            if j + 1 < njc:
                wls[j + 1] = stream_j(j + 1)
                stage_g(j + 1, wls[j + 1])
            if j >= 0:
                stage_a(j, wls[j])
        kb.fence_pe()
        def evac(mi, tb, ps, mrows, hoff=hoff):
            kb.wait("vector", kb.PE)
            kb.op("vector", lambda E, mi=mi, tb=tb, ps=ps, hoff=hoff:
                  E.scalar_tensor_tensor(st.zs(mi)[:, hoff + tb * TB:hoff + (tb + 1) * TB],
                                         ps, p.b2.ap()[:, mi:mi + 1],
                                         st.zs(mi)[:, hoff + tb * TB:hoff + (tb + 1) * TB],
                                         ALU.add, ALU.add), inc=kb.DV)
        emit_proj_fm(kb, p.w2, DI, D,
                     srcb=lambda c: sc.h[c].ap()[:, 0:1024],
                     evac=evac, evac_sem=kb.DV, mw=128, banks=(6, 7), ntb=2,
                     pre="L0_ffn_W2")
        kb.phase_fence()


def l0ffn_sublayer(kb, st, p):
    m_attn = kb.mark()
    sc = FfnScratch(kb)
    refresh_zb(kb, st)
    kb.phase_fence()
    for pair in range(NCH):
        pp = pair % 2
        wsem, wl = stream_pair_weights(kb, sc, p, pair, pp)
        kb.wait("tensor", wsem, wl)
        qk_prep(kb, st, sc, p, pair, "q", pp)
        qk_prep(kb, st, sc, p, pair, "k", pp)
        v_prep(kb, st, sc, pair, pp)
        for name in ("q", "k", "v"):
            sc.gd[("wb", name, pp)] = (kb.PE, kb.PE.n)
        local_attn_pair(kb, st, sc, pair)
    kb.phase_fence()
    wo_residual(kb, st, sc, p)
    kb.phase_fence()
    kb.reset(m_attn)
    refresh_zb(kb, st)
    emit_ln(kb, src=lambda c: st.zs(c, shifted=True),
            srcb=lambda c: st.zbs(c, shifted=True),
            dst=lambda c: st.zbs(c), grow=p.lng, bcol=p.lnb)
    kb.phase_fence()
    glu_w2(kb, st, p)
    kb.phase_fence()
    kb.reset(m_attn)


ROT_CONSTS = ["c_rotCp", "c_rotS2p", "c_rotCm", "c_rotS2m"]

def l0ffn_invocation(kb, st, d, xspill, nlayers=6):
    minv = kb.mark()
    kb.load_pconsts(ROT_CONSTS)
    spill = kb.take_sems(1, "sync")[0]
    kb.phase_fence()
    for c in range(NCH):
        kb.dma("sync", xspill[c * 128:(c + 1) * 128, :], st.zs(c), inc=spill)
    spill_n = spill.n
    kb.wait("vector", spill, spill_n)
    onesrow = kb.const("c_ones_rowD")
    emit_ln(kb, src=lambda c: st.zs(c), srcb=None,
            dst=lambda c: st.zs(c), grow=onesrow, bcol=None)
    refresh_zb(kb, st)
    kb.phase_fence()
    for li in range(nlayers):
        m = kb.mark()
        p = FfnParams(kb, d, li)
        l0ffn_sublayer(kb, st, p)
        kb.reset(m)
    m = kb.mark()
    xold = [kb.tile(128, T, F32) for _ in range(2)]
    tmp = [kb.tile(128, TB, F32) for _ in range(2)]
    rsems = kb.take_sems(2, "sync")
    rn0 = []
    for c in range(NCH):
        if c >= 2:
            kb.wait("sync", kb.DV, rn0[c - 2])
        kb.dma("sync", xold[c % 2].ap(), xspill[c * 128:(c + 1) * 128, :],
               inc=rsems[c % 2])
        rl = rsems[c % 2].n
        for tb in range(NTB):
            pp = tb % 2
            kb.wait("vector", rsems[c % 2], rl)
            kb.wait("vector", kb.DV)
            kb.op("vector", lambda E, c=c, tb=tb, pp=pp:
                  E.tensor_scalar_mul(tmp[pp].ap(), st.zs(c)[:, tb * TB:(tb + 1) * TB],
                                      0.5), inc=kb.DV)
            kb.wait("vector", kb.DV)
            kb.op("vector", lambda E, c=c, tb=tb, pp=pp:
                  E.tensor_tensor(st.zs(c)[:, tb * TB:(tb + 1) * TB], tmp[pp].ap(),
                                  xold[c % 2].ap()[:, tb * TB:(tb + 1) * TB],
                                  ALU.add), inc=kb.DV)
        rn0.append(kb.DV.n)
    kb.phase_fence()
    kb.reset(m)
    kb.clear_pconsts(ROT_CONSTS)
    kb.reset(minv)


"""Dense attention (L0_attn, L1_attn) and MoE (L1_ffn) emitters."""
import numpy as np
from ml_dtypes import bfloat16

SLOPES = (2.0 ** (-8.0 * np.arange(1, 17) / 16.0)).astype(np.float64)


def dense_consts():
    c = {}
    i = np.arange(128)[:, None]
    U0 = 1920
    u = np.arange(2304)[None, :]
    c["c_dmaster"] = ((u - U0) - i).astype(np.float32) * np.ones((128, 2304), np.float32)
    j = np.arange(128)[None, :]
    dm = (j - i).astype(np.float32) * np.ones((128, 128), np.float32)
    dm[np.broadcast_to(j > i, dm.shape)] = -3e11
    c["c_dmask"] = dm.astype(np.float32)
    return c


class DenseScratch:
    def __init__(self, kb, alibi, ptbufs=2):
        self.w_b = {n: [kb.tile(128, 8 * 128, BF16)] * 2
                    for n in ("q", "k", "v")}
        self.qf = kb.tile(128, T, BF16)
        self.kf = kb.tile(128, T, BF16)
        self.v_tm = [kb.tile(128, 128, BF16) for _ in range(16)]
        self.U = [kb.tile(128, T, F32) for _ in range(2)] if alibi else None
        self.P = [kb.tile(128, T, BF16) for _ in range(2)]
        self.P2 = kb.tile(128, T, BF16)
        self.PT = kb.tile(128, 4 * 128, BF16)
        self.nb = [kb.tile(128, 1, F32) for _ in range(2)]
        self.nb8 = [kb.tile(128, 1, F32) for _ in range(2)]
        self.sm = [kb.tile(128, 1, F32) for _ in range(2)]
        self.rs = [kb.tile(128, 1, F32) for _ in range(2)]
        self.out_all = [kb.tile(128, T, BF16) for _ in range(NCH)]
        self.wsem = kb.take_sems(2, "syncs")
        self.gd = {}

    def gwait(self, kb, eng, key):
        g = self.gd.get(key)
        if g:
            kb.wait(eng, g[0], g[1])

    def gset(self, kb, key, sem):
        self.gd[key] = (sem, sem.n)


def qkv_dense(kb, sc, srcb, pair, pp, which):
    wb = sc.w_b[which][pp]
    if which in ("q", "k"):
        dst = sc.qf if which == "q" else sc.kf
        for tb in range(NTB):
            b = tb % 2
            ps = kb.bank(b)
            kb.use_bank(b, "tensor")
            for kc in range(NCH):
                kb.op("tensor", lambda E, kc=kc, tb=tb, ps=ps:
                      E.matmul(ps, wb.ap()[:, kc * 128:(kc + 1) * 128],
                               srcb(kc)[:, tb * TB:(tb + 1) * TB],
                               start=(kc == 0), stop=(kc == NCH - 1)),
                      inc=(kb.PE if kc == NCH - 1 else None))
            sc.gwait(kb, "scalar", ("qk", which)) if tb == 0 else None
            kb.wait("scalar", kb.PE)
            kb.op("scalar", lambda E, ps=ps, tb=tb:
                  E.activation(dst.ap()[:, tb * TB:(tb + 1) * TB], ps, AF.Copy),
                  inc=kb.AC)
            kb.guard_bank(b, kb.AC)
    else:
        sc.gwait(kb, "scalar", ("vtm",))
        for tt in range(16):
            b = tt % 2
            ps = kb.bank(b, rows=128, cols=128)
            kb.use_bank(b, "tensor")
            for kc in range(NCH):
                kb.op("tensor", lambda E, kc=kc, tt=tt, ps=ps:
                      E.matmul(ps, srcb(kc)[:, tt * 128:(tt + 1) * 128],
                               wb.ap()[:, kc * 128:(kc + 1) * 128],
                               start=(kc == 0), stop=(kc == NCH - 1)),
                      inc=(kb.PE if kc == NCH - 1 else None))
            kb.wait("scalar", kb.PE)
            kb.op("scalar", lambda E, ps=ps, d=sc.v_tm[tt]:
                  E.activation(d.ap(), ps, AF.Copy), inc=kb.AC)
            kb.guard_bank(b, kb.AC)


def dense_attn_pair(kb, sc, pair, alibi):
    idn = kb.const("c_idn")
    tri = kb.const("c_tri")
    negI = kb.const("c_negI")
    if alibi:
        dmaster = kb.pconst("c_dmaster")
        dmask = kb.pconst("c_dmask")
    out = sc.out_all[pair]
    U0 = 1920

    def stage_A(it):
        hl, qt, pq = it["hl"], it["qt"], it["pq"]
        slope = float(SLOPES[2 * pair + hl])
        nfull = qt // 4
        a_d = qt % 4
        ncol_d = (a_d + 1) * 128
        for b_ in range(nfull + 1):
            ps_cols = 512 if b_ < nfull else ncol_d
            kb.use_bank(b_, "tensor")
            nmm = 1 if b_ < nfull else 2
            kb.op("tensor", lambda E, b_=b_, qt=qt, hl=hl, ps_cols=ps_cols,
                  en=(nmm == 1):
                  E.matmul(kb.bank(b_, cols=ps_cols),
                           sc.qf.ap()[hl * 64:(hl + 1) * 64,
                                      qt * 128:(qt + 1) * 128],
                           sc.kf.ap()[hl * 64:(hl + 1) * 64,
                                      b_ * 512:b_ * 512 + ps_cols],
                           start=True, stop=en),
                  inc=(kb.PE if nmm == 1 else None))
            if b_ == nfull:
                kb.op("tensor", lambda E, b_=b_, a_d=a_d:
                      E.matmul(kb.bank(b_, cols=128, coloff=a_d * 128),
                               tri.ap(), negI.ap(), start=False, stop=True),
                      inc=kb.PE)
        span = nfull * 512 + ncol_d
        it["span"] = span
        if alibi:
            sc.gwait(kb, "vector", ("U", pq))
            kb.wait("vector", kb.PE)
            for b_ in range(nfull):
                aoff = qt - 4 * b_
                kb.op("vector", lambda E, b_=b_, aoff=aoff, pq=pq, slope=slope:
                      E.scalar_tensor_tensor(
                          sc.U[pq].ap()[:, b_ * 512:(b_ + 1) * 512],
                          dmaster.ap()[:, U0 - 128 * aoff:U0 - 128 * aoff + 512],
                          slope, kb.bank(b_, cols=512), ALU.mult, ALU.add),
                      inc=kb.DV)
            for sub in range(a_d + 1):
                aoff = a_d - sub
                dsl = (dmask.ap() if aoff == 0 else
                       dmaster.ap()[:, U0 - 128 * aoff:U0 - 128 * aoff + 128])
                kb.op("vector", lambda E, sub=sub, pq=pq, slope=slope, dsl=dsl,
                      nfull=nfull:
                      E.scalar_tensor_tensor(
                          sc.U[pq].ap()[:, nfull * 512 + sub * 128:
                                        nfull * 512 + (sub + 1) * 128],
                          dsl, slope,
                          kb.bank(nfull, cols=128, coloff=sub * 128),
                          ALU.mult, ALU.add), inc=kb.DV)
            for b_ in range(nfull + 1):
                kb.guard_bank(b_, kb.DV)
            src_row = sc.U[pq].ap()[:, 0:span]
            kb.wait("vector", kb.DV)
        else:
            kb.wait("vector", kb.PE)
            src_row = kb.ps_view(128, 0, 0, span)
        sc.gwait(kb, "vector", ("nb", pq))
        kb.op("vector", lambda E, s=src_row, d=sc.nb[pq]:
              E.reduce_max(d.ap(), s, axis=AX.X, negate=True), inc=kb.DV)
        kb.wait("vector", kb.DV)
        sc.gwait(kb, "vector", ("nb8", pq))
        kb.op("vector", lambda E, pq=pq:
              E.tensor_scalar_mul(sc.nb8[pq].ap(), sc.nb[pq].ap(), 0.125),
              inc=kb.DV)
        sc.gset(kb, ("nb", pq), kb.DV)
        sc.gwait(kb, "scalar", ("P", pq))
        sc.gwait(kb, "scalar", ("sm", pq))
        kb.wait("scalar", kb.DV)
        kb.op("scalar", lambda E, s=src_row, pq=pq, span=span:
              E.activation(sc.P[pq].ap()[:, 0:span], s, AF.Exp, scale=0.125,
                           bias=sc.nb8[pq].ap(), accum_out=sc.sm[pq].ap()),
              inc=kb.AC)
        sc.gset(kb, ("nb8", pq), kb.AC)
        if alibi:
            sc.gset(kb, ("U", pq), kb.AC)
        else:
            for b_ in range(qt // 4 + 1):
                kb.guard_bank(b_, kb.AC)
        it["acexp"] = kb.AC.n

    def stage_BC(it):
        hl, qt, pq = it["hl"], it["qt"], it["pq"]
        span = it["span"]
        nkc = span // 128
        sc.gwait(kb, "vector", ("rs", pq))
        kb.wait("vector", kb.AC, it["acexp"])
        kb.op("vector", lambda E, pq=pq:
              E.reciprocal(sc.rs[pq].ap(), sc.sm[pq].ap()), inc=kb.DV)
        sc.gset(kb, ("sm", pq), kb.DV)
        sc.gwait(kb, "scalar", ("P2",))
        kb.wait("scalar", kb.DV)
        kb.op("scalar", lambda E, pq=pq, span=span:
              E.activation(sc.P2.ap()[:, 0:span], sc.P[pq].ap()[:, 0:span],
                           AF.Copy, scale=sc.rs[pq].ap()), inc=kb.AC)
        sc.gset(kb, ("rs", pq), kb.AC)
        sc.gset(kb, ("P", pq), kb.AC)
        acp2 = kb.AC.n
        tb_ = 4 + pq
        ob = 6 + pq
        eng = "vector" if pq == 0 else "scalar"
        esem = kb.DV if pq == 0 else kb.AC
        kb.use_bank(ob, "tensor")
        kb.wait("tensor", kb.AC, acp2)
        n_ = 0
        for grpb in range(0, nkc, 4):
            kb.use_bank(tb_, "tensor")
            gn = min(4, nkc - grpb)
            for ii in range(gn):
                kb.op("tensor", lambda E, grpb=grpb, ii=ii, tb_=tb_:
                      E.transpose(kb.ps_view(128, 0, 512 * tb_ + ii * 64, 64, BF16),
                                  sc.P2.ap()[:, (grpb + ii) * 128:
                                             (grpb + ii + 1) * 128],
                                  idn.ap()),
                      inc=(kb.PE if ii == gn - 1 else None))
            kb.wait(eng, kb.PE)
            for ii in range(gn):
                kb.op(eng, lambda E, ii=ii, tb_=tb_:
                      (E.tensor_copy if pq == 0 else
                       (lambda o, i_: E.activation(o, i_, AF.Copy)))(
                          sc.PT.ap()[:, ii * 128:(ii + 1) * 128],
                          kb.ps_view(128, 0, 512 * tb_ + ii * 64, 64, BF16)),
                      inc=(esem if ii == gn - 1 else None))
            kb.guard_bank(tb_, esem)
            kb.wait("tensor", esem)
            for ii in range(gn):
                n_ += 1
                kb.op("tensor", lambda E, kc=grpb + ii, ii=ii, hl=hl, ob=ob,
                      st_=(n_ == 1), en=(n_ == nkc):
                      E.matmul(kb.ps_view(64, 0, 512 * ob, 128),
                               sc.v_tm[kc].ap()[:, hl * 64:(hl + 1) * 64],
                               sc.PT.ap()[:, ii * 128:(ii + 1) * 128],
                               start=st_, stop=en),
                      inc=(kb.PE if n_ == nkc else None))
        sc.gset(kb, ("P2",), kb.PE)
        kb.wait("scalar", kb.PE)
        kb.op("scalar", lambda E, hl=hl, qt=qt, ob=ob:
              E.activation(out.ap()[hl * 64:(hl + 1) * 64,
                                    qt * 128:(qt + 1) * 128],
                           kb.ps_view(64, 0, 512 * ob, 128), AF.Copy), inc=kb.AC)
        kb.guard_bank(ob, kb.AC)

    prev = None
    for hl in range(2):
        for qt in range(16):
            it = {"hl": hl, "qt": qt, "pq": qt % 2}
            stage_A(it)
            if prev is not None:
                stage_BC(prev)
            prev = it
    stage_BC(prev)
    sc.gset(kb, ("qk", "q"), kb.PE)
    sc.gset(kb, ("qk", "k"), kb.PE)
    sc.gset(kb, ("vtm",), kb.PE)


def l0_attn_slot(kb, st, d):
    onesrow = kb.const("c_ones_rowD")
    klib_emitln(kb, st, onesrow, None)
    m = kb.mark()
    kb.load_pconsts(["c_dmaster", "c_dmask"])
    sc = DenseScratch(kb, alibi=True)
    ps = kb.take_sems(1, "sync")[0]
    bp = load_cols_f32(kb, d["L0_attn_b_proj"], D, sem=ps)
    kb.wait("vector", ps, ps.n)
    kb.phase_fence()
    for pair in range(NCH):
        pp = pair % 2
        s = sc.wsem[pp]
        for which, wd in (("q", d["L0_attn_W_query"]), ("k", d["L0_attn_W_key"]),
                          ("v", d["L0_attn_W_value"])):
            kb.wait_precast("sync", {"q": "L0_attn_W_query", "k": "L0_attn_W_key",
                                     "v": "L0_attn_W_value"}[which])
            sc.gwait(kb, "sync", ("wb", which, 0))
            sc.gwait(kb, "sync", ("wb", which, 1))
            kb.dma("sync",
                   sc.w_b[which][pp].ap().rearrange("p (kc m) -> p kc m", kc=NCH),
                   wd[:, pair * 128:(pair + 1) * 128]
                   .rearrange("(kc kp) m -> kp kc m", kp=128), inc=s)
        kb.wait("tensor", s, s.n)
        qkv_dense(kb, sc, lambda c: st.zbs(c), pair, pp, "q")
        qkv_dense(kb, sc, lambda c: st.zbs(c), pair, pp, "k")
        qkv_dense(kb, sc, lambda c: st.zbs(c), pair, pp, "v")
        for name in ("q", "k", "v"):
            sc.gd[("wb", name, pp)] = (kb.PE, kb.PE.n)
        dense_attn_pair(kb, sc, pair, alibi=True)
    kb.phase_fence()

    def evac(mi, tb, psu, mrows):
        kb.wait("vector", kb.PE)
        kb.wait("vector", kb.DV)
        kb.op("vector", lambda E, mi=mi, tb=tb, psu=psu:
              E.scalar_tensor_tensor(st.zs(mi)[:, tb * TB:(tb + 1) * TB], psu,
                                     bp.ap()[:, mi:mi + 1],
                                     st.zs(mi)[:, tb * TB:(tb + 1) * TB],
                                     ALU.add, ALU.add), inc=kb.DV)
    kb.dbg = {"sc": sc}
    emit_proj_fm(kb, d["L0_attn_W_proj"], D, D,
                 srcb=lambda c: sc.out_all[c].ap()[:, 0:T],
                 evac=evac, evac_sem=kb.DV, mw=256, banks=(0, 1, 2, 3),
                 pre="L0_attn_W_proj")
    kb.phase_fence()
    kb.clear_pconsts(["c_dmaster", "c_dmask"])
    kb.reset(m)


def klib_emitln(kb, st, grow, bcol, dstf32=False):
    emit_ln(kb, src=lambda c: st.zs(c), srcb=None,
            dst=(lambda c: st.zs(c)) if dstf32 else (lambda c: st.zbs(c)),
            grow=grow, bcol=bcol)
    if dstf32:
        refresh_zb(kb, st)
    kb.phase_fence()


def l1_attn_slot(kb, st, d):
    onesrow = kb.const("c_ones_rowD")
    klib_emitln(kb, st, onesrow, None)
    m = kb.mark()
    yb = [kb.tile(128, T, BF16) for _ in range(NCH)]
    ps = kb.take_sems(1, "sync")[0]
    pg = kb.take_sems(1, "gp")[0]
    lw = load_row_bf16(kb, d["L1_attn_ln_w"], D, sem=pg)
    lb = load_cols_f32(kb, d["L1_attn_ln_b"], D, sem=ps)
    for e in ("tensor", "vector", "scalar", "gpsimd"):
        kb.wait(e, ps, ps.n)
        kb.wait(e, pg, pg.n)
    emit_ln(kb, src=None, srcb=lambda c: st.zbs(c),
            dst=lambda c: yb[c].ap()[:, 0:T], grow=lw, bcol=lb, src_is_bf16=True)
    kb.phase_fence()
    sc = DenseScratch(kb, alibi=False, ptbufs=1)
    kb.phase_fence()
    for pair in range(NCH):
        pp = pair % 2
        s = sc.wsem[pp]
        for which, wd in (("q", d["L1_attn_wq"]), ("k", d["L1_attn_wk"]),
                          ("v", d["L1_attn_wv"])):
            kb.wait_precast("sync", {"q": "L1_attn_wq", "k": "L1_attn_wk",
                                     "v": "L1_attn_wv"}[which])
            sc.gwait(kb, "sync", ("wb", which, 0))
            sc.gwait(kb, "sync", ("wb", which, 1))
            for hl in range(2):
                kb.dma("sync",
                       sc.w_b[which][pp].ap()
                       .rearrange("p (kc m) -> p kc m", kc=NCH)[:, :, hl * 64:(hl + 1) * 64],
                       wd[2 * pair + hl].rearrange("(kc kp) m -> kp kc m", kp=128),
                       inc=s)
        kb.wait("tensor", s, s.n)
        qkv_dense(kb, sc, lambda c: yb[c].ap()[:, 0:T], pair, pp, "q")
        qkv_dense(kb, sc, lambda c: yb[c].ap()[:, 0:T], pair, pp, "k")
        qkv_dense(kb, sc, lambda c: yb[c].ap()[:, 0:T], pair, pp, "v")
        for name in ("q", "k", "v"):
            sc.gd[("wb", name, pp)] = (kb.PE, kb.PE.n)
        dense_attn_pair(kb, sc, pair, alibi=False)
    kb.phase_fence()

    def evac(mi, tb, psu, mrows):
        kb.wait("vector", kb.PE)
        kb.wait("vector", kb.DV)
        kb.op("vector", lambda E, mi=mi, tb=tb, psu=psu:
              E.scalar_tensor_tensor(st.zs(mi)[:, tb * TB:(tb + 1) * TB], psu, 0.0,
                                     st.zs(mi)[:, tb * TB:(tb + 1) * TB],
                                     ALU.add, ALU.add), inc=kb.DV)
        kb.wait("vector", kb.DV)
        kb.op("vector", lambda E, mi=mi, tb=tb:
              E.scalar_tensor_tensor(st.zs(mi)[:, tb * TB:(tb + 1) * TB],
                                     st.zbs(mi)[:, tb * TB:(tb + 1) * TB], 1.0,
                                     st.zs(mi)[:, tb * TB:(tb + 1) * TB],
                                     ALU.mult, ALU.add), inc=kb.DV)
    kb.dbg = {"yb": yb, "sc": sc}
    emit_proj_fm(kb, d["L1_attn_wo"], D, D,
                 srcb=lambda c: sc.out_all[c].ap()[:, 0:T],
                 evac=evac, evac_sem=kb.DV, mw=256, banks=(0, 1, 2, 3),
                 pre="L1_attn_wo")
    kb.phase_fence()
    kb.reset(m)


def moe_slot(kb, st, d):
    onesrow = kb.const("c_ones_rowD")
    ones128 = kb.const("c_ones_row")
    ones512 = kb.const("c_ones512")
    idn = kb.const("c_idn")
    klib_emitln(kb, st, onesrow, None)
    m0 = kb.mark()
    ps_ = kb.take_sems(1, "sync")[0]
    pg_ = kb.take_sems(1, "gp")[0]
    rw = kb.tile(128, 8 * 16, BF16)
    kb.dma("gpsimd", rw.ap().rearrange("p (kc m) -> p kc m", kc=NCH)[:, :, 0:15],
           d["L1_ffn_router_W"].rearrange("(kc kp) m -> kp kc m", kp=128), inc=pg_)
    rb = load_row_bf16(kb, d["L1_ffn_router_b"], 15, sem=pg_)
    sb1r = load_row_bf16(kb, d["L1_ffn_sb1"], D, sem=pg_)
    sb2r = load_row_bf16(kb, d["L1_ffn_sb2"], D, sem=pg_)
    eb2t = kb.tile(15, D, BF16)
    kb.dma("gpsimd", eb2t.ap(), d["L1_ffn_eb2"], inc=pg_)
    for e in ("tensor", "vector", "scalar", "gpsimd"):
        kb.wait(e, pg_, pg_.n)
        kb.wait(e, ps_, ps_.n)
    cw_fm = kb.tile(16, T, BF16)
    cw_tm = kb.tile(128, 128, BF16)
    l_sb = kb.tile(128, 15, F32)
    m1 = kb.tile(128, 1, F32)
    ee = kb.tile(128, 15, F32)
    ssum = kb.tile(128, 1, F32)
    rsr = kb.tile(128, 1, F32)
    tt_ = kb.tile(128, 15, F32)
    gi = kb.tile(128, 15, F32)
    msum = kb.tile(128, 15, F32)
    kb.op("vector", lambda E: E.memset(cw_tm.ap(), 0.0), inc=kb.DV)
    gdr = {}
    for tt in range(16):
        b = tt % 2
        psl = kb.bank(b, cols=15)
        kb.use_bank(b, "tensor")
        for kc in range(NCH):
            kb.op("tensor", lambda E, kc=kc, tt=tt, psl=psl:
                  E.matmul(psl, st.zbs(kc)[:, tt * 128:(tt + 1) * 128],
                           rw.ap().rearrange("p (kc m) -> p kc m", kc=NCH)[:, kc, 0:15],
                           start=(kc == 0), stop=False))
        kb.op("tensor", lambda E, psl=psl:
              E.matmul(psl, ones128.ap(), rb.ap(), start=False, stop=True), inc=kb.PE)
        kb.wait("vector", kb.PE)
        kb.wait("vector", kb.DV)
        kb.op("vector", lambda E, psl=psl: E.tensor_copy(l_sb.ap(), psl), inc=kb.DV)
        kb.guard_bank(b, kb.DV)
        kb.wait("vector", kb.DV)
        kb.op("vector", lambda E: E.reduce_max(m1.ap(), l_sb.ap(), axis=AX.X,
                                               negate=True), inc=kb.DV)
        kb.wait("scalar", kb.DV)
        kb.op("scalar", lambda E: E.activation(ee.ap(), l_sb.ap(), AF.Exp,
                                               bias=m1.ap(), accum_out=ssum.ap()),
              inc=kb.AC)
        kb.wait("vector", kb.AC)
        kb.op("vector", lambda E: E.reciprocal(rsr.ap(), ssum.ap()), inc=kb.DV)
        kb.wait("vector", kb.DV)
        kb.op("vector", lambda E: E.tensor_copy(tt_.ap(), ee.ap()), inc=kb.DV)
        for it in range(3):
            kb.wait("vector", kb.DV)
            kb.op("vector", lambda E: E.reduce_max(m1.ap(), tt_.ap(), axis=AX.X),
                  inc=kb.DV)
            kb.wait("vector", kb.DV)
            kb.op("vector", lambda E: E.tensor_single_scalar(gi.ap(), tt_.ap(),
                                                             m1.ap(), ALU.is_ge),
                  inc=kb.DV)
            kb.wait("vector", kb.DV)
            if it == 0:
                kb.op("vector", lambda E: E.tensor_copy(msum.ap(), gi.ap()),
                      inc=kb.DV)
            else:
                kb.op("vector", lambda E: E.tensor_add(msum.ap(), msum.ap(), gi.ap()),
                      inc=kb.DV)
            if it < 2:
                kb.wait("vector", kb.DV)
                kb.op("vector", lambda E: E.scalar_tensor_tensor(
                    tt_.ap(), gi.ap(), -1e30, tt_.ap(), ALU.mult, ALU.add), inc=kb.DV)
        kb.wait("vector", kb.DV)
        g = gdr.get("cwtm")
        if g:
            kb.wait("vector", g[0], g[1])
        kb.op("vector", lambda E: E.scalar_tensor_tensor(
            cw_tm.ap()[:, 0:15], ee.ap(), rsr.ap(), msum.ap(), ALU.mult, ALU.mult),
            inc=kb.DV)
        kb.use_bank(2, "tensor")
        kb.wait("tensor", kb.DV)
        kb.op("tensor", lambda E, tt=tt:
              E.transpose(kb.ps_view(128, 0, 512 * 2, 64, BF16), cw_tm.ap(), idn.ap()),
              inc=kb.PE)
        gdr["cwtm"] = (kb.PE, kb.PE.n)
        kb.wait("vector", kb.PE)
        kb.op("vector", lambda E, tt=tt:
              E.tensor_copy(cw_fm.ap()[0:16, tt * 128:(tt + 1) * 128],
                            kb.ps_view(16, 0, 512 * 2, 64, BF16)), inc=kb.DV)
        kb.guard_bank(2, kb.DV)
    kb.phase_fence()
    he = [[kb.tile(128, T, BF16) for _ in range(NCH)] for _ in range(2)]
    hs = he[0]
    cwb = [kb.tile(128, T, BF16) for _ in range(2)]
    crow = [kb.tile(1, TB, BF16) for _ in range(2)]
    w2buf = [[kb.tile(128, 8 * 128, BF16) for _ in range(2)] for _ in range(2)]
    csem = kb.take_sems(2, "gp")
    w2sem = kb.take_sems(4, "syncs")
    gd = {}
    def gwait(eng, key):
        g = gd.get(key)
        if g:
            kb.wait(eng, g[0], g[1])
    def gset(key, sem):
        gd[key] = (sem, sem.n)

    def w1_evac_shared(mi, tb, psu, mrows):
        kb.wait("scalar", kb.PE)
        kb.op("scalar", lambda E, mi=mi, tb=tb, psu=psu:
              E.activation(hs[mi].ap()[:, tb * TB:(tb + 1) * TB], psu, AF.Relu),
              inc=kb.AC)
    emit_proj_fm(kb, d["L1_ffn_sW1"], D, D, srcb=lambda c: st.zbs(c),
                 evac=w1_evac_shared, evac_sem=kb.AC, mw=256, banks=(0, 1, 2, 3),
                 aug=(lambda mi: sb1r.ap()[0:1, mi * 128:(mi + 1) * 128],
                      lambda mi, tb: ones512.ap()), pre="L1_ffn_sW1")
    kb.fence_pe()
    def w2_evac(mi, tb, psu, mrows):
        kb.wait("vector", kb.PE)
        kb.wait("vector", kb.DV)
        kb.op("vector", lambda E, mi=mi, tb=tb, psu=psu:
              E.scalar_tensor_tensor(st.zs(mi)[:, tb * TB:(tb + 1) * TB], psu, 0.5,
                                     st.zs(mi)[:, tb * TB:(tb + 1) * TB],
                                     ALU.mult, ALU.add), inc=kb.DV)
    emit_proj_fm(kb, d["L1_ffn_sW2"], D, D, srcb=lambda c: hs[c].ap()[:, 0:T],
                 evac=w2_evac, evac_sem=kb.DV, mw=256, banks=(4, 5, 6, 7),
                 aug=(lambda mi: sb2r.ap()[0:1, mi * 128:(mi + 1) * 128],
                      lambda mi, tb: ones512.ap()), pre="L1_ffn_sW2")
    kb.phase_fence()
    eb1r = [kb.tile(1, D, BF16) for _ in range(2)]
    ebsem = kb.take_sems(2, "gp")
    groups = [(0, 1), (2, 3), (4, 5), (6, 7), (8, 9), (10, 11), (12, 13), (14,)]
    for gidx, grp in enumerate(groups):
        for ei, e in enumerate(grp):
            pp = ei
            gwait("gpsimd", ("eb1", pp))
            kb.dma("gpsimd", eb1r[pp].ap(),
                   d["L1_ffn_eb1"][e].rearrange("(o n) -> o n", o=1), inc=ebsem[pp])
            en_ = ebsem[pp].n
            for tb in range(NTB):
                gwait("gpsimd", ("crow", pp)) if tb == 0 else None
                kb.wait("gpsimd", kb.PE)
                kb.dma("gpsimd", crow[pp].ap(),
                       cw_fm.ap()[e:e + 1, tb * TB:(tb + 1) * TB], inc=csem[pp])
                cn = csem[pp].n
                bb = 2 + tb % 2
                kb.use_bank(bb, "tensor")
                kb.wait("tensor", csem[pp], cn)
                kb.op("tensor", lambda E, tb=tb, pp=pp, bb=bb:
                      E.matmul(kb.bank(bb), ones128.ap(),
                               crow[pp].ap(), start=True, stop=True), inc=kb.PE)
                gwait("scalar", ("cwb", pp)) if tb == 0 else None
                kb.wait("scalar", kb.PE)
                kb.op("scalar", lambda E, tb=tb, pp=pp, bb=bb:
                      E.activation(cwb[pp].ap()[:, tb * TB:(tb + 1) * TB],
                                   kb.bank(bb), AF.Copy), inc=kb.AC)
                kb.guard_bank(bb, kb.AC)
            gset(("crow", pp), kb.AC)
            kb.wait("tensor", ebsem[pp], en_)
            def w1_evac_e(mi, tb, psu, mrows, pp=pp):
                kb.wait("vector", kb.PE)
                kb.wait("vector", kb.AC)
                kb.wait("vector", kb.DV)
                kb.op("vector", lambda E, mi=mi, tb=tb, psu=psu, pp=pp:
                      E.scalar_tensor_tensor(he[pp][mi].ap()[:, tb * TB:(tb + 1) * TB],
                                             psu, 0.0,
                                             cwb[pp].ap()[:, tb * TB:(tb + 1) * TB],
                                             ALU.max, ALU.mult), inc=kb.DV)
            emit_proj_fm(kb, d["L1_ffn_eW1"][e], D, D, srcb=lambda c: st.zbs(c),
                         evac=w1_evac_e, evac_sem=kb.DV, mw=256, banks=(0, 1, 2, 3),
                         aug=(lambda mi, pp=pp: eb1r[pp].ap()[0:1, mi * 128:(mi + 1) * 128],
                              lambda mi, tb: ones512.ap()), pre="L1_ffn_eW1")
            gset(("eb1", pp), kb.PE)
            gset(("cwb", pp), kb.DV)
        kb.fence_pe()
        w2n = [0, 0]
        for fc in range(NCH):
            fp = fc % 2
            for ei, e in enumerate(grp):
                gwait("sync", ("w2", ei, fp))
                kb.wait_precast("sync", "L1_ffn_eW2")
                kb.dma("sync",
                       w2buf[ei][fp].ap().rearrange("p (kc m) -> p kc m", kc=NCH),
                       d["L1_ffn_eW2"][e][:, fc * 128:(fc + 1) * 128]
                       .rearrange("(kc kp) m -> kp kc m", kp=128),
                       inc=w2sem[ei * 2 + fp])
            wn = [w2sem[ei * 2 + fp].n for ei in range(len(grp))]
            for tb in range(NTB):
                bb = 4 + (fc * NTB + tb) % 4
                kb.use_bank(bb, "tensor")
                if tb == 0:
                    for ei in range(len(grp)):
                        kb.wait("tensor", w2sem[ei * 2 + fp], wn[ei])
                nmm = len(grp) * NCH + (1 if gidx == 0 else 0)
                n_ = 0
                for ei, e in enumerate(grp):
                    for kc in range(NCH):
                        n_ += 1
                        en_f = (n_ == nmm)
                        kb.op("tensor", lambda E, ei=ei, kc=kc, tb=tb, fp=fp, bb=bb,
                              st_=(n_ == 1), en=en_f:
                              E.matmul(kb.bank(bb),
                                       w2buf[ei][fp].ap()[:, kc * 128:(kc + 1) * 128],
                                       he[ei][kc].ap()[:, tb * TB:(tb + 1) * TB],
                                       start=st_, stop=en),
                              inc=(kb.PE if en_f else None))
                if gidx == 0:
                    n_ += 1
                    kb.op("tensor", lambda E, fc=fc, tb=tb, bb=bb:
                          E.matmul(kb.bank(bb), eb2t.ap()[0:15, fc * 128:(fc + 1) * 128],
                                   cw_fm.ap()[0:15, tb * TB:(tb + 1) * TB],
                                   start=False, stop=True), inc=kb.PE)
                w2_evac(fc, tb, kb.bank(bb), 128)
                kb.guard_bank(bb, kb.DV)
            for ei in range(len(grp)):
                gset(("w2", ei, fp), kb.PE)
        kb.phase_fence()
    kb.reset(m0)


import concourse.bass as bass
from concourse.bass_utils import run_bass_kernel_spmd

_hostc().update(dense_consts())

FFN_CONSTS = ["c_idn", "c_tri", "c_tri2", "c_negI", "c_ones_col", "c_ones_row",
              "c_ones512", "c_ones_rowD", "c_blk2", "c_tiny", "c_eps"]
BASE_CONSTS = FFN_CONSTS
L0A_CONSTS = ["c_idn", "c_tri", "c_negI", "c_ones_col", "c_ones_row", "c_ones_rowD",
              "c_eps"]
L1A_CONSTS = ["c_idn", "c_tri", "c_negI", "c_ones_col", "c_ones_rowD", "c_eps"]
MOE_CONSTS = ["c_idn", "c_ones_col", "c_ones_row", "c_ones_rowD", "c_ones512",
              "c_eps"]

FFN_W = ["L0_ffn_Wqkv", "L0_ffn_Wo", "L0_ffn_W1", "L0_ffn_W2", "L0_ffn_q_scale",
         "L0_ffn_k_scale", "L0_ffn_ln_g", "L0_ffn_ln_b", "L0_ffn_b1", "L0_ffn_b2"]
L0A_W = ["L0_attn_W_query", "L0_attn_W_key", "L0_attn_W_value", "L0_attn_W_proj",
         "L0_attn_b_proj"]
L1A_W = ["L1_attn_ln_w", "L1_attn_ln_b", "L1_attn_wq", "L1_attn_wk", "L1_attn_wv",
         "L1_attn_wo"]
MOE_W = ["L1_ffn_router_W", "L1_ffn_router_b", "L1_ffn_sW1", "L1_ffn_sb1",
         "L1_ffn_sW2", "L1_ffn_sb2", "L1_ffn_eW1", "L1_ffn_eb1", "L1_ffn_eW2",
         "L1_ffn_eb2"]

SHAPES = {
    "L0_ffn_Wqkv": [6, D, 3072], "L0_ffn_Wo": [6, D, D], "L0_ffn_W1": [6, D, 5460],
    "L0_ffn_W2": [6, 2730, D], "L0_ffn_q_scale": [6, 64], "L0_ffn_k_scale": [6, 64],
    "L0_ffn_ln_g": [6, D], "L0_ffn_ln_b": [6, D], "L0_ffn_b1": [6, 5460],
    "L0_ffn_b2": [6, D],
    "L0_attn_W_query": [D, D], "L0_attn_W_key": [D, D], "L0_attn_W_value": [D, D],
    "L0_attn_W_proj": [D, D], "L0_attn_b_proj": [D],
    "L1_attn_ln_w": [D], "L1_attn_ln_b": [D], "L1_attn_wq": [16, D, 64],
    "L1_attn_wk": [16, D, 64], "L1_attn_wv": [16, D, 64], "L1_attn_wo": [D, D],
    "L1_ffn_router_W": [D, 15], "L1_ffn_router_b": [15], "L1_ffn_sW1": [D, D],
    "L1_ffn_sb1": [D], "L1_ffn_sW2": [D, D], "L1_ffn_sb2": [D],
    "L1_ffn_eW1": [15, D, D], "L1_ffn_eb1": [15, D], "L1_ffn_eW2": [15, D, D],
    "L1_ffn_eb2": [15, D],
}


BIG_W = ["L0_ffn_Wqkv", "L0_ffn_W1", "L0_ffn_W2", "L0_ffn_Wo",
         "L0_attn_W_query", "L0_attn_W_key", "L0_attn_W_value", "L0_attn_W_proj",
         "L1_ffn_sW1", "L1_ffn_sW2", "L1_ffn_eW1", "L1_ffn_eW2",
         "L1_attn_wq", "L1_attn_wk", "L1_attn_wv", "L1_attn_wo"]


def _prog_common(wnames, consts):
    nc = bass.Bass("TRN2", target_bir_lowering=False)
    kb = KB(nc)
    d = {n: kb.din(n, SHAPES[n], F32) for n in wnames}
    xd = kb.din("x", [D, T], F32)
    outz = kb.dout("outz", [D, T], F32)
    st = St(kb)
    for c in range(NCH):
        kb.dma("sync", st.zs(c), xd[c * 128:(c + 1) * 128, :])
    kb.preload_consts(consts)
    kb.barrier_loads()
    kb.precast_weights(d, [n for n in BIG_W if n in wnames], SHAPES)
    return nc, kb, d, st, outz


def _prog_finish(nc, kb, st, outz):
    STm = kb.sem("STOUT")
    for c in range(NCH):
        kb.wait("sync", kb.DV)
        kb.wait("sync", kb.GP)
        kb.dma("sync", outz[c * 128:(c + 1) * 128, :], st.zs(c), inc=STm)
    kb.wait("sync", STm)
    kb.finalize()
    return nc, kb


def build_ffn_prog():
    nc, kb, d, st, outz = _prog_common(FFN_W, FFN_CONSTS)
    xspill = kb.dtmp("xspill", [D, T], F32)
    l0ffn_invocation(kb, st, d, xspill, 6)
    return _prog_finish(nc, kb, st, outz)


def build_l0attn_prog():
    nc, kb, d, st, outz = _prog_common(L0A_W, L0A_CONSTS)
    l0_attn_slot(kb, st, d)
    return _prog_finish(nc, kb, st, outz)


def build_l1attn_prog():
    nc, kb, d, st, outz = _prog_common(L1A_W, L1A_CONSTS)
    l1_attn_slot(kb, st, d)
    return _prog_finish(nc, kb, st, outz)


def build_moe_prog():
    nc, kb, d, st, outz = _prog_common(MOE_W, MOE_CONSTS)
    moe_slot(kb, st, d)
    return _prog_finish(nc, kb, st, outz)


def build_fused_prog():
    allw = FFN_W + L0A_W + L1A_W + MOE_W
    nc, kb, d, st, outz = _prog_common(allw, BASE_CONSTS)
    xspill = kb.dtmp("xspill", [D, T], F32)
    l0ffn_invocation(kb, st, d, xspill, 6)
    l0_attn_slot(kb, st, d)
    l0ffn_invocation(kb, st, d, xspill, 6)
    moe_slot(kb, st, d)
    l1_attn_slot(kb, st, d)
    moe_slot(kb, st, d)
    return _prog_finish(nc, kb, st, outz)


def kernel(**inputs):
    xin = inputs["L0_attn_x"]
    ncores = 8
    xs = [np.ascontiguousarray(xin[i].T).astype(np.float32, copy=False)
          for i in range(ncores)]
    nc, kb = build_fused_prog()
    base = dict(kb.inputs)
    for n in FFN_W + L0A_W + L1A_W + MOE_W:
        base[n] = inputs[n]
    in_maps = [dict(base, x=xs[i]) for i in range(ncores)]
    res = run_bass_kernel_spmd(nc, in_maps, core_ids=list(range(ncores)))
    out = np.stack([np.asarray(res.results[i]["outz"]).T for i in range(ncores)],
                   axis=0).astype(np.float32)
    return out


def kernel_unfused(**inputs):
    xin = inputs["L0_attn_x"]
    ncores = 8
    xs = [np.ascontiguousarray(xin[i].T).astype(np.float32, copy=False)
          for i in range(ncores)]
    progs = {}
    progs["ffn"] = build_ffn_prog()
    progs["l0a"] = build_l0attn_prog()
    progs["moe"] = build_moe_prog()
    progs["l1a"] = build_l1attn_prog()
    wsets = {"ffn": FFN_W, "l0a": L0A_W, "moe": MOE_W, "l1a": L1A_W}
    for step in ("ffn", "l0a", "ffn", "moe", "l1a", "moe"):
        nc, kb = progs[step]
        base = dict(kb.inputs)
        for n in wsets[step]:
            base[n] = inputs[n]
        in_maps = [dict(base, x=xs[i]) for i in range(ncores)]
        res = run_bass_kernel_spmd(nc, in_maps, core_ids=list(range(ncores)))
        xs = [np.asarray(res.results[i]["outz"]) for i in range(ncores)]
    out = np.stack([xs[i].T for i in range(ncores)], axis=0).astype(np.float32)
    return out


# version 07
# speedup vs baseline: 1.1529x; 1.1529x over previous
import numpy as np
from ml_dtypes import bfloat16
import concourse.bass as bass
import concourse.mybir as mybir

F32 = mybir.dt.float32
BF16 = mybir.dt.bfloat16
U8 = mybir.dt.uint8
AF = mybir.ActivationFunctionType
ALU = mybir.AluOpType
AX = mybir.AxisListType

T = 2048
D = 1024
NCH = 8
TB = 512
NTB = T // TB
EPS = 1e-5
NEG = -1e30


class Sem:
    __slots__ = ("name", "n", "h")
    def __init__(self, name):
        self.name, self.n, self.h = name, 0, None


class Tile:
    __slots__ = ("kb", "boff", "rows", "cols", "dt")
    def __init__(self, kb, boff, rows, cols, dt):
        self.kb, self.boff, self.rows, self.cols, self.dt = kb, boff, rows, cols, dt

    def ap(self):
        esz = 2 if self.dt == BF16 else 4
        a = self.kb.arena_ap[:, self.boff:self.boff + self.cols * esz].bitcast(self.dt)
        if self.rows != 128:
            a = a[0:self.rows, :]
        return a

    def __getitem__(self, key):
        return self.ap()[key]


class KB:
    ENGINES = ("sync", "tensor", "vector", "scalar", "gpsimd")

    def __init__(self, nc, arena_bytes=212608):
        import contextlib
        self.nc = nc
        self.streams = {e: [] for e in self.ENGINES}
        self.sems = []
        self.arena_bytes = arena_bytes
        self.arena_off = 0
        self.arena_hwm = 0
        self._stack = contextlib.ExitStack()
        arena_t = self._stack.enter_context(nc.sbuf_tensor("ARENA", [128, arena_bytes], U8))
        self.arena_ap = arena_t.ap()
        ps_t = self._stack.enter_context(nc.psum_tensor("PS", [128, 4096], F32))
        self.ps = ps_t.ap()
        self.LD = self.sem("LD")
        self.WL = self.sem("WL")
        self.SD = self.sem("SD")
        self.PE = self.sem("PE")
        self.DV = self.sem("DV")
        self.AC = self.sem("AC")
        self.GP = self.sem("GP")
        self.consts = {}
        self.bank_guards = {}
        self.sealed = False
        self._pool = {"gp": [self.sem(f"SG{i}") for i in range(8)],
                      "gpw": [self.sem(f"SW{i}") for i in range(4)],
                      "sync": [self.sem(f"SY{i}") for i in range(6)],
                      "syncs": [self.sem(f"SS{i}") for i in range(8)],
                      "syncw": [self.sem(f"SV{i}") for i in range(4)]}
        self._pool_i = {"gp": 0, "gpw": 0, "sync": 0, "syncs": 0, "syncw": 0}
        self.precast = {}
        self._sem_q = {}
        self.dram = {}
        self.inputs = {}

    def sem(self, name):
        s = Sem(name)
        s.h = self._stack.enter_context(self.nc.semaphore(name))
        self.sems.append(s)
        return s

    def alloc(self, nbytes, align=4):
        self.arena_off = -(-self.arena_off // align) * align
        r = self.arena_off
        self.arena_off += nbytes
        self.arena_hwm = max(self.arena_hwm, self.arena_off)
        assert self.arena_off <= self.arena_bytes, \
            f"SBUF arena overflow: {self.arena_off} > {self.arena_bytes}"
        return r

    def mark(self):
        return self.arena_off

    def reset(self, m):
        self.arena_off = m

    def tile(self, rows, cols, dt, align=None):
        esz = 2 if dt == BF16 else 4
        boff = self.alloc(cols * esz, align or esz)
        return Tile(self, boff, rows, cols, dt)

    def emit(self, eng, fn):
        self.streams[eng].append(fn)

    def wait(self, eng, sem, val=None):
        v = sem.n if val is None else val
        assert v <= sem.n, f"wait on future value of {sem.name}: {v} > {sem.n}"
        if v <= 0:
            return
        self.emit(eng, lambda E, s=sem, v=v: E.wait_ge(s.h, v))

    def drain(self, eng):
        self.emit(eng, lambda E: E.drain())

    def op(self, eng, builder, inc=None, inck=1):
        if inc is not None:
            inc.n += inck
            def f(E, b=builder, s=inc, k=inck):
                b(E).then_inc(s.h, k)
        else:
            def f(E, b=builder):
                b(E)
        self.emit(eng, f)
        return inc.n if inc is not None else None

    def take_sems(self, n, q="gp"):
        p = self._pool[q]
        i = self._pool_i[q]
        r = [p[(i + k) % len(p)] for k in range(n)]
        self._pool_i[q] = (i + n) % len(p)
        return r

    def dma(self, eng, out_ap, in_ap, inc=None, accum=False):
        if inc is None:
            assert not self.sealed, "post-prologue DMA needs an explicit stream sem"
            inc = self.LD
            assert eng in ("sync", "gpsimd")
            if eng == "gpsimd":
                inc = self.WL
        if accum:
            def b(E, o=out_ap, i=in_ap):
                return E.dma_start(out=(o() if callable(o) else o),
                                   in_=(i() if callable(i) else i),
                                   accum_op=ALU.add)
        else:
            def b(E, o=out_ap, i=in_ap):
                return E.dma_start(out=(o() if callable(o) else o),
                                   in_=(i() if callable(i) else i))
        self.op(eng, b, inc=inc, inck=16)
        return inc.n

    def ps_view(self, rows=128, rowoff=0, col0=0, cols=512, dt=F32):
        a = self.ps[:, col0:col0 + cols]
        if dt != F32:
            a = a.bitcast(dt)
        return a[rowoff:rowoff + rows, :]

    def bank(self, b, rows=128, cols=512, rowoff=0, coloff=0, dt=F32):
        return self.ps_view(rows, rowoff, 512 * b + coloff, cols, dt)

    def use_bank(self, b, eng):
        for sem, cnt in self.bank_guards.get(b, ()):
            self.wait(eng, sem, cnt)
        self.bank_guards[b] = []

    def guard_bank(self, b, sem):
        self.bank_guards.setdefault(b, []).append((sem, sem.n))

    def fence_pe(self):
        self.wait("tensor", self.DV)
        self.wait("tensor", self.AC)
        self.wait("tensor", self.GP)

    def phase_fence(self):
        for e in self.ENGINES:
            for s in (self.PE, self.DV, self.AC, self.GP):
                self.wait(e, s)
        self.bank_guards = {}

    def barrier_loads(self):
        for e in self.ENGINES:
            self.wait(e, self.LD)
            self.wait(e, self.WL)
        self.sealed = True

    def const(self, name, arr=None):
        if name in self.consts:
            return self.consts[name][0]
        if arr is None:
            arr = _hostc()[name]
        dt = BF16 if arr.dtype == bfloat16 else F32
        t = self.tile(arr.shape[0], arr.shape[1], dt)
        self.consts[name] = (t, arr)
        assert not self.sealed, f"const {name} touched after prologue; preload it"
        d = self.nc.dram_tensor(name, list(arr.shape), dt, kind="ExternalInput").ap()
        self.inputs[name] = arr
        self.dma("sync", t.ap(), d)
        return t

    def preload_consts(self, names):
        for n in names:
            self.const(n)

    def load_consts(self):
        pass

    def const_dram(self, name):
        key = "dram_" + name
        if key in self.dram:
            return self.dram[key].ap()
        arr = _hostc()[name]
        dt = BF16 if arr.dtype == bfloat16 else F32
        h = self.nc.dram_tensor(name, list(arr.shape), dt, kind="ExternalInput")
        self.dram[key] = h
        self.inputs[name] = arr
        return h.ap()

    def pconst(self, name):
        t = getattr(self, "phase_consts", {}).get(name)
        return t if t is not None else self.const(name)

    def load_pconsts(self, names, eng="sync"):
        if not hasattr(self, "phase_consts"):
            self.phase_consts = {}
        sem = self.take_sems(1, "sync")[0]
        for n in names:
            arr = _hostc()[n]
            dt = BF16 if arr.dtype == bfloat16 else F32
            t = self.tile(arr.shape[0], arr.shape[1], dt)
            self.dma("sync", t.ap(), self.const_dram(n), inc=sem)
            self.phase_consts[n] = t
        for e in self.ENGINES:
            self.wait(e, sem, sem.n)

    def clear_pconsts(self, names):
        for n in names:
            self.phase_consts.pop(n, None)

    def precast_weights(self, d, names, shapes):
        psem = self.sem("PCAST")
        for n in names:
            ap16 = self.dtmp(n + "_bf16", shapes[n], BF16)
            self.dma("gpsimd", ap16, d[n], inc=psem)
            self.emit("gpsimd", lambda E, s=psem, v=psem.n: E.wait_ge(s.h, v))
            self.precast[n] = (psem, psem.n)
            d[n] = ap16

    def wait_precast(self, eng, name):
        g = self.precast.get(name)
        if g:
            self.wait(eng, g[0], g[1])

    def din(self, name, shape, dt=F32):
        h = self.nc.dram_tensor(name, list(shape), dt, kind="ExternalInput")
        self.dram[name] = h
        return h.ap()

    def dout(self, name, shape, dt=F32):
        h = self.nc.dram_tensor(name, list(shape), dt, kind="ExternalOutput")
        self.dram[name] = h
        return h.ap()

    def dtmp(self, name, shape, dt=F32):
        h = self.nc.dram_tensor(name, list(shape), dt)
        self.dram[name] = h
        return h.ap()

    def finalize(self):
        nc = self.nc
        block = self._stack.enter_context(nc.Block())
        for eng in self.ENGINES:
            ops = self.streams[eng]
            if ops:
                getattr(block, eng)(lambda E, ops=ops: [f(E) for f in ops])
        self._stack.close()
        return nc


def host_constants():
    c = {}
    c["c_idn"] = np.eye(128, dtype=bfloat16)
    tri = (np.arange(128)[:, None] > np.arange(128)[None, :])
    c["c_tri"] = tri.astype(bfloat16)
    c["c_negI"] = (NEG * np.eye(128)).astype(bfloat16)
    c["c_ones_col"] = np.ones((128, 1), dtype=bfloat16)
    c["c_ones_row"] = np.ones((1, 128), dtype=bfloat16)
    c["c_ones_rowT"] = np.ones((1, T), dtype=bfloat16)
    DH, WIN = 64, 256
    half = np.arange(0, DH, 2, dtype=np.float64)
    inv_freq = 1.0 / 10000.0 ** (half / DH)
    t = np.arange(T, dtype=np.float64)
    freqs = np.concatenate([t[:, None] * inv_freq[None, :]] * 2, -1)
    base_scale = (half + 0.4 * DH) / (1.4 * DH)
    power = (t - T // 2) / (WIN // 2)
    scale = np.concatenate([base_scale[None, :] ** power[:, None]] * 2, -1)
    cos, sin = np.cos(freqs), np.sin(freqs)
    def s2(S):
        S2 = np.empty_like(S)
        S2[0:32] = S[32:64]
        S2[32:64] = -S[0:32]
        return S2
    def dup(A):
        return np.concatenate([A, A], axis=0)
    c["c_rotCp"] = dup((cos * scale).T).astype(bfloat16)
    c["c_rotS2p"] = dup(s2((sin * scale).T)).astype(bfloat16)
    c["c_rotCm"] = dup((cos / scale).T).astype(bfloat16)
    c["c_rotS2m"] = dup(s2((sin / scale).T)).astype(bfloat16)
    i = np.arange(128)[:, None]
    j = np.arange(512)[None, :]
    c["c_delta"] = ((j - i).astype(np.float32) * np.ones((128, 512))).astype(np.float32)
    dc, mc = [], []
    for a in range(4):
        m = (j - 128 * a) > i
        dte = (j - i).astype(np.float32) * np.ones((128, 512))
        dte[np.broadcast_to(m, dte.shape)] = -3e11
        dc.append(dte.astype(np.float32))
        mc.append(np.where(m, NEG, 0.0).astype(np.float32) * np.ones((128, 512), np.float32))
    c["c_deltac"] = np.concatenate(dc, axis=1)
    c["c_causal"] = np.concatenate(mc, axis=1)
    lm = []
    for first in range(2):
        for hh in range(2):
            ia = 128 * hh + i
            if first:
                att = (j >= 256) & ((j - 256) <= ia)
            else:
                att = (j - 256 <= ia) & (ia <= j)
            lm.append(np.where(att, 0.0, NEG).astype(np.float32))
    c["c_lmask"] = np.concatenate(lm, axis=1)
    c["c_iota"] = np.arange(T, dtype=np.float32)[None, :]
    c["c_eps"] = np.full((1, 1), EPS, dtype=np.float32)
    c["c_tiny"] = np.full((128, 1), 1e-28, dtype=np.float32)
    c["c_one"] = np.full((1, 1), 1.0, dtype=np.float32)
    c["c_tri2"] = (np.arange(128)[:, None] < np.arange(128)[None, :]).astype(bfloat16)
    blk = (np.arange(128)[:, None] // 64 == np.arange(128)[None, :] // 64)
    c["c_blk2"] = blk.astype(bfloat16)
    c["c_ones512"] = np.ones((1, 512), dtype=bfloat16)
    c["c_ones_rowD"] = np.ones((1, D), dtype=bfloat16)
    return c


HOSTC = None
def _hostc():
    global HOSTC
    if HOSTC is None:
        HOSTC = host_constants()
    return HOSTC


def load_row_bf16(kb, dram_vec_ap, n, sem=None):
    t = kb.tile(1, n, BF16)
    kb.dma("gpsimd", t.ap(), dram_vec_ap.rearrange("(o n) -> o n", o=1) if len(dram_vec_ap.shape) == 1 else dram_vec_ap, inc=sem)
    return t

def load_row_f32(kb, dram_vec_ap, n, sem=None):
    t = kb.tile(1, n, F32)
    kb.dma("sync", t.ap(), dram_vec_ap.rearrange("(o n) -> o n", o=1) if len(dram_vec_ap.shape) == 1 else dram_vec_ap, inc=sem)
    return t

def load_cols_f32(kb, dram_vec_ap, n, sem=None):
    ncols = -(-n // 128)
    t = kb.tile(128, ncols, F32)
    for c in range(ncols):
        r = min(128, n - c * 128)
        kb.dma("sync", t.ap()[0:r, c:c + 1],
               dram_vec_ap[c * 128:c * 128 + r].rearrange("(p o) -> p o", o=1), inc=sem)
    return t


def emit_ln(kb, src, srcb, dst, grow, bcol, src_is_bf16=False, tag=""):
    if src is None:
        src = srcb
    _m = kb.mark()
    mu = kb.tile(1, T, F32); mu2 = kb.tile(1, T, F32)
    sr = kb.tile(1, T, F32); rr = kb.tile(1, T, F32)
    vr = mu2
    mb = mu2
    rrb = kb.tile(1, T, BF16); mbb = kb.tile(1, T, BF16)
    sq_pp = [kb.tile(128, TB, BF16) for _ in range(2)]
    xb_pp = [kb.tile(128, TB, BF16) for _ in range(2)] if srcb is None else None
    rg_sb = [kb.tile(128, TB, F32) for _ in range(2)]
    tmp_pp = [kb.tile(128, TB, F32) for _ in range(2)]
    onescol = kb.const("c_ones_col")

    kb.fence_pe()
    pe0 = kb.PE.n
    for tb in range(NTB):
        bk = 4 + tb
        sum_ps = kb.bank(bk, rows=1, rowoff=0)
        sq_ps = kb.bank(bk, rows=1, rowoff=32)
        for c in range(NCH):
            pp = c % 2
            k = tb * NCH + c
            if k >= 2:
                kb.wait("scalar", kb.PE, pe0 + k - 1)
            kb.op("scalar", lambda E, s=src(c), d=sq_pp[pp], tb=tb:
                  E.activation(d.ap(), s[:, tb * TB:(tb + 1) * TB], AF.Square),
                  inc=kb.AC)
            if srcb is None:
                if k >= 2:
                    kb.wait("gpsimd", kb.PE, pe0 + k - 1)
                kb.op("gpsimd", lambda E, s=src(c), d=xb_pp[pp], tb=tb:
                      E.tensor_copy(d.ap(), s[:, tb * TB:(tb + 1) * TB]), inc=kb.GP)
                xb = xb_pp[pp].ap()
            else:
                xb = srcb(c)[:, tb * TB:(tb + 1) * TB]
            kb.wait("tensor", kb.AC)
            if srcb is None:
                kb.wait("tensor", kb.GP)
            kb.op("tensor", lambda E, xb=xb, sp=sum_ps, st=(c == 0), en=(c == NCH - 1):
                  E.matmul(sp, onescol.ap(), xb, start=st, stop=en))
            kb.op("tensor", lambda E, d=sq_pp[pp], qp=sq_ps, st=(c == 0), en=(c == NCH - 1):
                  E.matmul(qp, onescol.ap(), d.ap(), start=st, stop=en), inc=kb.PE)

    sum_all = kb.ps_view(rows=1, rowoff=0, col0=512 * 4, cols=2048)
    sq_all = kb.ps_view(rows=1, rowoff=32, col0=512 * 4, cols=2048)
    kb.wait("vector", kb.PE)
    kb.op("vector", lambda E: E.tensor_scalar_mul(mu.ap(), sum_all, 1.0 / D))
    kb.drain("vector")
    kb.op("vector", lambda E: E.tensor_tensor(mu2.ap(), mu.ap(), mu.ap(), ALU.mult))
    kb.drain("vector")
    kb.op("vector", lambda E: E.scalar_tensor_tensor(vr.ap(), sq_all, 1.0 / D, mu2.ap(),
                                                     ALU.mult, ALU.subtract), inc=kb.DV)
    kb.wait("scalar", kb.DV)
    epst = kb.const("c_eps")
    kb.op("scalar", lambda E: E.activation(sr.ap(), vr.ap(), AF.Sqrt, bias=epst.ap()),
          inc=kb.AC)
    kb.wait("vector", kb.AC)
    kb.op("vector", lambda E: E.reciprocal(rr.ap(), sr.ap()))
    kb.drain("vector")
    kb.op("vector", lambda E: E.scalar_tensor_tensor(mb.ap(), mu.ap(), -1.0, rr.ap(),
                                                     ALU.mult, ALU.mult))
    kb.drain("vector")
    kb.op("vector", lambda E: E.tensor_copy(rrb.ap(), rr.ap()))
    kb.drain("vector")
    kb.op("vector", lambda E: E.tensor_copy(mbb.ap(), mb.ap()), inc=kb.DV)
    dv_rows = kb.DV.n

    ac0, gp0, dv0 = kb.AC.n, kb.GP.n, kb.DV.n
    pe1 = kb.PE.n
    for c in range(NCH):
        for tb in range(NTB):
            k = c * NTB + tb
            pp = k % 2
            rg_ps = kb.bank(pp)
            mb_ps = kb.bank(2 + pp)
            kb.wait("tensor", kb.DV, dv_rows)
            if k >= 2:
                kb.wait("tensor", kb.AC, ac0 + (k - 1))
                kb.wait("tensor", kb.DV, dv0 + (k - 1))
            kb.op("tensor", lambda E, c=c, tb=tb, d=rg_ps:
                  E.matmul(d, grow.ap()[0:1, c * 128:(c + 1) * 128],
                           rrb.ap()[0:1, tb * TB:(tb + 1) * TB], start=True, stop=True))
            kb.op("tensor", lambda E, c=c, tb=tb, d=mb_ps:
                  E.matmul(d, grow.ap()[0:1, c * 128:(c + 1) * 128],
                           mbb.ap()[0:1, tb * TB:(tb + 1) * TB], start=True, stop=True),
                  inc=kb.PE)
            kb.wait("scalar", kb.PE)
            if k >= 2:
                kb.wait("scalar", kb.GP, gp0 + (k - 1))
            kb.op("scalar", lambda E, pp=pp, s=rg_ps:
                  E.activation(rg_sb[pp].ap(), s, AF.Copy), inc=kb.AC)
            kb.wait("gpsimd", kb.AC)
            if k >= 2:
                kb.wait("gpsimd", kb.DV, dv0 + (k - 1))
            kb.op("gpsimd", lambda E, c=c, tb=tb, pp=pp:
                  E.tensor_tensor(tmp_pp[pp].ap(), src(c)[:, tb * TB:(tb + 1) * TB],
                                  rg_sb[pp].ap(), ALU.mult), inc=kb.GP)
            kb.wait("vector", kb.GP)
            b_ap = bcol.ap()[:, c:c + 1] if bcol is not None else 0.0
            kb.op("vector", lambda E, c=c, tb=tb, pp=pp, mp=mb_ps, b=b_ap:
                  E.scalar_tensor_tensor(dst(c)[:, tb * TB:(tb + 1) * TB], mp, b,
                                         tmp_pp[pp].ap(), ALU.add, ALU.add), inc=kb.DV)
    kb.reset(_m)


def emit_proj_fm(kb, wdram, din, dout, srcb, evac, evac_sem, aug=None,
                 mw=512, banks=(0, 1, 2, 3, 4, 5, 6, 7), ntb=NTB, pre=None, tag=""):
    if pre:
        kb.wait_precast("sync", pre)
    nkc = -(-din // 128)
    nmb = -(-dout // mw)
    _m = kb.mark()
    wbuf = [kb.tile(128, nkc * mw, BF16) for _ in range(2)]
    wsem = kb.take_sems(2, "syncw")
    evac_counts = []
    nbank = len(banks)
    use = 0
    pe_last_for_buf = [0, 0]
    for mb_i in range(nmb):
        m0 = mb_i * mw
        mwc = min(mw, dout - m0)
        bufp = mb_i % 2
        if mb_i >= 2:
            kb.wait("sync", kb.PE, pe_last_for_buf[bufp])
        else:
            kb.wait("sync", kb.PE)
        nkfull = din // 128
        if nkfull:
            kb.dma("sync",
                   wbuf[bufp].ap()[:, 0:nkfull * mwc]
                   .rearrange("p (kc m) -> p kc m", kc=nkfull),
                   wdram[0:nkfull * 128, m0:m0 + mwc]
                   .rearrange("(kc kp) m -> kp kc m", kp=128), inc=wsem[bufp])
        if nkfull * 128 != din:
            rem = din - nkfull * 128
            kb.dma("sync", wbuf[bufp].ap()[0:rem, nkfull * mwc:(nkfull + 1) * mwc],
                   wdram[nkfull * 128:din, m0:m0 + mwc], inc=wsem[bufp])
        wl_ready = wsem[bufp].n
        nmc = -(-mwc // 128)
        for mc in range(nmc):
            mm0 = mc * 128
            mrows = min(128, mwc - mm0)
            mi = (m0 + mm0) // 128
            for tb in range(ntb):
                bk = banks[use % nbank]
                ps = kb.bank(bk, rows=mrows)
                kb.use_bank(bk, "tensor")
                if use >= nbank:
                    kb.wait("tensor", evac_sem, evac_counts[use - nbank])
                if tb == 0 and mc == 0:
                    kb.wait("tensor", wsem[bufp], wl_ready)
                last_k = nkc - 1 if aug is None else None
                for kc in range(nkc):
                    krows = min(128, din - kc * 128)
                    if kc * 128 + 128 <= din:
                        w_ap = wbuf[bufp].ap()[:, kc * mwc + mm0:kc * mwc + mm0 + mrows]
                    else:
                        w_ap = wbuf[bufp].ap()[0:krows, (din // 128) * mwc + mm0:
                                               (din // 128) * mwc + mm0 + mrows]
                    st = (kc == 0)
                    en = (kc == nkc - 1) and aug is None
                    kb.op("tensor", lambda E, w=w_ap, s=srcb(kc), ps=ps, tb=tb, st=st, en=en, kr=krows:
                          E.matmul(ps, w, s[0:kr, tb * TB:(tb + 1) * TB], start=st, stop=en),
                          inc=(kb.PE if en else None))
                if aug is not None:
                    kb.op("tensor", lambda E, ps=ps, tb=tb, mi=mi, a=aug:
                          E.matmul(ps, a[0](mi), a[1](mi, tb), start=False, stop=True),
                          inc=kb.PE)
                pe_last_for_buf[bufp] = kb.PE.n
                evac(mi, tb, ps, mrows)
                kb.guard_bank(bk, evac_sem)
                evac_counts.append(evac_sem.n)
                use += 1
    kb.reset(_m)


"""Macaron megakernel emitters: streams, L0_ffn (local-attn transformer)."""
import numpy as np
from ml_dtypes import bfloat16

GT = T + 2


class St:
    def __init__(self, kb):
        self.kb = kb
        self.z = [kb.tile(128, GT, F32) for _ in range(NCH)]
        self.zb = [kb.tile(128, GT, BF16) for _ in range(NCH)]
        for c in range(NCH):
            kb.op("vector", lambda E, t=self.z[c]: E.memset(t.ap()[:, 0:1], 0.0))
            kb.op("vector", lambda E, t=self.zb[c]: E.memset(t.ap()[:, 0:1], 0.0),
                  inc=(kb.DV if c == NCH - 1 else None))

    def zs(self, c, shifted=False):
        off = 0 if (shifted and c >= 4) else 1
        return self.z[c].ap()[:, off:off + T]

    def zbs(self, c, shifted=False):
        off = 0 if (shifted and c >= 4) else 1
        return self.zb[c].ap()[:, off:off + T]


def refresh_zb(kb, st, chunks=None):
    for c in (chunks if chunks is not None else range(NCH)):
        if c % 2 == 0:
            kb.wait("vector", kb.DV)
            kb.op("vector", lambda E, c=c: E.tensor_copy(st.zbs(c), st.zs(c)),
                  inc=kb.DV)
        else:
            kb.wait("scalar", kb.DV)
            kb.wait("scalar", kb.AC)
            kb.op("scalar", lambda E, c=c:
                  E.activation(st.zbs(c), st.zs(c), AF.Copy), inc=kb.AC)


class FfnParams:
    def __init__(self, kb, d, li):
        self.wqkv = d["L0_ffn_Wqkv"][li]
        self.wo = d["L0_ffn_Wo"][li]
        self.w1 = d["L0_ffn_W1"][li]
        self.w2 = d["L0_ffn_W2"][li]
        ps = kb.take_sems(1, "sync")[0]
        pg = kb.take_sems(1, "gp")[0]
        self.qs8 = kb.tile(128, 1, F32)
        self.ks = kb.tile(128, 1, F32)
        for half in range(2):
            kb.dma("sync", self.qs8.ap()[64 * half:64 * half + 64, :],
                   d["L0_ffn_q_scale"][li].rearrange("(p o) -> p o", o=1), inc=ps)
            kb.dma("sync", self.ks.ap()[64 * half:64 * half + 64, :],
                   d["L0_ffn_k_scale"][li].rearrange("(p o) -> p o", o=1), inc=ps)
        self.lng = load_row_bf16(kb, d["L0_ffn_ln_g"][li], D, sem=pg)
        self.lnb = load_cols_f32(kb, d["L0_ffn_ln_b"][li], D, sem=ps)
        self.b1 = load_row_bf16(kb, d["L0_ffn_b1"][li], 5460, sem=pg)
        self.b2 = load_cols_f32(kb, d["L0_ffn_b2"][li], D, sem=ps)
        kb.wait("vector", ps, ps.n)
        kb.op("vector", lambda E: E.tensor_scalar_mul(self.qs8.ap(), self.qs8.ap(), 8.0))
        kb.drain("vector")
        kb.op("vector", lambda E: E.engine_nop(), inc=kb.DV)
        for e in ("tensor", "gpsimd", "scalar", "vector"):
            kb.wait(e, ps, ps.n)
            kb.wait(e, pg, pg.n)
            kb.wait(e, kb.DV)


class FfnScratch:
    def __init__(self, kb):
        self.out_all = [kb.tile(128, T, BF16) for _ in range(NCH)]
        self.wq_b = [kb.tile(128, 8 * 128, BF16) for _ in range(2)]
        self.wk_b = [kb.tile(128, 8 * 128, BF16) for _ in range(2)]
        self.wv_b = [kb.tile(128, 8 * 128, BF16) for _ in range(2)]
        self.qrot = kb.tile(128, T, BF16)
        self.krot = kb.tile(128, T, BF16)
        self.sq_pp = [kb.tile(128, TB, BF16) for _ in range(2)]
        self.rn_pp = [kb.tile(128, TB, BF16) for _ in range(2)]
        self.l2_pp = [kb.tile(128, TB, F32)] * 2
        self.nb_pp = [kb.tile(128, 1, F32) for _ in range(2)]
        self.sm_pp = [kb.tile(128, 1, F32) for _ in range(4)]
        self.rs_pp = [kb.tile(128, 1, F32) for _ in range(2)]
        m = kb.mark()
        self.u = kb.tile(128, T, BF16)
        self.s = kb.tile(128, T, BF16)
        hw1 = kb.arena_off
        kb.reset(m)
        self.v_tm = [kb.tile(128, 128, BF16) for _ in range(16)]
        self.P_pp = [kb.tile(128, 512, BF16) for _ in range(4)]
        self.P2 = [kb.tile(128, 512, BF16) for _ in range(2)]
        self.PT = [kb.tile(128, 256, BF16) for _ in range(4)]
        hw2 = kb.arena_off
        kb.arena_off = max(hw1, hw2)
        kb.arena_hwm = max(kb.arena_hwm, kb.arena_off)
        self.wsem = kb.take_sems(2, "syncs")
        self.rsem = kb.take_sems(2)
        self.gd = {}

    def gwait(self, kb, eng, key):
        g = self.gd.get(key)
        if g:
            kb.wait(eng, g[0], g[1])

    def gset(self, kb, key, sem):
        self.gd[key] = (sem, sem.n)


class GluScratch:
    def __init__(self, kb):
        self.r_pp = [kb.tile(128, 1024, BF16) for _ in range(2)]
        self.e1_pp = [kb.tile(128, TB, BF16) for _ in range(2)]
        self.q5_pp = [kb.tile(128, TB, BF16) for _ in range(2)]
        self.wg_pp = [kb.tile(128, 8 * 128, BF16) for _ in range(2)]
        self.wa_pp = [kb.tile(128, 8 * 128, BF16) for _ in range(2)]
        self.h = [kb.tile(128, 1024, BF16) for _ in range(22)]
        self.gd = {}

    def gwait(self, kb, eng, key):
        g = self.gd.get(key)
        if g:
            kb.wait(eng, g[0], g[1])

    def gset(self, kb, key, sem):
        self.gd[key] = (sem, sem.n)


def stream_pair_weights(kb, sc, p, pair, pp):
    s = sc.wsem[pp]
    kb.wait_precast("sync", "L0_ffn_Wqkv")
    for name, buf, base in (("q", sc.wq_b[pp], 0), ("k", sc.wk_b[pp], D),
                            ("v", sc.wv_b[pp], 2 * D)):
        sc.gwait(kb, "sync", ("wb", name, pp))
        kb.dma("sync",
               buf.ap().rearrange("p (kc m) -> p kc m", kc=NCH),
               p.wqkv[:, base + pair * 128: base + (pair + 1) * 128]
               .rearrange("(kc kp) m -> kp kc m", kp=128), inc=s)
    return (s, s.n)


def qk_prep(kb, st, sc, p, pair, which, pp):
    wb = (sc.wq_b if which == "q" else sc.wk_b)[pp]
    dst = sc.qrot if which == "q" else sc.krot
    Ctab = kb.pconst("c_rotCp" if which == "q" else "c_rotCm")
    Stab = kb.pconst("c_rotS2p" if which == "q" else "c_rotS2m")
    blk2 = kb.const("c_blk2")
    tiny = kb.const("c_tiny")
    scl = p.qs8 if which == "q" else p.ks
    sc.gwait(kb, "vector", ("u",))
    kb.wait("vector", kb.DV)

    def mm_stage(tb):
        ps = kb.bank(tb)
        kb.use_bank(tb, "tensor")
        for kc in range(NCH):
            kb.op("tensor", lambda E, kc=kc, tb=tb, ps=ps:
                  E.matmul(ps, wb.ap()[:, kc * 128:(kc + 1) * 128],
                           st.zbs(kc, shifted=True)[:, tb * TB:(tb + 1) * TB],
                           start=(kc == 0), stop=(kc == NCH - 1)),
                  inc=(kb.PE if kc == NCH - 1 else None))

    def norm_stage(tb):
        ps = kb.bank(tb)
        nb_ = 4 + tb
        sc.gwait(kb, "scalar", ("sq", tb % 2))
        kb.wait("scalar", kb.PE)
        kb.op("scalar", lambda E, ps=ps, d=sc.sq_pp[tb % 2]:
              E.activation(d.ap(), ps, AF.Square), inc=kb.AC)
        nps = kb.bank(nb_)
        kb.use_bank(nb_, "tensor")
        kb.wait("tensor", kb.AC)
        kb.op("tensor", lambda E, nps=nps, d=sc.sq_pp[tb % 2]:
              E.matmul(nps, blk2.ap(), d.ap(), start=True, stop=True), inc=kb.PE)
        sc.gset(kb, ("sq", tb % 2), kb.PE)
        sc.gwait(kb, "scalar", ("l2", tb % 2))
        kb.wait("scalar", kb.PE)
        kb.op("scalar", lambda E, nps=nps, d=sc.l2_pp[tb % 2]:
              E.activation(d.ap(), nps, AF.Ln, bias=tiny.ap()), inc=kb.AC)
        kb.guard_bank(nb_, kb.AC)
        sc.gwait(kb, "scalar", ("rn", tb % 2))
        kb.drain("scalar")
        kb.op("scalar", lambda E, d=sc.rn_pp[tb % 2], s=sc.l2_pp[tb % 2]:
              E.activation(d.ap(), s.ap(), AF.Exp, scale=-0.5), inc=kb.AC)
        sc.gset(kb, ("l2", tb % 2), kb.AC)
        kb.wait("vector", kb.AC)
        kb.op("vector", lambda E, ps=ps, tb=tb, s=scl, r=sc.rn_pp[tb % 2]:
              E.scalar_tensor_tensor(sc.u.ap()[:, tb * TB:(tb + 1) * TB], ps, s.ap(),
                                     r.ap(), ALU.mult, ALU.mult), inc=kb.DV)
        kb.guard_bank(tb, kb.DV)
        sc.gset(kb, ("rn", tb % 2), kb.DV)

    mm_stage(0)
    for tb in range(NTB):
        if tb + 1 < NTB:
            mm_stage(tb + 1)
        norm_stage(tb)
    sc.gwait(kb, "vector", ("rot", which))
    kb.wait("vector", kb.DV)
    kb.op("vector", lambda E: E.tensor_tensor(dst.ap()[:, 0:T], sc.u.ap(),
                                              Ctab.ap()[:, 0:T], ALU.mult), inc=kb.DV)
    sc.gset(kb, ("u",), kb.DV)
    sc.gwait(kb, "vector", ("s",))
    kb.op("vector", lambda E: E.tensor_tensor(sc.s.ap(), sc.u.ap(),
                                              Stab.ap()[:, 0:T], ALU.mult), inc=kb.DV)
    kb.wait("gpsimd", kb.DV)
    rs = sc.rsem[0 if which == "q" else 1]
    for blk in range(4):
        src_b = (blk // 2) * 2 + (1 - blk % 2)
        kb.dma("gpsimd", dst.ap()[blk * 32:(blk + 1) * 32, 0:T],
               sc.s.ap()[src_b * 32:(src_b + 1) * 32, :], inc=rs, accum=True)
    sc.gset(kb, ("s",), rs)
    sc.gd[("rotrdy", which)] = (rs, rs.n)


def v_prep(kb, st, sc, pair, pp):
    wb = sc.wv_b[pp]
    sc.gwait(kb, "scalar", ("rotrdy", "q"))
    sc.gwait(kb, "scalar", ("rotrdy", "k"))
    for tt in range(16):
        b = tt % 2
        ps = kb.bank(b, rows=128, cols=128)
        kb.use_bank(b, "tensor")
        for kc in range(NCH):
            kb.op("tensor", lambda E, kc=kc, tt=tt, ps=ps:
                  E.matmul(ps, st.zbs(kc, shifted=True)[:, tt * 128:(tt + 1) * 128],
                           wb.ap()[:, kc * 128:(kc + 1) * 128],
                           start=(kc == 0), stop=(kc == NCH - 1)),
                  inc=(kb.PE if kc == NCH - 1 else None))
        sc.gwait(kb, "scalar", ("vtm",)) if tt == 0 else None
        kb.wait("scalar", kb.PE)
        kb.op("scalar", lambda E, ps=ps, d=sc.v_tm[tt]:
              E.activation(d.ap(), ps, AF.Copy), inc=kb.AC)
        kb.guard_bank(b, kb.AC)


DBG = "full"

def local_attn_pair(kb, st, sc, pair):
    idn = kb.const("c_idn")
    tri = kb.const("c_tri")
    tri2 = kb.const("c_tri2")
    negI = kb.const("c_negI")
    out = sc.out_all[pair]

    def stage_A(it):
        w, hl, sidx = it["w"], it["hl"], it["sidx"]
        s2 = sidx % 2
        for hh in range(2):
            q0 = w * 256 + hh * 128
            mychunks = [(kc, kind) for kc, kind in
                        zip(range(2 * w + hh - 2, 2 * w + hh + 1),
                            ("tri2", "full", "tri")) if kc >= 0]
            it["chunks"][hh] = mychunks
            ncol = 128 * len(mychunks)
            it["ncol"][hh] = ncol
            sb = hh
            kb.use_bank(sb, "tensor")
            if w == 0 and hl == 0 and hh == 0:
                sc.gwait(kb, "tensor", ("rotrdy", "q"))
                sc.gwait(kb, "tensor", ("rotrdy", "k"))
                kb.wait("tensor", kb.AC)
            mms = []
            for idx, (kc, kind) in enumerate(mychunks):
                mms.append(("s", idx, kc))
            for idx, (kc, kind) in enumerate(mychunks):
                if kind != "full":
                    mms.append(("m", idx, kind))
            for n_, m in enumerate(mms):
                st_, en_ = (n_ == 0), (n_ == len(mms) - 1)
                if m[0] == "s":
                    _, idx, kc = m
                    kb.op("tensor", lambda E, idx=idx, kc=kc, hl=hl, q0=q0, sb=sb,
                          st_=st_, en_=en_:
                          E.matmul(kb.bank(sb, cols=128, coloff=idx * 128),
                                   sc.qrot.ap()[hl * 64:(hl + 1) * 64, q0:q0 + 128],
                                   sc.krot.ap()[hl * 64:(hl + 1) * 64,
                                                kc * 128:(kc + 1) * 128],
                                   start=st_, stop=en_),
                          inc=(kb.PE if en_ else None))
                else:
                    _, idx, kind = m
                    tm = tri if kind == "tri" else tri2
                    kb.op("tensor", lambda E, idx=idx, tm=tm, sb=sb, st_=st_, en_=en_:
                          E.matmul(kb.bank(sb, cols=128, coloff=idx * 128),
                                   tm.ap(), negI.ap(), start=st_, stop=en_),
                          inc=(kb.PE if en_ else None))
            S = kb.bank(sb, cols=ncol)
            ps_ = 2 * s2 + hh
            sc.gwait(kb, "vector", ("nb", hh))
            kb.wait("vector", kb.PE)
            kb.op("vector", lambda E, S=S, d=sc.nb_pp[hh]:
                  E.reduce_max(d.ap(), S, axis=AX.X, negate=True), inc=kb.DV)
            sc.gwait(kb, "scalar", ("P", ps_))
            sc.gwait(kb, "scalar", ("sm", ps_))
            kb.wait("scalar", kb.DV)
            kb.op("scalar", lambda E, S=S, ps_=ps_, hh=hh, ncol=ncol:
                  E.activation(sc.P_pp[ps_].ap()[:, 0:ncol], S, AF.Exp,
                               bias=sc.nb_pp[hh].ap(),
                               accum_out=sc.sm_pp[ps_].ap()), inc=kb.AC)
            kb.guard_bank(sb, kb.AC)
            sc.gset(kb, ("nb", hh), kb.AC)
            it["acexp"][hh] = kb.AC.n

    def stage_BC(it):
        w, hl, sidx = it["w"], it["hl"], it["sidx"]
        s2 = sidx % 2
        kcs = [kc for kc in range(2 * w - 2, 2 * w + 2) if kc >= 0]
        dvp2 = {}
        for hh in range(2):
            ncol = it["ncol"][hh]
            ps_ = 2 * s2 + hh
            sc.gwait(kb, "vector", ("rs", hh))
            kb.wait("vector", kb.AC, it["acexp"][hh])
            kb.op("vector", lambda E, hh=hh, ps_=ps_:
                  E.reciprocal(sc.rs_pp[hh].ap(), sc.sm_pp[ps_].ap()), inc=kb.DV)
            sc.gset(kb, ("sm", ps_), kb.DV)
            sc.gwait(kb, "vector", ("P2", hh))
            kb.wait("vector", kb.DV)
            kb.op("vector", lambda E, hh=hh, ps_=ps_, ncol=ncol:
                  E.tensor_scalar_mul(sc.P2[hh].ap()[:, 0:ncol],
                                      sc.P_pp[ps_].ap()[:, 0:ncol],
                                      sc.rs_pp[hh].ap()), inc=kb.DV)
            sc.gset(kb, ("rs", hh), kb.DV)
            sc.gset(kb, ("P", ps_), kb.DV)
            dvp2[hh] = kb.DV.n
        for hh in range(2):
            mychunks = it["chunks"][hh]
            tb_ = 2 + hh
            kb.use_bank(tb_, "tensor")
            kb.wait("tensor", kb.DV, dvp2[hh])
            for idx, (kc, kind) in enumerate(mychunks):
                kb.op("tensor", lambda E, idx=idx, tb_=tb_, hh=hh:
                      E.transpose(kb.ps_view(128, 0, 512 * tb_ + idx * 64, 64, BF16),
                                  sc.P2[hh].ap()[:, idx * 128:(idx + 1) * 128],
                                  idn.ap()),
                      inc=(kb.PE if idx == len(mychunks) - 1 else None))
            sc.gset(kb, ("P2", hh), kb.PE)
            eng = "vector" if hh == 0 else "scalar"
            for idx, (kc, kind) in enumerate(mychunks):
                slot = kc - (2 * w - 2)
                sc.gwait(kb, eng, ("PT", slot, hh))
                kb.wait(eng, kb.PE)
                src_ap = kb.ps_view(128, 0, 512 * tb_ + idx * 64, 64, BF16)
                dst_ap = sc.PT[slot].ap()[:, hh * 128:(hh + 1) * 128]
                if eng == "vector":
                    kb.op("vector", lambda E, s=src_ap, d=dst_ap:
                          E.tensor_copy(d, s), inc=kb.DV)
                    kb.guard_bank(tb_, kb.DV)
                else:
                    kb.op("scalar", lambda E, s=src_ap, d=dst_ap:
                          E.activation(d, s, AF.Copy), inc=kb.AC)
                    kb.guard_bank(tb_, kb.AC)
        ob = 4 + sidx % 2
        kb.use_bank(ob, "tensor")
        kb.wait("tensor", kb.DV)
        kb.wait("tensor", kb.AC)
        if w == 0:
            contribs = {0: [0], 1: [0, 1]}
        else:
            contribs = {0: [kc for kc in kcs if kc <= 2 * w],
                        1: [kc for kc in kcs if kc >= 2 * w - 1]}
        nmm = sum(len(v) for v in contribs.values())
        n_ = 0
        for r in (0, 1):
            lst = contribs[r]
            for i, kc in enumerate(lst):
                slot = kc - (2 * w - 2)
                n_ += 1
                kb.op("tensor", lambda E, kc=kc, hl=hl, slot=slot, r=r, ob=ob,
                      st_=(n_ == 1), en_=(n_ == nmm):
                      E.matmul(kb.ps_view(64, 0, 512 * ob + r * 128, 128),
                               sc.v_tm[kc].ap()[:, hl * 64:(hl + 1) * 64],
                               sc.PT[slot].ap()[:, r * 128:(r + 1) * 128],
                               start=st_, stop=en_),
                      inc=(kb.PE if n_ == nmm else None))
        for kc in kcs:
            slot = kc - (2 * w - 2)
            for hh in range(2):
                sc.gset(kb, ("PT", slot, hh), kb.PE)
        kb.wait("scalar", kb.PE)
        kb.op("scalar", lambda E, hl=hl, w=w, ob=ob:
              E.activation(out.ap()[hl * 64:(hl + 1) * 64, w * 256:(w + 1) * 256],
                           kb.ps_view(64, 0, 512 * ob, 256), AF.Copy), inc=kb.AC)
        kb.guard_bank(ob, kb.AC)

    prev = None
    sidx = 0
    for w in range(8):
        for hl in range(2):
            it = {"w": w, "hl": hl, "sidx": sidx, "chunks": {}, "ncol": {},
                  "acexp": {}}
            sidx += 1
            stage_A(it)
            if prev is not None:
                stage_BC(prev)
            prev = it
    stage_BC(prev)
    sc.gset(kb, ("rot", "q"), kb.PE)
    sc.gset(kb, ("rot", "k"), kb.PE)
    sc.gset(kb, ("vtm",), kb.PE)


def wo_residual(kb, st, sc, p):
    def evac(mi, tb, ps, mrows):
        kb.wait("vector", kb.PE)
        kb.op("vector", lambda E, mi=mi, tb=tb, ps=ps:
              E.scalar_tensor_tensor(st.zs(mi)[:, tb * TB:(tb + 1) * TB], ps, 0.0,
                                     st.zs(mi)[:, tb * TB:(tb + 1) * TB],
                                     ALU.add, ALU.add), inc=kb.DV)
    emit_proj_fm(kb, p.wo, D, D, srcb=lambda c: sc.out_all[c].ap()[:, 0:T],
                 evac=evac, evac_sem=kb.DV, mw=256, banks=(0, 1, 2, 3),
                 pre="L0_ffn_Wo")


def glu_w2(kb, st, p):
    sc = GluScratch(kb)
    gsem = kb.take_sems(2, "syncs")
    asem = kb.take_sems(2, "syncs")
    kb.wait_precast("sync", "L0_ffn_W1")
    ones512 = kb.const("c_ones512")
    DI = 2730
    njc = 22
    wg_pp, wa_pp = sc.wg_pp, sc.wa_pp
    for half in range(2):
        hoff = half * 1024

        def stream_j(j):
            m0 = j * 128
            mrows = min(128, DI - m0)
            pp = j % 2
            sc.gwait(kb, "sync", ("wg", pp))
            kb.dma("sync", wg_pp[pp].ap()[:, 0:NCH * mrows]
                   .rearrange("p (kc m) -> p kc m", kc=NCH),
                   p.w1[:, DI + m0:DI + m0 + mrows]
                   .rearrange("(kc kp) m -> kp kc m", kp=128), inc=gsem[pp])
            sc.gwait(kb, "sync", ("wa", pp))
            kb.dma("sync", wa_pp[pp].ap()[:, 0:NCH * mrows]
                   .rearrange("p (kc m) -> p kc m", kc=NCH),
                   p.w1[:, m0:m0 + mrows]
                   .rearrange("(kc kp) m -> kp kc m", kp=128), inc=asem[pp])
            return (gsem[pp].n, asem[pp].n)

        def stage_g(j, wl):
            m0 = j * 128
            mrows = min(128, DI - m0)
            pp = j % 2
            for tb in range(2):
                t0 = hoff + tb * TB
                b = tb % 2
                ps = kb.bank(b, rows=mrows)
                kb.use_bank(b, "tensor")
                if tb == 0:
                    kb.wait("tensor", gsem[pp], wl[0])
                for kc in range(NCH):
                    kb.op("tensor", lambda E, kc=kc, t0=t0, ps=ps, pp=pp, mr=mrows:
                          E.matmul(ps, wg_pp[pp].ap()[:, kc * mr:(kc + 1) * mr],
                                   st.zbs(kc)[:, t0:t0 + TB],
                                   start=(kc == 0), stop=False))
                kb.op("tensor", lambda E, ps=ps, m0=m0, mr=mrows:
                      E.matmul(ps, p.b1.ap()[0:1, DI + m0:DI + m0 + mr],
                               ones512.ap(), start=False, stop=True), inc=kb.PE)
                sc.gwait(kb, "scalar", ("e1", tb % 2))
                kb.wait("scalar", kb.PE)
                kb.op("scalar", lambda E, ps=ps, d=sc.e1_pp[tb % 2], mr=mrows:
                      E.activation(d.ap()[0:mr, :], ps, AF.Erf,
                                   scale=0.7071067811865476), inc=kb.AC)
                sc.gwait(kb, "scalar", ("q5", tb % 2))
                kb.op("scalar", lambda E, ps=ps, d=sc.q5_pp[tb % 2], mr=mrows:
                      E.activation(d.ap()[0:mr, :], ps, AF.Copy, scale=0.5),
                      inc=kb.AC)
                kb.guard_bank(b, kb.AC)
                sc.gwait(kb, "vector", ("r", pp)) if tb == 0 else None
                kb.wait("vector", kb.AC)
                kb.op("vector", lambda E, tb=tb, pp=pp, mr=mrows:
                      E.scalar_tensor_tensor(
                          sc.r_pp[pp].ap()[0:mr, tb * TB:(tb + 1) * TB],
                          sc.e1_pp[tb % 2].ap()[0:mr, :], 1.0,
                          sc.q5_pp[tb % 2].ap()[0:mr, :],
                          ALU.add, ALU.mult), inc=kb.DV)
                sc.gset(kb, ("e1", tb % 2), kb.DV)
                sc.gset(kb, ("q5", tb % 2), kb.DV)
            sc.gset(kb, ("wg", pp), kb.PE)

        def stage_a(j, wl):
            m0 = j * 128
            mrows = min(128, DI - m0)
            pp = j % 2
            for tb in range(2):
                t0 = hoff + tb * TB
                b = 2 + (2 * j + tb) % 4
                ps = kb.bank(b, rows=mrows)
                kb.use_bank(b, "tensor")
                if tb == 0:
                    kb.wait("tensor", asem[pp], wl[1])
                for kc in range(NCH):
                    kb.op("tensor", lambda E, kc=kc, t0=t0, ps=ps, pp=pp, mr=mrows:
                          E.matmul(ps, wa_pp[pp].ap()[:, kc * mr:(kc + 1) * mr],
                                   st.zbs(kc)[:, t0:t0 + TB],
                                   start=(kc == 0), stop=False))
                kb.op("tensor", lambda E, ps=ps, m0=m0, mr=mrows:
                      E.matmul(ps, p.b1.ap()[0:1, m0:m0 + mr],
                               ones512.ap(), start=False, stop=True), inc=kb.PE)
                kb.wait("vector", kb.PE)
                if tb == 0:
                    kb.wait("vector", kb.DV)
                kb.op("vector", lambda E, ps=ps, j=j, tb=tb, pp=pp, mr=mrows:
                      E.tensor_tensor(sc.h[j].ap()[0:mr, tb * TB:(tb + 1) * TB],
                                      ps, sc.r_pp[pp].ap()[0:mr,
                                                           tb * TB:(tb + 1) * TB],
                                      ALU.mult), inc=kb.DV)
                kb.guard_bank(b, kb.DV)
                if tb == 1:
                    sc.gset(kb, ("r", pp), kb.DV)
            sc.gset(kb, ("wa", pp), kb.PE)

        wls = {}
        wls[0] = stream_j(0)
        stage_g(0, wls[0])
        for j in range(njc):
            if j + 1 < njc:
                wls[j + 1] = stream_j(j + 1)
                stage_g(j + 1, wls[j + 1])
            if j >= 0:
                stage_a(j, wls[j])
        kb.fence_pe()
        def evac(mi, tb, ps, mrows, hoff=hoff):
            kb.wait("vector", kb.PE)
            kb.op("vector", lambda E, mi=mi, tb=tb, ps=ps, hoff=hoff:
                  E.scalar_tensor_tensor(st.zs(mi)[:, hoff + tb * TB:hoff + (tb + 1) * TB],
                                         ps, p.b2.ap()[:, mi:mi + 1],
                                         st.zs(mi)[:, hoff + tb * TB:hoff + (tb + 1) * TB],
                                         ALU.add, ALU.add), inc=kb.DV)
        emit_proj_fm(kb, p.w2, DI, D,
                     srcb=lambda c: sc.h[c].ap()[:, 0:1024],
                     evac=evac, evac_sem=kb.DV, mw=128, banks=(6, 7), ntb=2,
                     pre="L0_ffn_W2")
        kb.phase_fence()


def l0ffn_sublayer(kb, st, p):
    m_attn = kb.mark()
    sc = FfnScratch(kb)
    refresh_zb(kb, st)
    kb.phase_fence()
    for pair in range(NCH):
        pp = pair % 2
        wsem, wl = stream_pair_weights(kb, sc, p, pair, pp)
        kb.wait("tensor", wsem, wl)
        qk_prep(kb, st, sc, p, pair, "q", pp)
        qk_prep(kb, st, sc, p, pair, "k", pp)
        v_prep(kb, st, sc, pair, pp)
        for name in ("q", "k", "v"):
            sc.gd[("wb", name, pp)] = (kb.PE, kb.PE.n)
        local_attn_pair(kb, st, sc, pair)
    kb.phase_fence()
    wo_residual(kb, st, sc, p)
    kb.phase_fence()
    kb.reset(m_attn)
    refresh_zb(kb, st)
    emit_ln(kb, src=lambda c: st.zs(c, shifted=True),
            srcb=lambda c: st.zbs(c, shifted=True),
            dst=lambda c: st.zbs(c), grow=p.lng, bcol=p.lnb)
    kb.phase_fence()
    glu_w2(kb, st, p)
    kb.phase_fence()
    kb.reset(m_attn)


ROT_CONSTS = ["c_rotCp", "c_rotS2p", "c_rotCm", "c_rotS2m"]

def l0ffn_invocation(kb, st, d, xspill, nlayers=6):
    minv = kb.mark()
    kb.load_pconsts(ROT_CONSTS)
    spill = kb.take_sems(1, "sync")[0]
    kb.phase_fence()
    for c in range(NCH):
        kb.dma("sync", xspill[c * 128:(c + 1) * 128, :], st.zs(c), inc=spill)
    spill_n = spill.n
    kb.wait("vector", spill, spill_n)
    onesrow = kb.const("c_ones_rowD")
    emit_ln(kb, src=lambda c: st.zs(c), srcb=None,
            dst=lambda c: st.zs(c), grow=onesrow, bcol=None)
    refresh_zb(kb, st)
    kb.phase_fence()
    for li in range(nlayers):
        m = kb.mark()
        p = FfnParams(kb, d, li)
        l0ffn_sublayer(kb, st, p)
        kb.reset(m)
    m = kb.mark()
    xold = [kb.tile(128, T, F32) for _ in range(2)]
    tmp = [kb.tile(128, TB, F32) for _ in range(2)]
    rsems = kb.take_sems(2, "sync")
    rn0 = []
    for c in range(NCH):
        if c >= 2:
            kb.wait("sync", kb.DV, rn0[c - 2])
        kb.dma("sync", xold[c % 2].ap(), xspill[c * 128:(c + 1) * 128, :],
               inc=rsems[c % 2])
        rl = rsems[c % 2].n
        for tb in range(NTB):
            pp = tb % 2
            kb.wait("vector", rsems[c % 2], rl)
            kb.wait("vector", kb.DV)
            kb.op("vector", lambda E, c=c, tb=tb, pp=pp:
                  E.tensor_scalar_mul(tmp[pp].ap(), st.zs(c)[:, tb * TB:(tb + 1) * TB],
                                      0.5), inc=kb.DV)
            kb.wait("vector", kb.DV)
            kb.op("vector", lambda E, c=c, tb=tb, pp=pp:
                  E.tensor_tensor(st.zs(c)[:, tb * TB:(tb + 1) * TB], tmp[pp].ap(),
                                  xold[c % 2].ap()[:, tb * TB:(tb + 1) * TB],
                                  ALU.add), inc=kb.DV)
        rn0.append(kb.DV.n)
    kb.phase_fence()
    kb.reset(m)
    kb.clear_pconsts(ROT_CONSTS)
    kb.reset(minv)


"""Dense attention (L0_attn, L1_attn) and MoE (L1_ffn) emitters."""
import numpy as np
from ml_dtypes import bfloat16

SLOPES = (2.0 ** (-8.0 * np.arange(1, 17) / 16.0)).astype(np.float64)


def dense_consts():
    c = {}
    i = np.arange(128)[:, None]
    U0 = 1920
    u = np.arange(2304)[None, :]
    c["c_dmaster"] = ((u - U0) - i).astype(np.float32) * np.ones((128, 2304), np.float32)
    j = np.arange(128)[None, :]
    dm = (j - i).astype(np.float32) * np.ones((128, 128), np.float32)
    dm[np.broadcast_to(j > i, dm.shape)] = -3e11
    c["c_dmask"] = dm.astype(np.float32)
    return c


class DenseScratch:
    def __init__(self, kb, alibi, ptbufs=2):
        self.w_b = {n: [kb.tile(128, 8 * 128, BF16)] * 2
                    for n in ("q", "k", "v")}
        self.qf = kb.tile(128, T, BF16)
        self.kf = kb.tile(128, T, BF16)
        self.v_tm = [kb.tile(128, 128, BF16) for _ in range(16)]
        self.U = [kb.tile(128, T, F32) for _ in range(2)] if alibi else None
        self.P = [kb.tile(128, T, BF16) for _ in range(2)]
        self.P2 = kb.tile(128, T, BF16)
        self.PT = kb.tile(128, 4 * 128, BF16)
        self.nb = [kb.tile(128, 1, F32) for _ in range(2)]
        self.nb8 = [kb.tile(128, 1, F32) for _ in range(2)]
        self.sm = [kb.tile(128, 1, F32) for _ in range(2)]
        self.rs = [kb.tile(128, 1, F32) for _ in range(2)]
        self.out_all = [kb.tile(128, T, BF16) for _ in range(NCH)]
        self.wsem = kb.take_sems(2, "syncs")
        self.gd = {}

    def gwait(self, kb, eng, key):
        g = self.gd.get(key)
        if g:
            kb.wait(eng, g[0], g[1])

    def gset(self, kb, key, sem):
        self.gd[key] = (sem, sem.n)


def qkv_dense(kb, sc, srcb, pair, pp, which):
    wb = sc.w_b[which][pp]
    if which in ("q", "k"):
        dst = sc.qf if which == "q" else sc.kf
        for tb in range(NTB):
            b = tb % 2
            ps = kb.bank(b)
            kb.use_bank(b, "tensor")
            for kc in range(NCH):
                kb.op("tensor", lambda E, kc=kc, tb=tb, ps=ps:
                      E.matmul(ps, wb.ap()[:, kc * 128:(kc + 1) * 128],
                               srcb(kc)[:, tb * TB:(tb + 1) * TB],
                               start=(kc == 0), stop=(kc == NCH - 1)),
                      inc=(kb.PE if kc == NCH - 1 else None))
            sc.gwait(kb, "scalar", ("qk", which)) if tb == 0 else None
            kb.wait("scalar", kb.PE)
            kb.op("scalar", lambda E, ps=ps, tb=tb:
                  E.activation(dst.ap()[:, tb * TB:(tb + 1) * TB], ps, AF.Copy),
                  inc=kb.AC)
            kb.guard_bank(b, kb.AC)
    else:
        sc.gwait(kb, "scalar", ("vtm",))
        for tt in range(16):
            b = tt % 2
            ps = kb.bank(b, rows=128, cols=128)
            kb.use_bank(b, "tensor")
            for kc in range(NCH):
                kb.op("tensor", lambda E, kc=kc, tt=tt, ps=ps:
                      E.matmul(ps, srcb(kc)[:, tt * 128:(tt + 1) * 128],
                               wb.ap()[:, kc * 128:(kc + 1) * 128],
                               start=(kc == 0), stop=(kc == NCH - 1)),
                      inc=(kb.PE if kc == NCH - 1 else None))
            kb.wait("scalar", kb.PE)
            kb.op("scalar", lambda E, ps=ps, d=sc.v_tm[tt]:
                  E.activation(d.ap(), ps, AF.Copy), inc=kb.AC)
            kb.guard_bank(b, kb.AC)


def dense_attn_pair(kb, sc, pair, alibi):
    idn = kb.const("c_idn")
    tri = kb.const("c_tri")
    negI = kb.const("c_negI")
    if alibi:
        dmaster = kb.pconst("c_dmaster")
        dmask = kb.pconst("c_dmask")
    out = sc.out_all[pair]
    U0 = 1920

    def stage_A(it):
        hl, qt, pq = it["hl"], it["qt"], it["pq"]
        slope = float(SLOPES[2 * pair + hl])
        nfull = qt // 4
        a_d = qt % 4
        ncol_d = (a_d + 1) * 128
        for b_ in range(nfull + 1):
            ps_cols = 512 if b_ < nfull else ncol_d
            kb.use_bank(b_, "tensor")
            nmm = 1 if b_ < nfull else 2
            kb.op("tensor", lambda E, b_=b_, qt=qt, hl=hl, ps_cols=ps_cols,
                  en=(nmm == 1):
                  E.matmul(kb.bank(b_, cols=ps_cols),
                           sc.qf.ap()[hl * 64:(hl + 1) * 64,
                                      qt * 128:(qt + 1) * 128],
                           sc.kf.ap()[hl * 64:(hl + 1) * 64,
                                      b_ * 512:b_ * 512 + ps_cols],
                           start=True, stop=en),
                  inc=(kb.PE if nmm == 1 else None))
            if b_ == nfull:
                kb.op("tensor", lambda E, b_=b_, a_d=a_d:
                      E.matmul(kb.bank(b_, cols=128, coloff=a_d * 128),
                               tri.ap(), negI.ap(), start=False, stop=True),
                      inc=kb.PE)
        span = nfull * 512 + ncol_d
        it["span"] = span
        if alibi:
            sc.gwait(kb, "vector", ("U", pq))
            kb.wait("vector", kb.PE)
            for b_ in range(nfull):
                aoff = qt - 4 * b_
                kb.op("vector", lambda E, b_=b_, aoff=aoff, pq=pq, slope=slope:
                      E.scalar_tensor_tensor(
                          sc.U[pq].ap()[:, b_ * 512:(b_ + 1) * 512],
                          dmaster.ap()[:, U0 - 128 * aoff:U0 - 128 * aoff + 512],
                          slope, kb.bank(b_, cols=512), ALU.mult, ALU.add),
                      inc=kb.DV)
            for sub in range(a_d + 1):
                aoff = a_d - sub
                dsl = (dmask.ap() if aoff == 0 else
                       dmaster.ap()[:, U0 - 128 * aoff:U0 - 128 * aoff + 128])
                kb.op("vector", lambda E, sub=sub, pq=pq, slope=slope, dsl=dsl,
                      nfull=nfull:
                      E.scalar_tensor_tensor(
                          sc.U[pq].ap()[:, nfull * 512 + sub * 128:
                                        nfull * 512 + (sub + 1) * 128],
                          dsl, slope,
                          kb.bank(nfull, cols=128, coloff=sub * 128),
                          ALU.mult, ALU.add), inc=kb.DV)
            for b_ in range(nfull + 1):
                kb.guard_bank(b_, kb.DV)
            src_row = sc.U[pq].ap()[:, 0:span]
            kb.wait("vector", kb.DV)
        else:
            kb.wait("vector", kb.PE)
            src_row = kb.ps_view(128, 0, 0, span)
        sc.gwait(kb, "vector", ("nb", pq))
        kb.op("vector", lambda E, s=src_row, d=sc.nb[pq]:
              E.reduce_max(d.ap(), s, axis=AX.X, negate=True), inc=kb.DV)
        kb.wait("vector", kb.DV)
        sc.gwait(kb, "vector", ("nb8", pq))
        kb.op("vector", lambda E, pq=pq:
              E.tensor_scalar_mul(sc.nb8[pq].ap(), sc.nb[pq].ap(), 0.125),
              inc=kb.DV)
        sc.gset(kb, ("nb", pq), kb.DV)
        sc.gwait(kb, "scalar", ("P", pq))
        sc.gwait(kb, "scalar", ("sm", pq))
        kb.wait("scalar", kb.DV)
        kb.op("scalar", lambda E, s=src_row, pq=pq, span=span:
              E.activation(sc.P[pq].ap()[:, 0:span], s, AF.Exp, scale=0.125,
                           bias=sc.nb8[pq].ap(), accum_out=sc.sm[pq].ap()),
              inc=kb.AC)
        sc.gset(kb, ("nb8", pq), kb.AC)
        if alibi:
            sc.gset(kb, ("U", pq), kb.AC)
        else:
            for b_ in range(qt // 4 + 1):
                kb.guard_bank(b_, kb.AC)
        it["acexp"] = kb.AC.n

    def stage_BC(it):
        hl, qt, pq = it["hl"], it["qt"], it["pq"]
        span = it["span"]
        nkc = span // 128
        sc.gwait(kb, "vector", ("rs", pq))
        kb.wait("vector", kb.AC, it["acexp"])
        kb.op("vector", lambda E, pq=pq:
              E.reciprocal(sc.rs[pq].ap(), sc.sm[pq].ap()), inc=kb.DV)
        sc.gset(kb, ("sm", pq), kb.DV)
        sc.gwait(kb, "scalar", ("P2",))
        kb.wait("scalar", kb.DV)
        kb.op("scalar", lambda E, pq=pq, span=span:
              E.activation(sc.P2.ap()[:, 0:span], sc.P[pq].ap()[:, 0:span],
                           AF.Copy, scale=sc.rs[pq].ap()), inc=kb.AC)
        sc.gset(kb, ("rs", pq), kb.AC)
        sc.gset(kb, ("P", pq), kb.AC)
        acp2 = kb.AC.n
        tb_ = 4 + pq
        ob = 6 + pq
        eng = "vector" if pq == 0 else "scalar"
        esem = kb.DV if pq == 0 else kb.AC
        kb.use_bank(ob, "tensor")
        kb.wait("tensor", kb.AC, acp2)
        n_ = 0
        for grpb in range(0, nkc, 4):
            kb.use_bank(tb_, "tensor")
            gn = min(4, nkc - grpb)
            for ii in range(gn):
                kb.op("tensor", lambda E, grpb=grpb, ii=ii, tb_=tb_:
                      E.transpose(kb.ps_view(128, 0, 512 * tb_ + ii * 64, 64, BF16),
                                  sc.P2.ap()[:, (grpb + ii) * 128:
                                             (grpb + ii + 1) * 128],
                                  idn.ap()),
                      inc=(kb.PE if ii == gn - 1 else None))
            kb.wait(eng, kb.PE)
            for ii in range(gn):
                kb.op(eng, lambda E, ii=ii, tb_=tb_:
                      (E.tensor_copy if pq == 0 else
                       (lambda o, i_: E.activation(o, i_, AF.Copy)))(
                          sc.PT.ap()[:, ii * 128:(ii + 1) * 128],
                          kb.ps_view(128, 0, 512 * tb_ + ii * 64, 64, BF16)),
                      inc=(esem if ii == gn - 1 else None))
            kb.guard_bank(tb_, esem)
            kb.wait("tensor", esem)
            for ii in range(gn):
                n_ += 1
                kb.op("tensor", lambda E, kc=grpb + ii, ii=ii, hl=hl, ob=ob,
                      st_=(n_ == 1), en=(n_ == nkc):
                      E.matmul(kb.ps_view(64, 0, 512 * ob, 128),
                               sc.v_tm[kc].ap()[:, hl * 64:(hl + 1) * 64],
                               sc.PT.ap()[:, ii * 128:(ii + 1) * 128],
                               start=st_, stop=en),
                      inc=(kb.PE if n_ == nkc else None))
        sc.gset(kb, ("P2",), kb.PE)
        kb.wait("scalar", kb.PE)
        kb.op("scalar", lambda E, hl=hl, qt=qt, ob=ob:
              E.activation(out.ap()[hl * 64:(hl + 1) * 64,
                                    qt * 128:(qt + 1) * 128],
                           kb.ps_view(64, 0, 512 * ob, 128), AF.Copy), inc=kb.AC)
        kb.guard_bank(ob, kb.AC)

    prev = None
    for hl in range(2):
        for qt in range(16):
            it = {"hl": hl, "qt": qt, "pq": qt % 2}
            stage_A(it)
            if prev is not None:
                stage_BC(prev)
            prev = it
    stage_BC(prev)
    sc.gset(kb, ("qk", "q"), kb.PE)
    sc.gset(kb, ("qk", "k"), kb.PE)
    sc.gset(kb, ("vtm",), kb.PE)


def l0_attn_slot(kb, st, d):
    onesrow = kb.const("c_ones_rowD")
    klib_emitln(kb, st, onesrow, None)
    m = kb.mark()
    kb.load_pconsts(["c_dmaster", "c_dmask"])
    sc = DenseScratch(kb, alibi=True)
    ps = kb.take_sems(1, "sync")[0]
    bp = load_cols_f32(kb, d["L0_attn_b_proj"], D, sem=ps)
    kb.wait("vector", ps, ps.n)
    kb.phase_fence()
    for pair in range(NCH):
        pp = pair % 2
        s = sc.wsem[pp]
        for which, wd in (("q", d["L0_attn_W_query"]), ("k", d["L0_attn_W_key"]),
                          ("v", d["L0_attn_W_value"])):
            kb.wait_precast("sync", {"q": "L0_attn_W_query", "k": "L0_attn_W_key",
                                     "v": "L0_attn_W_value"}[which])
            sc.gwait(kb, "sync", ("wb", which, 0))
            sc.gwait(kb, "sync", ("wb", which, 1))
            kb.dma("sync",
                   sc.w_b[which][pp].ap().rearrange("p (kc m) -> p kc m", kc=NCH),
                   wd[:, pair * 128:(pair + 1) * 128]
                   .rearrange("(kc kp) m -> kp kc m", kp=128), inc=s)
        kb.wait("tensor", s, s.n)
        qkv_dense(kb, sc, lambda c: st.zbs(c), pair, pp, "q")
        qkv_dense(kb, sc, lambda c: st.zbs(c), pair, pp, "k")
        qkv_dense(kb, sc, lambda c: st.zbs(c), pair, pp, "v")
        for name in ("q", "k", "v"):
            sc.gd[("wb", name, pp)] = (kb.PE, kb.PE.n)
        dense_attn_pair(kb, sc, pair, alibi=True)
    kb.phase_fence()

    def evac(mi, tb, psu, mrows):
        kb.wait("vector", kb.PE)
        kb.wait("vector", kb.DV)
        kb.op("vector", lambda E, mi=mi, tb=tb, psu=psu:
              E.scalar_tensor_tensor(st.zs(mi)[:, tb * TB:(tb + 1) * TB], psu,
                                     bp.ap()[:, mi:mi + 1],
                                     st.zs(mi)[:, tb * TB:(tb + 1) * TB],
                                     ALU.add, ALU.add), inc=kb.DV)
    kb.dbg = {"sc": sc}
    emit_proj_fm(kb, d["L0_attn_W_proj"], D, D,
                 srcb=lambda c: sc.out_all[c].ap()[:, 0:T],
                 evac=evac, evac_sem=kb.DV, mw=256, banks=(0, 1, 2, 3),
                 pre="L0_attn_W_proj")
    kb.phase_fence()
    kb.clear_pconsts(["c_dmaster", "c_dmask"])
    kb.reset(m)


def klib_emitln(kb, st, grow, bcol, dstf32=False):
    emit_ln(kb, src=lambda c: st.zs(c), srcb=None,
            dst=(lambda c: st.zs(c)) if dstf32 else (lambda c: st.zbs(c)),
            grow=grow, bcol=bcol)
    if dstf32:
        refresh_zb(kb, st)
    kb.phase_fence()


def l1_attn_slot(kb, st, d):
    onesrow = kb.const("c_ones_rowD")
    klib_emitln(kb, st, onesrow, None)
    m = kb.mark()
    yb = [kb.tile(128, T, BF16) for _ in range(NCH)]
    ps = kb.take_sems(1, "sync")[0]
    pg = kb.take_sems(1, "gp")[0]
    lw = load_row_bf16(kb, d["L1_attn_ln_w"], D, sem=pg)
    lb = load_cols_f32(kb, d["L1_attn_ln_b"], D, sem=ps)
    for e in ("tensor", "vector", "scalar", "gpsimd"):
        kb.wait(e, ps, ps.n)
        kb.wait(e, pg, pg.n)
    emit_ln(kb, src=None, srcb=lambda c: st.zbs(c),
            dst=lambda c: yb[c].ap()[:, 0:T], grow=lw, bcol=lb, src_is_bf16=True)
    kb.phase_fence()
    sc = DenseScratch(kb, alibi=False, ptbufs=1)
    kb.phase_fence()
    for pair in range(NCH):
        pp = pair % 2
        s = sc.wsem[pp]
        for which, wd in (("q", d["L1_attn_wq"]), ("k", d["L1_attn_wk"]),
                          ("v", d["L1_attn_wv"])):
            kb.wait_precast("sync", {"q": "L1_attn_wq", "k": "L1_attn_wk",
                                     "v": "L1_attn_wv"}[which])
            sc.gwait(kb, "sync", ("wb", which, 0))
            sc.gwait(kb, "sync", ("wb", which, 1))
            for hl in range(2):
                kb.dma("sync",
                       sc.w_b[which][pp].ap()
                       .rearrange("p (kc m) -> p kc m", kc=NCH)[:, :, hl * 64:(hl + 1) * 64],
                       wd[2 * pair + hl].rearrange("(kc kp) m -> kp kc m", kp=128),
                       inc=s)
        kb.wait("tensor", s, s.n)
        qkv_dense(kb, sc, lambda c: yb[c].ap()[:, 0:T], pair, pp, "q")
        qkv_dense(kb, sc, lambda c: yb[c].ap()[:, 0:T], pair, pp, "k")
        qkv_dense(kb, sc, lambda c: yb[c].ap()[:, 0:T], pair, pp, "v")
        for name in ("q", "k", "v"):
            sc.gd[("wb", name, pp)] = (kb.PE, kb.PE.n)
        dense_attn_pair(kb, sc, pair, alibi=False)
    kb.phase_fence()

    def evac(mi, tb, psu, mrows):
        kb.wait("vector", kb.PE)
        kb.wait("vector", kb.DV)
        kb.op("vector", lambda E, mi=mi, tb=tb, psu=psu:
              E.scalar_tensor_tensor(st.zs(mi)[:, tb * TB:(tb + 1) * TB], psu, 0.0,
                                     st.zs(mi)[:, tb * TB:(tb + 1) * TB],
                                     ALU.add, ALU.add), inc=kb.DV)
        kb.wait("vector", kb.DV)
        kb.op("vector", lambda E, mi=mi, tb=tb:
              E.scalar_tensor_tensor(st.zs(mi)[:, tb * TB:(tb + 1) * TB],
                                     st.zbs(mi)[:, tb * TB:(tb + 1) * TB], 1.0,
                                     st.zs(mi)[:, tb * TB:(tb + 1) * TB],
                                     ALU.mult, ALU.add), inc=kb.DV)
    kb.dbg = {"yb": yb, "sc": sc}
    emit_proj_fm(kb, d["L1_attn_wo"], D, D,
                 srcb=lambda c: sc.out_all[c].ap()[:, 0:T],
                 evac=evac, evac_sem=kb.DV, mw=256, banks=(0, 1, 2, 3),
                 pre="L1_attn_wo")
    kb.phase_fence()
    kb.reset(m)


def moe_slot(kb, st, d):
    onesrow = kb.const("c_ones_rowD")
    ones128 = kb.const("c_ones_row")
    ones512 = kb.const("c_ones512")
    idn = kb.const("c_idn")
    klib_emitln(kb, st, onesrow, None)
    m0 = kb.mark()
    ps_ = kb.take_sems(1, "sync")[0]
    pg_ = kb.take_sems(1, "gp")[0]
    rw = kb.tile(128, 8 * 16, BF16)
    kb.dma("gpsimd", rw.ap().rearrange("p (kc m) -> p kc m", kc=NCH)[:, :, 0:15],
           d["L1_ffn_router_W"].rearrange("(kc kp) m -> kp kc m", kp=128), inc=pg_)
    rb = load_row_bf16(kb, d["L1_ffn_router_b"], 15, sem=pg_)
    sb1r = load_row_bf16(kb, d["L1_ffn_sb1"], D, sem=pg_)
    sb2r = load_row_bf16(kb, d["L1_ffn_sb2"], D, sem=pg_)
    eb2t = kb.tile(15, D, BF16)
    kb.dma("gpsimd", eb2t.ap(), d["L1_ffn_eb2"], inc=pg_)
    for e in ("tensor", "vector", "scalar", "gpsimd"):
        kb.wait(e, pg_, pg_.n)
        kb.wait(e, ps_, ps_.n)
    cw_fm = kb.tile(16, T, BF16)
    cw_tm = kb.tile(128, 128, BF16)
    l_sb = kb.tile(128, 15, F32)
    m1 = kb.tile(128, 1, F32)
    ee = kb.tile(128, 15, F32)
    ssum = kb.tile(128, 1, F32)
    rsr = kb.tile(128, 1, F32)
    tt_ = kb.tile(128, 15, F32)
    gi = kb.tile(128, 15, F32)
    msum = kb.tile(128, 15, F32)
    kb.op("vector", lambda E: E.memset(cw_tm.ap(), 0.0), inc=kb.DV)
    gdr = {}
    for tt in range(16):
        b = tt % 2
        psl = kb.bank(b, cols=15)
        kb.use_bank(b, "tensor")
        for kc in range(NCH):
            kb.op("tensor", lambda E, kc=kc, tt=tt, psl=psl:
                  E.matmul(psl, st.zbs(kc)[:, tt * 128:(tt + 1) * 128],
                           rw.ap().rearrange("p (kc m) -> p kc m", kc=NCH)[:, kc, 0:15],
                           start=(kc == 0), stop=False))
        kb.op("tensor", lambda E, psl=psl:
              E.matmul(psl, ones128.ap(), rb.ap(), start=False, stop=True), inc=kb.PE)
        kb.wait("vector", kb.PE)
        kb.wait("vector", kb.DV)
        kb.op("vector", lambda E, psl=psl: E.tensor_copy(l_sb.ap(), psl), inc=kb.DV)
        kb.guard_bank(b, kb.DV)
        kb.wait("vector", kb.DV)
        kb.op("vector", lambda E: E.reduce_max(m1.ap(), l_sb.ap(), axis=AX.X,
                                               negate=True), inc=kb.DV)
        kb.wait("scalar", kb.DV)
        kb.op("scalar", lambda E: E.activation(ee.ap(), l_sb.ap(), AF.Exp,
                                               bias=m1.ap(), accum_out=ssum.ap()),
              inc=kb.AC)
        kb.wait("vector", kb.AC)
        kb.op("vector", lambda E: E.reciprocal(rsr.ap(), ssum.ap()), inc=kb.DV)
        kb.wait("vector", kb.DV)
        kb.op("vector", lambda E: E.tensor_copy(tt_.ap(), ee.ap()), inc=kb.DV)
        for it in range(3):
            kb.wait("vector", kb.DV)
            kb.op("vector", lambda E: E.reduce_max(m1.ap(), tt_.ap(), axis=AX.X),
                  inc=kb.DV)
            kb.wait("vector", kb.DV)
            kb.op("vector", lambda E: E.tensor_single_scalar(gi.ap(), tt_.ap(),
                                                             m1.ap(), ALU.is_ge),
                  inc=kb.DV)
            kb.wait("vector", kb.DV)
            if it == 0:
                kb.op("vector", lambda E: E.tensor_copy(msum.ap(), gi.ap()),
                      inc=kb.DV)
            else:
                kb.op("vector", lambda E: E.tensor_add(msum.ap(), msum.ap(), gi.ap()),
                      inc=kb.DV)
            if it < 2:
                kb.wait("vector", kb.DV)
                kb.op("vector", lambda E: E.scalar_tensor_tensor(
                    tt_.ap(), gi.ap(), -1e30, tt_.ap(), ALU.mult, ALU.add), inc=kb.DV)
        kb.wait("vector", kb.DV)
        g = gdr.get("cwtm")
        if g:
            kb.wait("vector", g[0], g[1])
        kb.op("vector", lambda E: E.scalar_tensor_tensor(
            cw_tm.ap()[:, 0:15], ee.ap(), rsr.ap(), msum.ap(), ALU.mult, ALU.mult),
            inc=kb.DV)
        kb.use_bank(2, "tensor")
        kb.wait("tensor", kb.DV)
        kb.op("tensor", lambda E, tt=tt:
              E.transpose(kb.ps_view(128, 0, 512 * 2, 64, BF16), cw_tm.ap(), idn.ap()),
              inc=kb.PE)
        gdr["cwtm"] = (kb.PE, kb.PE.n)
        kb.wait("vector", kb.PE)
        kb.op("vector", lambda E, tt=tt:
              E.tensor_copy(cw_fm.ap()[0:16, tt * 128:(tt + 1) * 128],
                            kb.ps_view(16, 0, 512 * 2, 64, BF16)), inc=kb.DV)
        kb.guard_bank(2, kb.DV)
    kb.phase_fence()
    he = [[kb.tile(128, T, BF16) for _ in range(NCH)] for _ in range(2)]
    hs = he[0]
    cwb = [kb.tile(128, T, BF16) for _ in range(2)]
    crow = [kb.tile(1, TB, BF16) for _ in range(2)]
    w2buf = [[kb.tile(128, 8 * 128, BF16) for _ in range(2)] for _ in range(2)]
    csem = kb.take_sems(2, "gp")
    w2sem = kb.take_sems(4, "syncs")
    gd = {}
    def gwait(eng, key):
        g = gd.get(key)
        if g:
            kb.wait(eng, g[0], g[1])
    def gset(key, sem):
        gd[key] = (sem, sem.n)

    def w1_evac_shared(mi, tb, psu, mrows):
        kb.wait("scalar", kb.PE)
        kb.op("scalar", lambda E, mi=mi, tb=tb, psu=psu:
              E.activation(hs[mi].ap()[:, tb * TB:(tb + 1) * TB], psu, AF.Relu),
              inc=kb.AC)
    emit_proj_fm(kb, d["L1_ffn_sW1"], D, D, srcb=lambda c: st.zbs(c),
                 evac=w1_evac_shared, evac_sem=kb.AC, mw=256, banks=(0, 1, 2, 3),
                 aug=(lambda mi: sb1r.ap()[0:1, mi * 128:(mi + 1) * 128],
                      lambda mi, tb: ones512.ap()), pre="L1_ffn_sW1")
    kb.fence_pe()
    def w2_evac(mi, tb, psu, mrows):
        kb.wait("vector", kb.PE)
        kb.wait("vector", kb.DV)
        kb.op("vector", lambda E, mi=mi, tb=tb, psu=psu:
              E.scalar_tensor_tensor(st.zs(mi)[:, tb * TB:(tb + 1) * TB], psu, 0.5,
                                     st.zs(mi)[:, tb * TB:(tb + 1) * TB],
                                     ALU.mult, ALU.add), inc=kb.DV)
    emit_proj_fm(kb, d["L1_ffn_sW2"], D, D, srcb=lambda c: hs[c].ap()[:, 0:T],
                 evac=w2_evac, evac_sem=kb.DV, mw=256, banks=(4, 5, 6, 7),
                 aug=(lambda mi: sb2r.ap()[0:1, mi * 128:(mi + 1) * 128],
                      lambda mi, tb: ones512.ap()), pre="L1_ffn_sW2")
    kb.phase_fence()
    eb1r = [kb.tile(1, D, BF16) for _ in range(2)]
    ebsem = kb.take_sems(2, "gp")
    groups = [(0, 1), (2, 3), (4, 5), (6, 7), (8, 9), (10, 11), (12, 13), (14,)]
    for gidx, grp in enumerate(groups):
        for ei, e in enumerate(grp):
            pp = ei
            gwait("gpsimd", ("eb1", pp))
            kb.dma("gpsimd", eb1r[pp].ap(),
                   d["L1_ffn_eb1"][e].rearrange("(o n) -> o n", o=1), inc=ebsem[pp])
            en_ = ebsem[pp].n
            for tb in range(NTB):
                gwait("gpsimd", ("crow", pp)) if tb == 0 else None
                kb.wait("gpsimd", kb.PE)
                kb.dma("gpsimd", crow[pp].ap(),
                       cw_fm.ap()[e:e + 1, tb * TB:(tb + 1) * TB], inc=csem[pp])
                cn = csem[pp].n
                bb = 2 + tb % 2
                kb.use_bank(bb, "tensor")
                kb.wait("tensor", csem[pp], cn)
                kb.op("tensor", lambda E, tb=tb, pp=pp, bb=bb:
                      E.matmul(kb.bank(bb), ones128.ap(),
                               crow[pp].ap(), start=True, stop=True), inc=kb.PE)
                gwait("scalar", ("cwb", pp)) if tb == 0 else None
                kb.wait("scalar", kb.PE)
                kb.op("scalar", lambda E, tb=tb, pp=pp, bb=bb:
                      E.activation(cwb[pp].ap()[:, tb * TB:(tb + 1) * TB],
                                   kb.bank(bb), AF.Copy), inc=kb.AC)
                kb.guard_bank(bb, kb.AC)
            gset(("crow", pp), kb.AC)
            kb.wait("tensor", ebsem[pp], en_)
            def w1_evac_e(mi, tb, psu, mrows, pp=pp):
                kb.wait("vector", kb.PE)
                kb.wait("vector", kb.AC)
                kb.wait("vector", kb.DV)
                kb.op("vector", lambda E, mi=mi, tb=tb, psu=psu, pp=pp:
                      E.scalar_tensor_tensor(he[pp][mi].ap()[:, tb * TB:(tb + 1) * TB],
                                             psu, 0.0,
                                             cwb[pp].ap()[:, tb * TB:(tb + 1) * TB],
                                             ALU.max, ALU.mult), inc=kb.DV)
            emit_proj_fm(kb, d["L1_ffn_eW1"][e], D, D, srcb=lambda c: st.zbs(c),
                         evac=w1_evac_e, evac_sem=kb.DV, mw=256, banks=(0, 1, 2, 3),
                         aug=(lambda mi, pp=pp: eb1r[pp].ap()[0:1, mi * 128:(mi + 1) * 128],
                              lambda mi, tb: ones512.ap()), pre="L1_ffn_eW1")
            gset(("eb1", pp), kb.PE)
            gset(("cwb", pp), kb.DV)
        kb.fence_pe()
        w2n = [0, 0]
        for fc in range(NCH):
            fp = fc % 2
            for ei, e in enumerate(grp):
                gwait("sync", ("w2", ei, fp))
                kb.wait_precast("sync", "L1_ffn_eW2")
                kb.dma("sync",
                       w2buf[ei][fp].ap().rearrange("p (kc m) -> p kc m", kc=NCH),
                       d["L1_ffn_eW2"][e][:, fc * 128:(fc + 1) * 128]
                       .rearrange("(kc kp) m -> kp kc m", kp=128),
                       inc=w2sem[ei * 2 + fp])
            wn = [w2sem[ei * 2 + fp].n for ei in range(len(grp))]
            for tb in range(NTB):
                bb = 4 + (fc * NTB + tb) % 4
                kb.use_bank(bb, "tensor")
                if tb == 0:
                    for ei in range(len(grp)):
                        kb.wait("tensor", w2sem[ei * 2 + fp], wn[ei])
                nmm = len(grp) * NCH + (1 if gidx == 0 else 0)
                n_ = 0
                for ei, e in enumerate(grp):
                    for kc in range(NCH):
                        n_ += 1
                        en_f = (n_ == nmm)
                        kb.op("tensor", lambda E, ei=ei, kc=kc, tb=tb, fp=fp, bb=bb,
                              st_=(n_ == 1), en=en_f:
                              E.matmul(kb.bank(bb),
                                       w2buf[ei][fp].ap()[:, kc * 128:(kc + 1) * 128],
                                       he[ei][kc].ap()[:, tb * TB:(tb + 1) * TB],
                                       start=st_, stop=en),
                              inc=(kb.PE if en_f else None))
                if gidx == 0:
                    n_ += 1
                    kb.op("tensor", lambda E, fc=fc, tb=tb, bb=bb:
                          E.matmul(kb.bank(bb), eb2t.ap()[0:15, fc * 128:(fc + 1) * 128],
                                   cw_fm.ap()[0:15, tb * TB:(tb + 1) * TB],
                                   start=False, stop=True), inc=kb.PE)
                w2_evac(fc, tb, kb.bank(bb), 128)
                kb.guard_bank(bb, kb.DV)
            for ei in range(len(grp)):
                gset(("w2", ei, fp), kb.PE)
        kb.phase_fence()
    kb.reset(m0)


import concourse.bass as bass
from concourse.bass_utils import run_bass_kernel_spmd

_hostc().update(dense_consts())

FFN_CONSTS = ["c_idn", "c_tri", "c_tri2", "c_negI", "c_ones_col", "c_ones_row",
              "c_ones512", "c_ones_rowD", "c_blk2", "c_tiny", "c_eps"]
BASE_CONSTS = FFN_CONSTS
L0A_CONSTS = ["c_idn", "c_tri", "c_negI", "c_ones_col", "c_ones_row", "c_ones_rowD",
              "c_eps"]
L1A_CONSTS = ["c_idn", "c_tri", "c_negI", "c_ones_col", "c_ones_rowD", "c_eps"]
MOE_CONSTS = ["c_idn", "c_ones_col", "c_ones_row", "c_ones_rowD", "c_ones512",
              "c_eps"]

FFN_W = ["L0_ffn_Wqkv", "L0_ffn_Wo", "L0_ffn_W1", "L0_ffn_W2", "L0_ffn_q_scale",
         "L0_ffn_k_scale", "L0_ffn_ln_g", "L0_ffn_ln_b", "L0_ffn_b1", "L0_ffn_b2"]
L0A_W = ["L0_attn_W_query", "L0_attn_W_key", "L0_attn_W_value", "L0_attn_W_proj",
         "L0_attn_b_proj"]
L1A_W = ["L1_attn_ln_w", "L1_attn_ln_b", "L1_attn_wq", "L1_attn_wk", "L1_attn_wv",
         "L1_attn_wo"]
MOE_W = ["L1_ffn_router_W", "L1_ffn_router_b", "L1_ffn_sW1", "L1_ffn_sb1",
         "L1_ffn_sW2", "L1_ffn_sb2", "L1_ffn_eW1", "L1_ffn_eb1", "L1_ffn_eW2",
         "L1_ffn_eb2"]

SHAPES = {
    "L0_ffn_Wqkv": [6, D, 3072], "L0_ffn_Wo": [6, D, D], "L0_ffn_W1": [6, D, 5460],
    "L0_ffn_W2": [6, 2730, D], "L0_ffn_q_scale": [6, 64], "L0_ffn_k_scale": [6, 64],
    "L0_ffn_ln_g": [6, D], "L0_ffn_ln_b": [6, D], "L0_ffn_b1": [6, 5460],
    "L0_ffn_b2": [6, D],
    "L0_attn_W_query": [D, D], "L0_attn_W_key": [D, D], "L0_attn_W_value": [D, D],
    "L0_attn_W_proj": [D, D], "L0_attn_b_proj": [D],
    "L1_attn_ln_w": [D], "L1_attn_ln_b": [D], "L1_attn_wq": [16, D, 64],
    "L1_attn_wk": [16, D, 64], "L1_attn_wv": [16, D, 64], "L1_attn_wo": [D, D],
    "L1_ffn_router_W": [D, 15], "L1_ffn_router_b": [15], "L1_ffn_sW1": [D, D],
    "L1_ffn_sb1": [D], "L1_ffn_sW2": [D, D], "L1_ffn_sb2": [D],
    "L1_ffn_eW1": [15, D, D], "L1_ffn_eb1": [15, D], "L1_ffn_eW2": [15, D, D],
    "L1_ffn_eb2": [15, D],
}


BIG_W = ["L0_ffn_Wqkv", "L0_ffn_W1", "L0_ffn_W2", "L0_ffn_Wo",
         "L0_attn_W_query", "L0_attn_W_key", "L0_attn_W_value", "L0_attn_W_proj",
         "L1_ffn_sW1", "L1_ffn_sW2", "L1_ffn_eW1", "L1_ffn_eW2",
         "L1_attn_wq", "L1_attn_wk", "L1_attn_wv", "L1_attn_wo"]


def _prog_common(wnames, consts):
    nc = bass.Bass("TRN2", target_bir_lowering=False)
    kb = KB(nc)
    d = {n: kb.din(n, SHAPES[n], F32) for n in wnames}
    xd = kb.din("x", [D, T], F32)
    outz = kb.dout("outz", [D, T], F32)
    st = St(kb)
    for c in range(NCH):
        kb.dma("sync", st.zs(c), xd[c * 128:(c + 1) * 128, :])
    kb.preload_consts(consts)
    kb.barrier_loads()
    kb.precast_weights(d, [n for n in BIG_W if n in wnames], SHAPES)
    return nc, kb, d, st, outz


def _prog_finish(nc, kb, st, outz):
    STm = kb.sem("STOUT")
    for c in range(NCH):
        kb.wait("sync", kb.DV)
        kb.wait("sync", kb.GP)
        kb.dma("sync", outz[c * 128:(c + 1) * 128, :], st.zs(c), inc=STm)
    kb.wait("sync", STm)
    kb.finalize()
    return nc, kb


def build_ffn_prog():
    nc, kb, d, st, outz = _prog_common(FFN_W, FFN_CONSTS)
    xspill = kb.dtmp("xspill", [D, T], F32)
    l0ffn_invocation(kb, st, d, xspill, 6)
    return _prog_finish(nc, kb, st, outz)


def build_l0attn_prog():
    nc, kb, d, st, outz = _prog_common(L0A_W, L0A_CONSTS)
    l0_attn_slot(kb, st, d)
    return _prog_finish(nc, kb, st, outz)


def build_l1attn_prog():
    nc, kb, d, st, outz = _prog_common(L1A_W, L1A_CONSTS)
    l1_attn_slot(kb, st, d)
    return _prog_finish(nc, kb, st, outz)


def build_moe_prog():
    nc, kb, d, st, outz = _prog_common(MOE_W, MOE_CONSTS)
    moe_slot(kb, st, d)
    return _prog_finish(nc, kb, st, outz)


def build_fused_prog():
    allw = FFN_W + L0A_W + L1A_W + MOE_W
    nc, kb, d, st, outz = _prog_common(allw, BASE_CONSTS)
    xspill = kb.dtmp("xspill", [D, T], F32)
    l0ffn_invocation(kb, st, d, xspill, 6)
    l0_attn_slot(kb, st, d)
    l0ffn_invocation(kb, st, d, xspill, 6)
    moe_slot(kb, st, d)
    l1_attn_slot(kb, st, d)
    moe_slot(kb, st, d)
    return _prog_finish(nc, kb, st, outz)


def kernel(**inputs):
    xin = inputs["L0_attn_x"]
    ncores = 8
    xs = [np.ascontiguousarray(xin[i].T).astype(np.float32, copy=False)
          for i in range(ncores)]
    nc, kb = build_fused_prog()
    base = dict(kb.inputs)
    for n in FFN_W + L0A_W + L1A_W + MOE_W:
        base[n] = inputs[n]
    in_maps = [dict(base, x=xs[i]) for i in range(ncores)]
    res = run_bass_kernel_spmd(nc, in_maps, core_ids=list(range(ncores)))
    out = np.stack([np.asarray(res.results[i]["outz"]).T for i in range(ncores)],
                   axis=0).astype(np.float32)
    return out


def kernel_unfused(**inputs):
    xin = inputs["L0_attn_x"]
    ncores = 8
    xs = [np.ascontiguousarray(xin[i].T).astype(np.float32, copy=False)
          for i in range(ncores)]
    progs = {}
    progs["ffn"] = build_ffn_prog()
    progs["l0a"] = build_l0attn_prog()
    progs["moe"] = build_moe_prog()
    progs["l1a"] = build_l1attn_prog()
    wsets = {"ffn": FFN_W, "l0a": L0A_W, "moe": MOE_W, "l1a": L1A_W}
    for step in ("ffn", "l0a", "ffn", "moe", "l1a", "moe"):
        nc, kb = progs[step]
        base = dict(kb.inputs)
        for n in wsets[step]:
            base[n] = inputs[n]
        in_maps = [dict(base, x=xs[i]) for i in range(ncores)]
        res = run_bass_kernel_spmd(nc, in_maps, core_ids=list(range(ncores)))
        xs = [np.asarray(res.results[i]["outz"]) for i in range(ncores)]
    out = np.stack([xs[i].T for i in range(ncores)], axis=0).astype(np.float32)
    return out


# version 08 (final)
# speedup vs baseline: 1.1580x; 1.0045x over previous
import numpy as np
from ml_dtypes import bfloat16
import concourse.bass as bass
import concourse.mybir as mybir

F32 = mybir.dt.float32
BF16 = mybir.dt.bfloat16
U8 = mybir.dt.uint8
AF = mybir.ActivationFunctionType
ALU = mybir.AluOpType
AX = mybir.AxisListType

T = 2048
D = 1024
NCH = 8
TB = 512
NTB = T // TB
EPS = 1e-5
NEG = -1e30


class Sem:
    __slots__ = ("name", "n", "h")
    def __init__(self, name):
        self.name, self.n, self.h = name, 0, None


class Tile:
    __slots__ = ("kb", "boff", "rows", "cols", "dt")
    def __init__(self, kb, boff, rows, cols, dt):
        self.kb, self.boff, self.rows, self.cols, self.dt = kb, boff, rows, cols, dt

    def ap(self):
        esz = 2 if self.dt == BF16 else 4
        a = self.kb.arena_ap[:, self.boff:self.boff + self.cols * esz].bitcast(self.dt)
        if self.rows != 128:
            a = a[0:self.rows, :]
        return a

    def __getitem__(self, key):
        return self.ap()[key]


class KB:
    ENGINES = ("sync", "tensor", "vector", "scalar", "gpsimd")

    def __init__(self, nc, arena_bytes=212608):
        import contextlib
        self.nc = nc
        self.streams = {e: [] for e in self.ENGINES}
        self.sems = []
        self.arena_bytes = arena_bytes
        self.arena_off = 0
        self.arena_hwm = 0
        self._stack = contextlib.ExitStack()
        arena_t = self._stack.enter_context(nc.sbuf_tensor("ARENA", [128, arena_bytes], U8))
        self.arena_ap = arena_t.ap()
        ps_t = self._stack.enter_context(nc.psum_tensor("PS", [128, 4096], F32))
        self.ps = ps_t.ap()
        self.LD = self.sem("LD")
        self.WL = self.sem("WL")
        self.SD = self.sem("SD")
        self.PE = self.sem("PE")
        self.DV = self.sem("DV")
        self.AC = self.sem("AC")
        self.GP = self.sem("GP")
        self.consts = {}
        self.bank_guards = {}
        self.sealed = False
        self._pool = {"gp": [self.sem(f"SG{i}") for i in range(8)],
                      "gpw": [self.sem(f"SW{i}") for i in range(4)],
                      "sync": [self.sem(f"SY{i}") for i in range(6)],
                      "syncs": [self.sem(f"SS{i}") for i in range(8)],
                      "syncw": [self.sem(f"SV{i}") for i in range(4)]}
        self._pool_i = {"gp": 0, "gpw": 0, "sync": 0, "syncs": 0, "syncw": 0}
        self.precast = {}
        self._sem_q = {}
        self.dram = {}
        self.inputs = {}

    def sem(self, name):
        s = Sem(name)
        s.h = self._stack.enter_context(self.nc.semaphore(name))
        self.sems.append(s)
        return s

    def alloc(self, nbytes, align=4):
        self.arena_off = -(-self.arena_off // align) * align
        r = self.arena_off
        self.arena_off += nbytes
        self.arena_hwm = max(self.arena_hwm, self.arena_off)
        assert self.arena_off <= self.arena_bytes, \
            f"SBUF arena overflow: {self.arena_off} > {self.arena_bytes}"
        return r

    def mark(self):
        return self.arena_off

    def reset(self, m):
        self.arena_off = m

    def tile(self, rows, cols, dt, align=None):
        esz = 2 if dt == BF16 else 4
        boff = self.alloc(cols * esz, align or esz)
        return Tile(self, boff, rows, cols, dt)

    def emit(self, eng, fn):
        self.streams[eng].append(fn)

    def wait(self, eng, sem, val=None):
        v = sem.n if val is None else val
        assert v <= sem.n, f"wait on future value of {sem.name}: {v} > {sem.n}"
        if v <= 0:
            return
        self.emit(eng, lambda E, s=sem, v=v: E.wait_ge(s.h, v))

    def drain(self, eng):
        self.emit(eng, lambda E: E.drain())

    def op(self, eng, builder, inc=None, inck=1):
        if inc is not None:
            inc.n += inck
            def f(E, b=builder, s=inc, k=inck):
                b(E).then_inc(s.h, k)
        else:
            def f(E, b=builder):
                b(E)
        self.emit(eng, f)
        return inc.n if inc is not None else None

    def take_sems(self, n, q="gp"):
        p = self._pool[q]
        i = self._pool_i[q]
        r = [p[(i + k) % len(p)] for k in range(n)]
        self._pool_i[q] = (i + n) % len(p)
        return r

    def dma(self, eng, out_ap, in_ap, inc=None, accum=False):
        if inc is None:
            assert not self.sealed, "post-prologue DMA needs an explicit stream sem"
            inc = self.LD
            assert eng in ("sync", "gpsimd")
            if eng == "gpsimd":
                inc = self.WL
        if accum:
            def b(E, o=out_ap, i=in_ap):
                return E.dma_start(out=(o() if callable(o) else o),
                                   in_=(i() if callable(i) else i),
                                   accum_op=ALU.add)
        else:
            def b(E, o=out_ap, i=in_ap):
                return E.dma_start(out=(o() if callable(o) else o),
                                   in_=(i() if callable(i) else i))
        self.op(eng, b, inc=inc, inck=16)
        return inc.n

    def ps_view(self, rows=128, rowoff=0, col0=0, cols=512, dt=F32):
        a = self.ps[:, col0:col0 + cols]
        if dt != F32:
            a = a.bitcast(dt)
        return a[rowoff:rowoff + rows, :]

    def bank(self, b, rows=128, cols=512, rowoff=0, coloff=0, dt=F32):
        return self.ps_view(rows, rowoff, 512 * b + coloff, cols, dt)

    def use_bank(self, b, eng):
        for sem, cnt in self.bank_guards.get(b, ()):
            self.wait(eng, sem, cnt)
        self.bank_guards[b] = []

    def guard_bank(self, b, sem):
        self.bank_guards.setdefault(b, []).append((sem, sem.n))

    def fence_pe(self):
        self.wait("tensor", self.DV)
        self.wait("tensor", self.AC)
        self.wait("tensor", self.GP)

    def phase_fence(self):
        for e in self.ENGINES:
            for s in (self.PE, self.DV, self.AC, self.GP):
                self.wait(e, s)
        self.bank_guards = {}

    def barrier_loads(self):
        for e in self.ENGINES:
            self.wait(e, self.LD)
            self.wait(e, self.WL)
        self.sealed = True

    def const(self, name, arr=None):
        if name in self.consts:
            return self.consts[name][0]
        if arr is None:
            arr = _hostc()[name]
        dt = BF16 if arr.dtype == bfloat16 else F32
        t = self.tile(arr.shape[0], arr.shape[1], dt)
        self.consts[name] = (t, arr)
        assert not self.sealed, f"const {name} touched after prologue; preload it"
        d = self.nc.dram_tensor(name, list(arr.shape), dt, kind="ExternalInput").ap()
        self.inputs[name] = arr
        self.dma("sync", t.ap(), d)
        return t

    def preload_consts(self, names):
        for n in names:
            self.const(n)

    def load_consts(self):
        pass

    def const_dram(self, name):
        key = "dram_" + name
        if key in self.dram:
            return self.dram[key].ap()
        arr = _hostc()[name]
        dt = BF16 if arr.dtype == bfloat16 else F32
        h = self.nc.dram_tensor(name, list(arr.shape), dt, kind="ExternalInput")
        self.dram[key] = h
        self.inputs[name] = arr
        return h.ap()

    def pconst(self, name):
        t = getattr(self, "phase_consts", {}).get(name)
        return t if t is not None else self.const(name)

    def load_pconsts(self, names, eng="sync"):
        if not hasattr(self, "phase_consts"):
            self.phase_consts = {}
        sem = self.take_sems(1, "sync")[0]
        for n in names:
            arr = _hostc()[n]
            dt = BF16 if arr.dtype == bfloat16 else F32
            t = self.tile(arr.shape[0], arr.shape[1], dt)
            self.dma("sync", t.ap(), self.const_dram(n), inc=sem)
            self.phase_consts[n] = t
        for e in self.ENGINES:
            self.wait(e, sem, sem.n)

    def clear_pconsts(self, names):
        for n in names:
            self.phase_consts.pop(n, None)

    def precast_weights(self, d, names, shapes):
        psem = self.sem("PCAST")
        for n in names:
            ap16 = self.dtmp(n + "_bf16", shapes[n], BF16)
            self.dma("gpsimd", ap16, d[n], inc=psem)
            self.emit("gpsimd", lambda E, s=psem, v=psem.n: E.wait_ge(s.h, v))
            self.precast[n] = (psem, psem.n)
            d[n] = ap16

    def wait_precast(self, eng, name):
        g = self.precast.get(name)
        if g:
            self.wait(eng, g[0], g[1])

    def din(self, name, shape, dt=F32):
        h = self.nc.dram_tensor(name, list(shape), dt, kind="ExternalInput")
        self.dram[name] = h
        return h.ap()

    def dout(self, name, shape, dt=F32):
        h = self.nc.dram_tensor(name, list(shape), dt, kind="ExternalOutput")
        self.dram[name] = h
        return h.ap()

    def dtmp(self, name, shape, dt=F32):
        h = self.nc.dram_tensor(name, list(shape), dt)
        self.dram[name] = h
        return h.ap()

    def finalize(self):
        nc = self.nc
        block = self._stack.enter_context(nc.Block())
        for eng in self.ENGINES:
            ops = self.streams[eng]
            if ops:
                getattr(block, eng)(lambda E, ops=ops: [f(E) for f in ops])
        self._stack.close()
        return nc


def host_constants():
    c = {}
    c["c_idn"] = np.eye(128, dtype=bfloat16)
    tri = (np.arange(128)[:, None] > np.arange(128)[None, :])
    c["c_tri"] = tri.astype(bfloat16)
    c["c_negI"] = (NEG * np.eye(128)).astype(bfloat16)
    c["c_ones_col"] = np.ones((128, 1), dtype=bfloat16)
    c["c_ones_row"] = np.ones((1, 128), dtype=bfloat16)
    c["c_ones_rowT"] = np.ones((1, T), dtype=bfloat16)
    DH, WIN = 64, 256
    half = np.arange(0, DH, 2, dtype=np.float64)
    inv_freq = 1.0 / 10000.0 ** (half / DH)
    t = np.arange(T, dtype=np.float64)
    freqs = np.concatenate([t[:, None] * inv_freq[None, :]] * 2, -1)
    base_scale = (half + 0.4 * DH) / (1.4 * DH)
    power = (t - T // 2) / (WIN // 2)
    scale = np.concatenate([base_scale[None, :] ** power[:, None]] * 2, -1)
    cos, sin = np.cos(freqs), np.sin(freqs)
    def s2(S):
        S2 = np.empty_like(S)
        S2[0:32] = S[32:64]
        S2[32:64] = -S[0:32]
        return S2
    def dup(A):
        return np.concatenate([A, A], axis=0)
    c["c_rotCp"] = dup((cos * scale).T).astype(bfloat16)
    c["c_rotS2p"] = dup(s2((sin * scale).T)).astype(bfloat16)
    c["c_rotCm"] = dup((cos / scale).T).astype(bfloat16)
    c["c_rotS2m"] = dup(s2((sin / scale).T)).astype(bfloat16)
    i = np.arange(128)[:, None]
    j = np.arange(512)[None, :]
    c["c_delta"] = ((j - i).astype(np.float32) * np.ones((128, 512))).astype(np.float32)
    dc, mc = [], []
    for a in range(4):
        m = (j - 128 * a) > i
        dte = (j - i).astype(np.float32) * np.ones((128, 512))
        dte[np.broadcast_to(m, dte.shape)] = -3e11
        dc.append(dte.astype(np.float32))
        mc.append(np.where(m, NEG, 0.0).astype(np.float32) * np.ones((128, 512), np.float32))
    c["c_deltac"] = np.concatenate(dc, axis=1)
    c["c_causal"] = np.concatenate(mc, axis=1)
    lm = []
    for first in range(2):
        for hh in range(2):
            ia = 128 * hh + i
            if first:
                att = (j >= 256) & ((j - 256) <= ia)
            else:
                att = (j - 256 <= ia) & (ia <= j)
            lm.append(np.where(att, 0.0, NEG).astype(np.float32))
    c["c_lmask"] = np.concatenate(lm, axis=1)
    c["c_iota"] = np.arange(T, dtype=np.float32)[None, :]
    c["c_eps"] = np.full((1, 1), EPS, dtype=np.float32)
    c["c_tiny"] = np.full((128, 1), 1e-28, dtype=np.float32)
    c["c_one"] = np.full((1, 1), 1.0, dtype=np.float32)
    c["c_tri2"] = (np.arange(128)[:, None] < np.arange(128)[None, :]).astype(bfloat16)
    blk = (np.arange(128)[:, None] // 64 == np.arange(128)[None, :] // 64)
    c["c_blk2"] = blk.astype(bfloat16)
    c["c_ones512"] = np.ones((1, 512), dtype=bfloat16)
    c["c_ones_rowD"] = np.ones((1, D), dtype=bfloat16)
    return c


HOSTC = None
def _hostc():
    global HOSTC
    if HOSTC is None:
        HOSTC = host_constants()
    return HOSTC


def load_row_bf16(kb, dram_vec_ap, n, sem=None):
    t = kb.tile(1, n, BF16)
    kb.dma("gpsimd", t.ap(), dram_vec_ap.rearrange("(o n) -> o n", o=1) if len(dram_vec_ap.shape) == 1 else dram_vec_ap, inc=sem)
    return t

def load_row_f32(kb, dram_vec_ap, n, sem=None):
    t = kb.tile(1, n, F32)
    kb.dma("sync", t.ap(), dram_vec_ap.rearrange("(o n) -> o n", o=1) if len(dram_vec_ap.shape) == 1 else dram_vec_ap, inc=sem)
    return t

def load_cols_f32(kb, dram_vec_ap, n, sem=None):
    ncols = -(-n // 128)
    t = kb.tile(128, ncols, F32)
    for c in range(ncols):
        r = min(128, n - c * 128)
        kb.dma("sync", t.ap()[0:r, c:c + 1],
               dram_vec_ap[c * 128:c * 128 + r].rearrange("(p o) -> p o", o=1), inc=sem)
    return t


def emit_ln(kb, src, srcb, dst, grow, bcol, src_is_bf16=False, tag=""):
    if src is None:
        src = srcb
    _m = kb.mark()
    mu = kb.tile(1, T, F32); mu2 = kb.tile(1, T, F32)
    sr = kb.tile(1, T, F32); rr = kb.tile(1, T, F32)
    vr = mu2
    mb = mu2
    rrb = kb.tile(1, T, BF16); mbb = kb.tile(1, T, BF16)
    sq_pp = [kb.tile(128, TB, BF16) for _ in range(2)]
    xb_pp = [kb.tile(128, TB, BF16) for _ in range(2)] if srcb is None else None
    rg_sb = [kb.tile(128, TB, F32) for _ in range(2)]
    tmp_pp = [kb.tile(128, TB, F32) for _ in range(2)]
    onescol = kb.const("c_ones_col")

    kb.fence_pe()
    pe0 = kb.PE.n
    for tb in range(NTB):
        bk = 4 + tb
        sum_ps = kb.bank(bk, rows=1, rowoff=0)
        sq_ps = kb.bank(bk, rows=1, rowoff=32)
        for c in range(NCH):
            pp = c % 2
            k = tb * NCH + c
            if k >= 2:
                kb.wait("scalar", kb.PE, pe0 + k - 1)
            kb.op("scalar", lambda E, s=src(c), d=sq_pp[pp], tb=tb:
                  E.activation(d.ap(), s[:, tb * TB:(tb + 1) * TB], AF.Square),
                  inc=kb.AC)
            if srcb is None:
                if k >= 2:
                    kb.wait("gpsimd", kb.PE, pe0 + k - 1)
                kb.op("gpsimd", lambda E, s=src(c), d=xb_pp[pp], tb=tb:
                      E.tensor_copy(d.ap(), s[:, tb * TB:(tb + 1) * TB]), inc=kb.GP)
                xb = xb_pp[pp].ap()
            else:
                xb = srcb(c)[:, tb * TB:(tb + 1) * TB]
            kb.wait("tensor", kb.AC)
            if srcb is None:
                kb.wait("tensor", kb.GP)
            kb.op("tensor", lambda E, xb=xb, sp=sum_ps, st=(c == 0), en=(c == NCH - 1):
                  E.matmul(sp, onescol.ap(), xb, start=st, stop=en))
            kb.op("tensor", lambda E, d=sq_pp[pp], qp=sq_ps, st=(c == 0), en=(c == NCH - 1):
                  E.matmul(qp, onescol.ap(), d.ap(), start=st, stop=en), inc=kb.PE)

    sum_all = kb.ps_view(rows=1, rowoff=0, col0=512 * 4, cols=2048)
    sq_all = kb.ps_view(rows=1, rowoff=32, col0=512 * 4, cols=2048)
    kb.wait("vector", kb.PE)
    kb.op("vector", lambda E: E.tensor_scalar_mul(mu.ap(), sum_all, 1.0 / D))
    kb.drain("vector")
    kb.op("vector", lambda E: E.tensor_tensor(mu2.ap(), mu.ap(), mu.ap(), ALU.mult))
    kb.drain("vector")
    kb.op("vector", lambda E: E.scalar_tensor_tensor(vr.ap(), sq_all, 1.0 / D, mu2.ap(),
                                                     ALU.mult, ALU.subtract), inc=kb.DV)
    kb.wait("scalar", kb.DV)
    epst = kb.const("c_eps")
    kb.op("scalar", lambda E: E.activation(sr.ap(), vr.ap(), AF.Sqrt, bias=epst.ap()),
          inc=kb.AC)
    kb.wait("vector", kb.AC)
    kb.op("vector", lambda E: E.reciprocal(rr.ap(), sr.ap()))
    kb.drain("vector")
    kb.op("vector", lambda E: E.scalar_tensor_tensor(mb.ap(), mu.ap(), -1.0, rr.ap(),
                                                     ALU.mult, ALU.mult))
    kb.drain("vector")
    kb.op("vector", lambda E: E.tensor_copy(rrb.ap(), rr.ap()))
    kb.drain("vector")
    kb.op("vector", lambda E: E.tensor_copy(mbb.ap(), mb.ap()), inc=kb.DV)
    dv_rows = kb.DV.n

    ac0, gp0, dv0 = kb.AC.n, kb.GP.n, kb.DV.n
    pe1 = kb.PE.n
    for c in range(NCH):
        for tb in range(NTB):
            k = c * NTB + tb
            pp = k % 2
            rg_ps = kb.bank(pp)
            mb_ps = kb.bank(2 + pp)
            kb.wait("tensor", kb.DV, dv_rows)
            if k >= 2:
                kb.wait("tensor", kb.AC, ac0 + (k - 1))
                kb.wait("tensor", kb.DV, dv0 + (k - 1))
            kb.op("tensor", lambda E, c=c, tb=tb, d=rg_ps:
                  E.matmul(d, grow.ap()[0:1, c * 128:(c + 1) * 128],
                           rrb.ap()[0:1, tb * TB:(tb + 1) * TB], start=True, stop=True))
            kb.op("tensor", lambda E, c=c, tb=tb, d=mb_ps:
                  E.matmul(d, grow.ap()[0:1, c * 128:(c + 1) * 128],
                           mbb.ap()[0:1, tb * TB:(tb + 1) * TB], start=True, stop=True),
                  inc=kb.PE)
            kb.wait("scalar", kb.PE)
            if k >= 2:
                kb.wait("scalar", kb.GP, gp0 + (k - 1))
            kb.op("scalar", lambda E, pp=pp, s=rg_ps:
                  E.activation(rg_sb[pp].ap(), s, AF.Copy), inc=kb.AC)
            kb.wait("gpsimd", kb.AC)
            if k >= 2:
                kb.wait("gpsimd", kb.DV, dv0 + (k - 1))
            kb.op("gpsimd", lambda E, c=c, tb=tb, pp=pp:
                  E.tensor_tensor(tmp_pp[pp].ap(), src(c)[:, tb * TB:(tb + 1) * TB],
                                  rg_sb[pp].ap(), ALU.mult), inc=kb.GP)
            kb.wait("vector", kb.GP)
            b_ap = bcol.ap()[:, c:c + 1] if bcol is not None else 0.0
            kb.op("vector", lambda E, c=c, tb=tb, pp=pp, mp=mb_ps, b=b_ap:
                  E.scalar_tensor_tensor(dst(c)[:, tb * TB:(tb + 1) * TB], mp, b,
                                         tmp_pp[pp].ap(), ALU.add, ALU.add), inc=kb.DV)
    kb.reset(_m)


def emit_proj_fm(kb, wdram, din, dout, srcb, evac, evac_sem, aug=None,
                 mw=512, banks=(0, 1, 2, 3, 4, 5, 6, 7), ntb=NTB, pre=None, tag=""):
    if pre:
        kb.wait_precast("sync", pre)
    nkc = -(-din // 128)
    nmb = -(-dout // mw)
    _m = kb.mark()
    wbuf = [kb.tile(128, nkc * mw, BF16) for _ in range(2)]
    wsem = kb.take_sems(2, "syncw")
    evac_counts = []
    nbank = len(banks)
    use = 0
    pe_last_for_buf = [0, 0]
    for mb_i in range(nmb):
        m0 = mb_i * mw
        mwc = min(mw, dout - m0)
        bufp = mb_i % 2
        if mb_i >= 2:
            kb.wait("sync", kb.PE, pe_last_for_buf[bufp])
        else:
            kb.wait("sync", kb.PE)
        nkfull = din // 128
        if nkfull:
            kb.dma("sync",
                   wbuf[bufp].ap()[:, 0:nkfull * mwc]
                   .rearrange("p (kc m) -> p kc m", kc=nkfull),
                   wdram[0:nkfull * 128, m0:m0 + mwc]
                   .rearrange("(kc kp) m -> kp kc m", kp=128), inc=wsem[bufp])
        if nkfull * 128 != din:
            rem = din - nkfull * 128
            kb.dma("sync", wbuf[bufp].ap()[0:rem, nkfull * mwc:(nkfull + 1) * mwc],
                   wdram[nkfull * 128:din, m0:m0 + mwc], inc=wsem[bufp])
        wl_ready = wsem[bufp].n
        nmc = -(-mwc // 128)
        for mc in range(nmc):
            mm0 = mc * 128
            mrows = min(128, mwc - mm0)
            mi = (m0 + mm0) // 128
            for tb in range(ntb):
                bk = banks[use % nbank]
                ps = kb.bank(bk, rows=mrows)
                kb.use_bank(bk, "tensor")
                if use >= nbank:
                    kb.wait("tensor", evac_sem, evac_counts[use - nbank])
                if tb == 0 and mc == 0:
                    kb.wait("tensor", wsem[bufp], wl_ready)
                last_k = nkc - 1 if aug is None else None
                for kc in range(nkc):
                    krows = min(128, din - kc * 128)
                    if kc * 128 + 128 <= din:
                        w_ap = wbuf[bufp].ap()[:, kc * mwc + mm0:kc * mwc + mm0 + mrows]
                    else:
                        w_ap = wbuf[bufp].ap()[0:krows, (din // 128) * mwc + mm0:
                                               (din // 128) * mwc + mm0 + mrows]
                    st = (kc == 0)
                    en = (kc == nkc - 1) and aug is None
                    kb.op("tensor", lambda E, w=w_ap, s=srcb(kc), ps=ps, tb=tb, st=st, en=en, kr=krows:
                          E.matmul(ps, w, s[0:kr, tb * TB:(tb + 1) * TB], start=st, stop=en),
                          inc=(kb.PE if en else None))
                if aug is not None:
                    kb.op("tensor", lambda E, ps=ps, tb=tb, mi=mi, a=aug:
                          E.matmul(ps, a[0](mi), a[1](mi, tb), start=False, stop=True),
                          inc=kb.PE)
                pe_last_for_buf[bufp] = kb.PE.n
                evac(mi, tb, ps, mrows)
                kb.guard_bank(bk, evac_sem)
                evac_counts.append(evac_sem.n)
                use += 1
    kb.reset(_m)


"""Macaron megakernel emitters: streams, L0_ffn (local-attn transformer)."""
import numpy as np
from ml_dtypes import bfloat16

GT = T + 2


class St:
    def __init__(self, kb):
        self.kb = kb
        self.z = [kb.tile(128, GT, F32) for _ in range(NCH)]
        self.zb = [kb.tile(128, GT, BF16) for _ in range(NCH)]
        for c in range(NCH):
            kb.op("vector", lambda E, t=self.z[c]: E.memset(t.ap()[:, 0:1], 0.0))
            kb.op("vector", lambda E, t=self.zb[c]: E.memset(t.ap()[:, 0:1], 0.0),
                  inc=(kb.DV if c == NCH - 1 else None))

    def zs(self, c, shifted=False):
        off = 0 if (shifted and c >= 4) else 1
        return self.z[c].ap()[:, off:off + T]

    def zbs(self, c, shifted=False):
        off = 0 if (shifted and c >= 4) else 1
        return self.zb[c].ap()[:, off:off + T]


def refresh_zb(kb, st, chunks=None):
    for c in (chunks if chunks is not None else range(NCH)):
        if c % 2 == 0:
            kb.wait("vector", kb.DV)
            kb.op("vector", lambda E, c=c: E.tensor_copy(st.zbs(c), st.zs(c)),
                  inc=kb.DV)
        else:
            kb.wait("scalar", kb.DV)
            kb.wait("scalar", kb.AC)
            kb.op("scalar", lambda E, c=c:
                  E.activation(st.zbs(c), st.zs(c), AF.Copy), inc=kb.AC)


class FfnParams:
    def __init__(self, kb, d, li):
        self.wqkv = d["L0_ffn_Wqkv"][li]
        self.wo = d["L0_ffn_Wo"][li]
        self.w1 = d["L0_ffn_W1"][li]
        self.w2 = d["L0_ffn_W2"][li]
        ps = kb.take_sems(1, "sync")[0]
        pg = kb.take_sems(1, "gp")[0]
        self.qs8 = kb.tile(128, 1, F32)
        self.ks = kb.tile(128, 1, F32)
        for half in range(2):
            kb.dma("sync", self.qs8.ap()[64 * half:64 * half + 64, :],
                   d["L0_ffn_q_scale"][li].rearrange("(p o) -> p o", o=1), inc=ps)
            kb.dma("sync", self.ks.ap()[64 * half:64 * half + 64, :],
                   d["L0_ffn_k_scale"][li].rearrange("(p o) -> p o", o=1), inc=ps)
        self.lng = load_row_bf16(kb, d["L0_ffn_ln_g"][li], D, sem=pg)
        self.lnb = load_cols_f32(kb, d["L0_ffn_ln_b"][li], D, sem=ps)
        self.b1 = load_row_bf16(kb, d["L0_ffn_b1"][li], 5460, sem=pg)
        self.b2 = load_cols_f32(kb, d["L0_ffn_b2"][li], D, sem=ps)
        kb.wait("vector", ps, ps.n)
        kb.op("vector", lambda E: E.tensor_scalar_mul(self.qs8.ap(), self.qs8.ap(), 8.0))
        kb.drain("vector")
        kb.op("vector", lambda E: E.engine_nop(), inc=kb.DV)
        for e in ("tensor", "gpsimd", "scalar", "vector"):
            kb.wait(e, ps, ps.n)
            kb.wait(e, pg, pg.n)
            kb.wait(e, kb.DV)


class FfnScratch:
    def __init__(self, kb):
        self.out_all = [kb.tile(128, T, BF16) for _ in range(NCH)]
        self.wq_b = [kb.tile(128, 8 * 128, BF16) for _ in range(2)]
        self.wk_b = [kb.tile(128, 8 * 128, BF16) for _ in range(2)]
        self.wv_b = [kb.tile(128, 8 * 128, BF16) for _ in range(2)]
        self.qrot = kb.tile(128, T, BF16)
        self.krot = kb.tile(128, T, BF16)
        self.sq_pp = [kb.tile(128, TB, BF16) for _ in range(2)]
        self.rn_pp = [kb.tile(128, TB, BF16) for _ in range(2)]
        self.l2_pp = [kb.tile(128, TB, F32)] * 2
        self.nb_pp = [kb.tile(128, 1, F32) for _ in range(2)]
        self.sm_pp = [kb.tile(128, 1, F32) for _ in range(4)]
        self.rs_pp = [kb.tile(128, 1, F32) for _ in range(2)]
        m = kb.mark()
        self.u = kb.tile(128, T, BF16)
        self.s = kb.tile(128, T, BF16)
        hw1 = kb.arena_off
        kb.reset(m)
        self.v_tm = [kb.tile(128, 128, BF16) for _ in range(16)]
        self.P_pp = [kb.tile(128, 512, BF16) for _ in range(4)]
        self.P2 = [kb.tile(128, 512, BF16) for _ in range(2)]
        self.PT = [kb.tile(128, 256, BF16) for _ in range(4)]
        hw2 = kb.arena_off
        kb.arena_off = max(hw1, hw2)
        kb.arena_hwm = max(kb.arena_hwm, kb.arena_off)
        self.wsem = kb.take_sems(2, "syncs")
        self.rsem = kb.take_sems(2)
        self.gd = {}

    def gwait(self, kb, eng, key):
        g = self.gd.get(key)
        if g:
            kb.wait(eng, g[0], g[1])

    def gset(self, kb, key, sem):
        self.gd[key] = (sem, sem.n)


class GluScratch:
    def __init__(self, kb):
        self.r_pp = [kb.tile(128, 1024, BF16) for _ in range(2)]
        self.e1_pp = [kb.tile(128, TB, BF16) for _ in range(2)]
        self.q5_pp = [kb.tile(128, TB, BF16) for _ in range(2)]
        self.wg_pp = [kb.tile(128, 8 * 128, BF16) for _ in range(2)]
        self.wa_pp = [kb.tile(128, 8 * 128, BF16) for _ in range(2)]
        self.h = [kb.tile(128, 1024, BF16) for _ in range(22)]
        self.gd = {}

    def gwait(self, kb, eng, key):
        g = self.gd.get(key)
        if g:
            kb.wait(eng, g[0], g[1])

    def gset(self, kb, key, sem):
        self.gd[key] = (sem, sem.n)


def stream_pair_weights(kb, sc, p, pair, pp):
    s = sc.wsem[pp]
    kb.wait_precast("sync", "L0_ffn_Wqkv")
    for name, buf, base in (("q", sc.wq_b[pp], 0), ("k", sc.wk_b[pp], D),
                            ("v", sc.wv_b[pp], 2 * D)):
        sc.gwait(kb, "sync", ("wb", name, pp))
        kb.dma("sync",
               buf.ap().rearrange("p (kc m) -> p kc m", kc=NCH),
               p.wqkv[:, base + pair * 128: base + (pair + 1) * 128]
               .rearrange("(kc kp) m -> kp kc m", kp=128), inc=s)
    return (s, s.n)


def qk_prep(kb, st, sc, p, pair, which, pp):
    wb = (sc.wq_b if which == "q" else sc.wk_b)[pp]
    dst = sc.qrot if which == "q" else sc.krot
    Ctab = kb.pconst("c_rotCp" if which == "q" else "c_rotCm")
    Stab = kb.pconst("c_rotS2p" if which == "q" else "c_rotS2m")
    blk2 = kb.const("c_blk2")
    tiny = kb.const("c_tiny")
    scl = p.qs8 if which == "q" else p.ks
    sc.gwait(kb, "vector", ("u",))
    kb.wait("vector", kb.DV)

    def mm_stage(tb):
        ps = kb.bank(tb)
        kb.use_bank(tb, "tensor")
        for kc in range(NCH):
            kb.op("tensor", lambda E, kc=kc, tb=tb, ps=ps:
                  E.matmul(ps, wb.ap()[:, kc * 128:(kc + 1) * 128],
                           st.zbs(kc, shifted=True)[:, tb * TB:(tb + 1) * TB],
                           start=(kc == 0), stop=(kc == NCH - 1)),
                  inc=(kb.PE if kc == NCH - 1 else None))

    def norm_stage(tb):
        ps = kb.bank(tb)
        nb_ = 4 + tb
        sc.gwait(kb, "scalar", ("sq", tb % 2))
        kb.wait("scalar", kb.PE)
        kb.op("scalar", lambda E, ps=ps, d=sc.sq_pp[tb % 2]:
              E.activation(d.ap(), ps, AF.Square), inc=kb.AC)
        nps = kb.bank(nb_)
        kb.use_bank(nb_, "tensor")
        kb.wait("tensor", kb.AC)
        kb.op("tensor", lambda E, nps=nps, d=sc.sq_pp[tb % 2]:
              E.matmul(nps, blk2.ap(), d.ap(), start=True, stop=True), inc=kb.PE)
        sc.gset(kb, ("sq", tb % 2), kb.PE)
        sc.gwait(kb, "scalar", ("l2", tb % 2))
        kb.wait("scalar", kb.PE)
        kb.op("scalar", lambda E, nps=nps, d=sc.l2_pp[tb % 2]:
              E.activation(d.ap(), nps, AF.Ln, bias=tiny.ap()), inc=kb.AC)
        kb.guard_bank(nb_, kb.AC)
        sc.gwait(kb, "scalar", ("rn", tb % 2))
        kb.drain("scalar")
        kb.op("scalar", lambda E, d=sc.rn_pp[tb % 2], s=sc.l2_pp[tb % 2]:
              E.activation(d.ap(), s.ap(), AF.Exp, scale=-0.5), inc=kb.AC)
        sc.gset(kb, ("l2", tb % 2), kb.AC)
        kb.wait("vector", kb.AC)
        kb.op("vector", lambda E, ps=ps, tb=tb, s=scl, r=sc.rn_pp[tb % 2]:
              E.scalar_tensor_tensor(sc.u.ap()[:, tb * TB:(tb + 1) * TB], ps, s.ap(),
                                     r.ap(), ALU.mult, ALU.mult), inc=kb.DV)
        kb.guard_bank(tb, kb.DV)
        sc.gset(kb, ("rn", tb % 2), kb.DV)

    mm_stage(0)
    for tb in range(NTB):
        if tb + 1 < NTB:
            mm_stage(tb + 1)
        norm_stage(tb)
    sc.gwait(kb, "vector", ("rot", which))
    kb.wait("vector", kb.DV)
    kb.op("vector", lambda E: E.tensor_tensor(dst.ap()[:, 0:T], sc.u.ap(),
                                              Ctab.ap()[:, 0:T], ALU.mult), inc=kb.DV)
    sc.gset(kb, ("u",), kb.DV)
    sc.gwait(kb, "vector", ("s",))
    kb.op("vector", lambda E: E.tensor_tensor(sc.s.ap(), sc.u.ap(),
                                              Stab.ap()[:, 0:T], ALU.mult), inc=kb.DV)
    kb.wait("gpsimd", kb.DV)
    rs = sc.rsem[0 if which == "q" else 1]
    for blk in range(4):
        src_b = (blk // 2) * 2 + (1 - blk % 2)
        kb.dma("gpsimd", dst.ap()[blk * 32:(blk + 1) * 32, 0:T],
               sc.s.ap()[src_b * 32:(src_b + 1) * 32, :], inc=rs, accum=True)
    sc.gset(kb, ("s",), rs)
    sc.gd[("rotrdy", which)] = (rs, rs.n)


def v_prep(kb, st, sc, pair, pp):
    wb = sc.wv_b[pp]
    sc.gwait(kb, "scalar", ("rotrdy", "q"))
    sc.gwait(kb, "scalar", ("rotrdy", "k"))
    for tt in range(16):
        b = tt % 2
        ps = kb.bank(b, rows=128, cols=128)
        kb.use_bank(b, "tensor")
        for kc in range(NCH):
            kb.op("tensor", lambda E, kc=kc, tt=tt, ps=ps:
                  E.matmul(ps, st.zbs(kc, shifted=True)[:, tt * 128:(tt + 1) * 128],
                           wb.ap()[:, kc * 128:(kc + 1) * 128],
                           start=(kc == 0), stop=(kc == NCH - 1)),
                  inc=(kb.PE if kc == NCH - 1 else None))
        sc.gwait(kb, "scalar", ("vtm",)) if tt == 0 else None
        kb.wait("scalar", kb.PE)
        kb.op("scalar", lambda E, ps=ps, d=sc.v_tm[tt]:
              E.activation(d.ap(), ps, AF.Copy), inc=kb.AC)
        kb.guard_bank(b, kb.AC)


DBG = "full"

def local_attn_pair(kb, st, sc, pair):
    idn = kb.const("c_idn")
    tri = kb.const("c_tri")
    tri2 = kb.const("c_tri2")
    negI = kb.const("c_negI")
    out = sc.out_all[pair]

    def stage_A(it):
        w, hl, sidx = it["w"], it["hl"], it["sidx"]
        s2 = sidx % 2
        for hh in range(2):
            q0 = w * 256 + hh * 128
            mychunks = [(kc, kind) for kc, kind in
                        zip(range(2 * w + hh - 2, 2 * w + hh + 1),
                            ("tri2", "full", "tri")) if kc >= 0]
            it["chunks"][hh] = mychunks
            ncol = 128 * len(mychunks)
            it["ncol"][hh] = ncol
            sb = hh
            kb.use_bank(sb, "tensor")
            if w == 0 and hl == 0 and hh == 0:
                sc.gwait(kb, "tensor", ("rotrdy", "q"))
                sc.gwait(kb, "tensor", ("rotrdy", "k"))
                kb.wait("tensor", kb.AC)
            mms = []
            for idx, (kc, kind) in enumerate(mychunks):
                mms.append(("s", idx, kc))
            for idx, (kc, kind) in enumerate(mychunks):
                if kind != "full":
                    mms.append(("m", idx, kind))
            for n_, m in enumerate(mms):
                st_, en_ = (n_ == 0), (n_ == len(mms) - 1)
                if m[0] == "s":
                    _, idx, kc = m
                    kb.op("tensor", lambda E, idx=idx, kc=kc, hl=hl, q0=q0, sb=sb,
                          st_=st_, en_=en_:
                          E.matmul(kb.bank(sb, cols=128, coloff=idx * 128),
                                   sc.qrot.ap()[hl * 64:(hl + 1) * 64, q0:q0 + 128],
                                   sc.krot.ap()[hl * 64:(hl + 1) * 64,
                                                kc * 128:(kc + 1) * 128],
                                   start=st_, stop=en_),
                          inc=(kb.PE if en_ else None))
                else:
                    _, idx, kind = m
                    tm = tri if kind == "tri" else tri2
                    kb.op("tensor", lambda E, idx=idx, tm=tm, sb=sb, st_=st_, en_=en_:
                          E.matmul(kb.bank(sb, cols=128, coloff=idx * 128),
                                   tm.ap(), negI.ap(), start=st_, stop=en_),
                          inc=(kb.PE if en_ else None))
            S = kb.bank(sb, cols=ncol)
            ps_ = 2 * s2 + hh
            sc.gwait(kb, "vector", ("nb", hh))
            kb.wait("vector", kb.PE)
            kb.op("vector", lambda E, S=S, d=sc.nb_pp[hh]:
                  E.reduce_max(d.ap(), S, axis=AX.X, negate=True), inc=kb.DV)
            sc.gwait(kb, "scalar", ("P", ps_))
            sc.gwait(kb, "scalar", ("sm", ps_))
            kb.wait("scalar", kb.DV)
            kb.op("scalar", lambda E, S=S, ps_=ps_, hh=hh, ncol=ncol:
                  E.activation(sc.P_pp[ps_].ap()[:, 0:ncol], S, AF.Exp,
                               bias=sc.nb_pp[hh].ap(),
                               accum_out=sc.sm_pp[ps_].ap()), inc=kb.AC)
            kb.guard_bank(sb, kb.AC)
            sc.gset(kb, ("nb", hh), kb.AC)
            it["acexp"][hh] = kb.AC.n

    def stage_BC(it):
        w, hl, sidx = it["w"], it["hl"], it["sidx"]
        s2 = sidx % 2
        kcs = [kc for kc in range(2 * w - 2, 2 * w + 2) if kc >= 0]
        dvp2 = {}
        for hh in range(2):
            ncol = it["ncol"][hh]
            ps_ = 2 * s2 + hh
            sc.gwait(kb, "vector", ("rs", hh))
            kb.wait("vector", kb.AC, it["acexp"][hh])
            kb.op("vector", lambda E, hh=hh, ps_=ps_:
                  E.reciprocal(sc.rs_pp[hh].ap(), sc.sm_pp[ps_].ap()), inc=kb.DV)
            sc.gset(kb, ("sm", ps_), kb.DV)
            sc.gwait(kb, "vector", ("P2", hh))
            kb.wait("vector", kb.DV)
            kb.op("vector", lambda E, hh=hh, ps_=ps_, ncol=ncol:
                  E.tensor_scalar_mul(sc.P2[hh].ap()[:, 0:ncol],
                                      sc.P_pp[ps_].ap()[:, 0:ncol],
                                      sc.rs_pp[hh].ap()), inc=kb.DV)
            sc.gset(kb, ("rs", hh), kb.DV)
            sc.gset(kb, ("P", ps_), kb.DV)
            dvp2[hh] = kb.DV.n
        for hh in range(2):
            mychunks = it["chunks"][hh]
            tb_ = 2 + hh
            kb.use_bank(tb_, "tensor")
            kb.wait("tensor", kb.DV, dvp2[hh])
            for idx, (kc, kind) in enumerate(mychunks):
                kb.op("tensor", lambda E, idx=idx, tb_=tb_, hh=hh:
                      E.transpose(kb.ps_view(128, 0, 512 * tb_ + idx * 64, 64, BF16),
                                  sc.P2[hh].ap()[:, idx * 128:(idx + 1) * 128],
                                  idn.ap()),
                      inc=(kb.PE if idx == len(mychunks) - 1 else None))
            sc.gset(kb, ("P2", hh), kb.PE)
            eng = "vector" if hh == 0 else "scalar"
            for idx, (kc, kind) in enumerate(mychunks):
                slot = kc - (2 * w - 2)
                sc.gwait(kb, eng, ("PT", slot, hh))
                kb.wait(eng, kb.PE)
                src_ap = kb.ps_view(128, 0, 512 * tb_ + idx * 64, 64, BF16)
                dst_ap = sc.PT[slot].ap()[:, hh * 128:(hh + 1) * 128]
                if eng == "vector":
                    kb.op("vector", lambda E, s=src_ap, d=dst_ap:
                          E.tensor_copy(d, s), inc=kb.DV)
                    kb.guard_bank(tb_, kb.DV)
                else:
                    kb.op("scalar", lambda E, s=src_ap, d=dst_ap:
                          E.activation(d, s, AF.Copy), inc=kb.AC)
                    kb.guard_bank(tb_, kb.AC)
        ob = 4 + sidx % 2
        kb.use_bank(ob, "tensor")
        kb.wait("tensor", kb.DV)
        kb.wait("tensor", kb.AC)
        if w == 0:
            contribs = {0: [0], 1: [0, 1]}
        else:
            contribs = {0: [kc for kc in kcs if kc <= 2 * w],
                        1: [kc for kc in kcs if kc >= 2 * w - 1]}
        nmm = sum(len(v) for v in contribs.values())
        n_ = 0
        for r in (0, 1):
            lst = contribs[r]
            for i, kc in enumerate(lst):
                slot = kc - (2 * w - 2)
                n_ += 1
                kb.op("tensor", lambda E, kc=kc, hl=hl, slot=slot, r=r, ob=ob,
                      st_=(n_ == 1), en_=(n_ == nmm):
                      E.matmul(kb.ps_view(64, 0, 512 * ob + r * 128, 128),
                               sc.v_tm[kc].ap()[:, hl * 64:(hl + 1) * 64],
                               sc.PT[slot].ap()[:, r * 128:(r + 1) * 128],
                               start=st_, stop=en_),
                      inc=(kb.PE if n_ == nmm else None))
        for kc in kcs:
            slot = kc - (2 * w - 2)
            for hh in range(2):
                sc.gset(kb, ("PT", slot, hh), kb.PE)
        kb.wait("scalar", kb.PE)
        kb.op("scalar", lambda E, hl=hl, w=w, ob=ob:
              E.activation(out.ap()[hl * 64:(hl + 1) * 64, w * 256:(w + 1) * 256],
                           kb.ps_view(64, 0, 512 * ob, 256), AF.Copy), inc=kb.AC)
        kb.guard_bank(ob, kb.AC)

    prev = None
    sidx = 0
    for w in range(8):
        for hl in range(2):
            it = {"w": w, "hl": hl, "sidx": sidx, "chunks": {}, "ncol": {},
                  "acexp": {}}
            sidx += 1
            stage_A(it)
            if prev is not None:
                stage_BC(prev)
            prev = it
    stage_BC(prev)
    sc.gset(kb, ("rot", "q"), kb.PE)
    sc.gset(kb, ("rot", "k"), kb.PE)
    sc.gset(kb, ("vtm",), kb.PE)


def wo_residual(kb, st, sc, p):
    def evac(mi, tb, ps, mrows):
        kb.wait("vector", kb.PE)
        kb.op("vector", lambda E, mi=mi, tb=tb, ps=ps:
              E.scalar_tensor_tensor(st.zs(mi)[:, tb * TB:(tb + 1) * TB], ps, 0.0,
                                     st.zs(mi)[:, tb * TB:(tb + 1) * TB],
                                     ALU.add, ALU.add), inc=kb.DV)
    emit_proj_fm(kb, p.wo, D, D, srcb=lambda c: sc.out_all[c].ap()[:, 0:T],
                 evac=evac, evac_sem=kb.DV, mw=256, banks=(0, 1, 2, 3),
                 pre="L0_ffn_Wo")


def glu_w2(kb, st, p):
    sc = GluScratch(kb)
    gsem = kb.take_sems(2, "syncs")
    asem = kb.take_sems(2, "syncs")
    kb.wait_precast("sync", "L0_ffn_W1")
    ones512 = kb.const("c_ones512")
    DI = 2730
    njc = 22
    wg_pp, wa_pp = sc.wg_pp, sc.wa_pp
    for half in range(2):
        hoff = half * 1024

        def stream_j(j):
            m0 = j * 128
            mrows = min(128, DI - m0)
            pp = j % 2
            sc.gwait(kb, "sync", ("wg", pp))
            kb.dma("sync", wg_pp[pp].ap()[:, 0:NCH * mrows]
                   .rearrange("p (kc m) -> p kc m", kc=NCH),
                   p.w1[:, DI + m0:DI + m0 + mrows]
                   .rearrange("(kc kp) m -> kp kc m", kp=128), inc=gsem[pp])
            sc.gwait(kb, "sync", ("wa", pp))
            kb.dma("sync", wa_pp[pp].ap()[:, 0:NCH * mrows]
                   .rearrange("p (kc m) -> p kc m", kc=NCH),
                   p.w1[:, m0:m0 + mrows]
                   .rearrange("(kc kp) m -> kp kc m", kp=128), inc=asem[pp])
            return (gsem[pp].n, asem[pp].n)

        def stage_g(j, wl):
            m0 = j * 128
            mrows = min(128, DI - m0)
            pp = j % 2
            for tb in range(2):
                t0 = hoff + tb * TB
                b = tb % 2
                ps = kb.bank(b, rows=mrows)
                kb.use_bank(b, "tensor")
                if tb == 0:
                    kb.wait("tensor", gsem[pp], wl[0])
                for kc in range(NCH):
                    kb.op("tensor", lambda E, kc=kc, t0=t0, ps=ps, pp=pp, mr=mrows:
                          E.matmul(ps, wg_pp[pp].ap()[:, kc * mr:(kc + 1) * mr],
                                   st.zbs(kc)[:, t0:t0 + TB],
                                   start=(kc == 0), stop=False))
                kb.op("tensor", lambda E, ps=ps, m0=m0, mr=mrows:
                      E.matmul(ps, p.b1.ap()[0:1, DI + m0:DI + m0 + mr],
                               ones512.ap(), start=False, stop=True), inc=kb.PE)
                sc.gwait(kb, "scalar", ("e1", tb % 2))
                kb.wait("scalar", kb.PE)
                kb.op("scalar", lambda E, ps=ps, d=sc.e1_pp[tb % 2], mr=mrows:
                      E.activation(d.ap()[0:mr, :], ps, AF.Erf,
                                   scale=0.7071067811865476), inc=kb.AC)
                sc.gwait(kb, "scalar", ("q5", tb % 2))
                kb.op("scalar", lambda E, ps=ps, d=sc.q5_pp[tb % 2], mr=mrows:
                      E.activation(d.ap()[0:mr, :], ps, AF.Copy, scale=0.5),
                      inc=kb.AC)
                kb.guard_bank(b, kb.AC)
                sc.gwait(kb, "vector", ("r", pp)) if tb == 0 else None
                kb.wait("vector", kb.AC)
                kb.op("vector", lambda E, tb=tb, pp=pp, mr=mrows:
                      E.scalar_tensor_tensor(
                          sc.r_pp[pp].ap()[0:mr, tb * TB:(tb + 1) * TB],
                          sc.e1_pp[tb % 2].ap()[0:mr, :], 1.0,
                          sc.q5_pp[tb % 2].ap()[0:mr, :],
                          ALU.add, ALU.mult), inc=kb.DV)
                sc.gset(kb, ("e1", tb % 2), kb.DV)
                sc.gset(kb, ("q5", tb % 2), kb.DV)
            sc.gset(kb, ("wg", pp), kb.PE)

        def stage_a(j, wl):
            m0 = j * 128
            mrows = min(128, DI - m0)
            pp = j % 2
            for tb in range(2):
                t0 = hoff + tb * TB
                b = 2 + (2 * j + tb) % 4
                ps = kb.bank(b, rows=mrows)
                kb.use_bank(b, "tensor")
                if tb == 0:
                    kb.wait("tensor", asem[pp], wl[1])
                for kc in range(NCH):
                    kb.op("tensor", lambda E, kc=kc, t0=t0, ps=ps, pp=pp, mr=mrows:
                          E.matmul(ps, wa_pp[pp].ap()[:, kc * mr:(kc + 1) * mr],
                                   st.zbs(kc)[:, t0:t0 + TB],
                                   start=(kc == 0), stop=False))
                kb.op("tensor", lambda E, ps=ps, m0=m0, mr=mrows:
                      E.matmul(ps, p.b1.ap()[0:1, m0:m0 + mr],
                               ones512.ap(), start=False, stop=True), inc=kb.PE)
                kb.wait("vector", kb.PE)
                if tb == 0:
                    kb.wait("vector", kb.DV)
                kb.op("vector", lambda E, ps=ps, j=j, tb=tb, pp=pp, mr=mrows:
                      E.tensor_tensor(sc.h[j].ap()[0:mr, tb * TB:(tb + 1) * TB],
                                      ps, sc.r_pp[pp].ap()[0:mr,
                                                           tb * TB:(tb + 1) * TB],
                                      ALU.mult), inc=kb.DV)
                kb.guard_bank(b, kb.DV)
                if tb == 1:
                    sc.gset(kb, ("r", pp), kb.DV)
            sc.gset(kb, ("wa", pp), kb.PE)

        wls = {}
        wls[0] = stream_j(0)
        stage_g(0, wls[0])
        for j in range(njc):
            if j + 1 < njc:
                wls[j + 1] = stream_j(j + 1)
                stage_g(j + 1, wls[j + 1])
            if j >= 0:
                stage_a(j, wls[j])
        kb.fence_pe()
        def evac(mi, tb, ps, mrows, hoff=hoff):
            kb.wait("vector", kb.PE)
            kb.op("vector", lambda E, mi=mi, tb=tb, ps=ps, hoff=hoff:
                  E.scalar_tensor_tensor(st.zs(mi)[:, hoff + tb * TB:hoff + (tb + 1) * TB],
                                         ps, p.b2.ap()[:, mi:mi + 1],
                                         st.zs(mi)[:, hoff + tb * TB:hoff + (tb + 1) * TB],
                                         ALU.add, ALU.add), inc=kb.DV)
        emit_proj_fm(kb, p.w2, DI, D,
                     srcb=lambda c: sc.h[c].ap()[:, 0:1024],
                     evac=evac, evac_sem=kb.DV, mw=128, banks=(6, 7), ntb=2,
                     pre="L0_ffn_W2")
        kb.phase_fence()


def l0ffn_sublayer(kb, st, p):
    m_attn = kb.mark()
    sc = FfnScratch(kb)
    refresh_zb(kb, st)
    kb.phase_fence()
    for pair in range(NCH):
        pp = pair % 2
        wsem, wl = stream_pair_weights(kb, sc, p, pair, pp)
        kb.wait("tensor", wsem, wl)
        qk_prep(kb, st, sc, p, pair, "q", pp)
        qk_prep(kb, st, sc, p, pair, "k", pp)
        v_prep(kb, st, sc, pair, pp)
        for name in ("q", "k", "v"):
            sc.gd[("wb", name, pp)] = (kb.PE, kb.PE.n)
        local_attn_pair(kb, st, sc, pair)
    kb.phase_fence()
    wo_residual(kb, st, sc, p)
    kb.phase_fence()
    kb.reset(m_attn)
    refresh_zb(kb, st)
    emit_ln(kb, src=lambda c: st.zs(c, shifted=True),
            srcb=lambda c: st.zbs(c, shifted=True),
            dst=lambda c: st.zbs(c), grow=p.lng, bcol=p.lnb)
    kb.phase_fence()
    glu_w2(kb, st, p)
    kb.phase_fence()
    kb.reset(m_attn)


ROT_CONSTS = ["c_rotCp", "c_rotS2p", "c_rotCm", "c_rotS2m"]

def l0ffn_invocation(kb, st, d, xspill, nlayers=6):
    minv = kb.mark()
    kb.load_pconsts(ROT_CONSTS)
    spill = kb.take_sems(1, "sync")[0]
    kb.phase_fence()
    for c in range(NCH):
        kb.dma("sync", xspill[c * 128:(c + 1) * 128, :], st.zs(c), inc=spill)
    spill_n = spill.n
    kb.wait("vector", spill, spill_n)
    onesrow = kb.const("c_ones_rowD")
    emit_ln(kb, src=lambda c: st.zs(c), srcb=None,
            dst=lambda c: st.zs(c), grow=onesrow, bcol=None)
    refresh_zb(kb, st)
    kb.phase_fence()
    for li in range(nlayers):
        m = kb.mark()
        p = FfnParams(kb, d, li)
        l0ffn_sublayer(kb, st, p)
        kb.reset(m)
    m = kb.mark()
    xold = [kb.tile(128, T, F32) for _ in range(2)]
    tmp = [kb.tile(128, TB, F32) for _ in range(2)]
    rsems = kb.take_sems(2, "sync")
    rn0 = []
    for c in range(NCH):
        if c >= 2:
            kb.wait("sync", kb.DV, rn0[c - 2])
        kb.dma("sync", xold[c % 2].ap(), xspill[c * 128:(c + 1) * 128, :],
               inc=rsems[c % 2])
        rl = rsems[c % 2].n
        for tb in range(NTB):
            pp = tb % 2
            kb.wait("vector", rsems[c % 2], rl)
            kb.wait("vector", kb.DV)
            kb.op("vector", lambda E, c=c, tb=tb, pp=pp:
                  E.tensor_scalar_mul(tmp[pp].ap(), st.zs(c)[:, tb * TB:(tb + 1) * TB],
                                      0.5), inc=kb.DV)
            kb.wait("vector", kb.DV)
            kb.op("vector", lambda E, c=c, tb=tb, pp=pp:
                  E.tensor_tensor(st.zs(c)[:, tb * TB:(tb + 1) * TB], tmp[pp].ap(),
                                  xold[c % 2].ap()[:, tb * TB:(tb + 1) * TB],
                                  ALU.add), inc=kb.DV)
        rn0.append(kb.DV.n)
    kb.phase_fence()
    kb.reset(m)
    kb.clear_pconsts(ROT_CONSTS)
    kb.reset(minv)


"""Dense attention (L0_attn, L1_attn) and MoE (L1_ffn) emitters."""
import numpy as np
from ml_dtypes import bfloat16

SLOPES = (2.0 ** (-8.0 * np.arange(1, 17) / 16.0)).astype(np.float64)


def dense_consts():
    c = {}
    i = np.arange(128)[:, None]
    U0 = 1920
    u = np.arange(2304)[None, :]
    c["c_dmaster"] = ((u - U0) - i).astype(np.float32) * np.ones((128, 2304), np.float32)
    j = np.arange(128)[None, :]
    dm = (j - i).astype(np.float32) * np.ones((128, 128), np.float32)
    dm[np.broadcast_to(j > i, dm.shape)] = -3e11
    c["c_dmask"] = dm.astype(np.float32)
    return c


class DenseScratch:
    def __init__(self, kb, alibi, ptbufs=2):
        self.w_b = {n: [kb.tile(128, 8 * 128, BF16)] * 2
                    for n in ("q", "k", "v")}
        self.qf = kb.tile(128, T, BF16)
        self.kf = kb.tile(128, T, BF16)
        self.v_tm = [kb.tile(128, 128, BF16) for _ in range(16)]
        self.U = [kb.tile(128, T, F32) for _ in range(2)] if alibi else None
        self.P = [kb.tile(128, T, BF16) for _ in range(2)]
        self.P2 = [kb.tile(128, T, BF16) for _ in range(2 if alibi else 1)]
        self.PT = kb.tile(128, 4 * 128, BF16)
        self.pm = [kb.tile(128, 4, F32) for _ in range(2)]
        self.nb = [kb.tile(128, 1, F32) for _ in range(2)]
        self.nb8 = [kb.tile(128, 1, F32) for _ in range(2)]
        self.sm = [kb.tile(128, 1, F32) for _ in range(2)]
        self.rs = [kb.tile(128, 1, F32) for _ in range(2)]
        self.out_all = [kb.tile(128, T, BF16) for _ in range(NCH)]
        self.wsem = kb.take_sems(2, "syncs")
        self.gd = {}

    def gwait(self, kb, eng, key):
        g = self.gd.get(key)
        if g:
            kb.wait(eng, g[0], g[1])

    def gset(self, kb, key, sem):
        self.gd[key] = (sem, sem.n)


def qkv_dense(kb, sc, srcb, pair, pp, which):
    wb = sc.w_b[which][pp]
    if which in ("q", "k"):
        dst = sc.qf if which == "q" else sc.kf
        for tb in range(NTB):
            b = tb % 2
            ps = kb.bank(b)
            kb.use_bank(b, "tensor")
            for kc in range(NCH):
                kb.op("tensor", lambda E, kc=kc, tb=tb, ps=ps:
                      E.matmul(ps, wb.ap()[:, kc * 128:(kc + 1) * 128],
                               srcb(kc)[:, tb * TB:(tb + 1) * TB],
                               start=(kc == 0), stop=(kc == NCH - 1)),
                      inc=(kb.PE if kc == NCH - 1 else None))
            sc.gwait(kb, "scalar", ("qk", which)) if tb == 0 else None
            kb.wait("scalar", kb.PE)
            kb.op("scalar", lambda E, ps=ps, tb=tb:
                  E.activation(dst.ap()[:, tb * TB:(tb + 1) * TB], ps, AF.Copy),
                  inc=kb.AC)
            kb.guard_bank(b, kb.AC)
    else:
        sc.gwait(kb, "scalar", ("vtm",))
        for tt in range(16):
            b = tt % 2
            ps = kb.bank(b, rows=128, cols=128)
            kb.use_bank(b, "tensor")
            for kc in range(NCH):
                kb.op("tensor", lambda E, kc=kc, tt=tt, ps=ps:
                      E.matmul(ps, srcb(kc)[:, tt * 128:(tt + 1) * 128],
                               wb.ap()[:, kc * 128:(kc + 1) * 128],
                               start=(kc == 0), stop=(kc == NCH - 1)),
                      inc=(kb.PE if kc == NCH - 1 else None))
            kb.wait("scalar", kb.PE)
            kb.op("scalar", lambda E, ps=ps, d=sc.v_tm[tt]:
                  E.activation(d.ap(), ps, AF.Copy), inc=kb.AC)
            kb.guard_bank(b, kb.AC)


def dense_attn_pair(kb, sc, pair, alibi):
    idn = kb.const("c_idn")
    tri = kb.const("c_tri")
    negI = kb.const("c_negI")
    if alibi:
        dmaster = kb.pconst("c_dmaster")
        dmask = kb.pconst("c_dmask")
    out = sc.out_all[pair]
    U0 = 1920
    p2k = (lambda pq: pq) if len(sc.P2) == 2 else (lambda pq: 0)

    def stage_A(it):
        hl, qt, pq = it["hl"], it["qt"], it["pq"]
        slope = float(SLOPES[2 * pair + hl])
        nfull = qt // 4
        a_d = qt % 4
        ncol_d = (a_d + 1) * 128
        pe_b = {}
        for b_ in range(nfull + 1):
            ps_cols = 512 if b_ < nfull else ncol_d
            kb.use_bank(b_, "tensor")
            nmm = 1 if b_ < nfull else 2
            kb.op("tensor", lambda E, b_=b_, qt=qt, hl=hl, ps_cols=ps_cols,
                  en=(nmm == 1):
                  E.matmul(kb.bank(b_, cols=ps_cols),
                           sc.qf.ap()[hl * 64:(hl + 1) * 64,
                                      qt * 128:(qt + 1) * 128],
                           sc.kf.ap()[hl * 64:(hl + 1) * 64,
                                      b_ * 512:b_ * 512 + ps_cols],
                           start=True, stop=en),
                  inc=(kb.PE if nmm == 1 else None))
            if b_ == nfull:
                kb.op("tensor", lambda E, b_=b_, a_d=a_d:
                      E.matmul(kb.bank(b_, cols=128, coloff=a_d * 128),
                               tri.ap(), negI.ap(), start=False, stop=True),
                      inc=kb.PE)
            pe_b[b_] = kb.PE.n
        span = nfull * 512 + ncol_d
        it["span"] = span
        pm = sc.pm[pq]
        if alibi:
            sc.gwait(kb, "vector", ("U", pq))
            for b_ in range(nfull):
                aoff = qt - 4 * b_
                kb.wait("vector", kb.PE, pe_b[b_])
                kb.op("vector", lambda E, b_=b_, aoff=aoff, pq=pq, slope=slope:
                      E.scalar_tensor_tensor(
                          sc.U[pq].ap()[:, b_ * 512:(b_ + 1) * 512],
                          dmaster.ap()[:, U0 - 128 * aoff:U0 - 128 * aoff + 512],
                          slope, kb.bank(b_, cols=512), ALU.mult, ALU.add),
                      inc=kb.DV)
                kb.wait("vector", kb.DV)
                kb.op("vector", lambda E, b_=b_, pq=pq, pm=pm:
                      E.reduce_max(pm.ap()[:, b_:b_ + 1],
                                   sc.U[pq].ap()[:, b_ * 512:(b_ + 1) * 512],
                                   axis=AX.X), inc=kb.DV)
            kb.wait("vector", kb.PE, pe_b[nfull])
            for sub in range(a_d + 1):
                aoff = a_d - sub
                dsl = (dmask.ap() if aoff == 0 else
                       dmaster.ap()[:, U0 - 128 * aoff:U0 - 128 * aoff + 128])
                kb.op("vector", lambda E, sub=sub, pq=pq, slope=slope, dsl=dsl,
                      nfull=nfull:
                      E.scalar_tensor_tensor(
                          sc.U[pq].ap()[:, nfull * 512 + sub * 128:
                                        nfull * 512 + (sub + 1) * 128],
                          dsl, slope,
                          kb.bank(nfull, cols=128, coloff=sub * 128),
                          ALU.mult, ALU.add), inc=kb.DV)
            kb.wait("vector", kb.DV)
            kb.op("vector", lambda E, pq=pq, pm=pm, nfull=nfull, ncol_d=ncol_d:
                  E.reduce_max(pm.ap()[:, nfull:nfull + 1],
                               sc.U[pq].ap()[:, nfull * 512:
                                             nfull * 512 + ncol_d],
                               axis=AX.X), inc=kb.DV)
            for b_ in range(nfull + 1):
                kb.guard_bank(b_, kb.DV)
            src_row = sc.U[pq].ap()[:, 0:span]
        else:
            for b_ in range(nfull + 1):
                cs = 512 if b_ < nfull else ncol_d
                kb.wait("vector", kb.PE, pe_b[b_])
                kb.op("vector", lambda E, b_=b_, cs=cs, pm=pm:
                      E.reduce_max(pm.ap()[:, b_:b_ + 1],
                                   kb.ps_view(128, 0, b_ * 512, cs),
                                   axis=AX.X), inc=kb.DV)
            src_row = kb.ps_view(128, 0, 0, span)
        sc.gwait(kb, "vector", ("nb", pq))
        kb.wait("vector", kb.DV)
        kb.op("vector", lambda E, pm=pm, d=sc.nb[pq], nb_=nfull + 1:
              E.reduce_max(d.ap(), pm.ap()[:, 0:nb_], axis=AX.X), inc=kb.DV)
        kb.wait("vector", kb.DV)
        sc.gwait(kb, "vector", ("nb8", pq))
        kb.op("vector", lambda E, pq=pq:
              E.tensor_scalar_mul(sc.nb8[pq].ap(), sc.nb[pq].ap(), -0.125),
              inc=kb.DV)
        sc.gset(kb, ("nb", pq), kb.DV)
        sc.gwait(kb, "scalar", ("P", pq))
        sc.gwait(kb, "scalar", ("sm", pq))
        kb.wait("scalar", kb.DV)
        kb.op("scalar", lambda E, s=src_row, pq=pq, span=span:
              E.activation(sc.P[pq].ap()[:, 0:span], s, AF.Exp, scale=0.125,
                           bias=sc.nb8[pq].ap(), accum_out=sc.sm[pq].ap()),
              inc=kb.AC)
        sc.gset(kb, ("nb8", pq), kb.AC)
        if alibi:
            sc.gset(kb, ("U", pq), kb.AC)
        else:
            for b_ in range(qt // 4 + 1):
                kb.guard_bank(b_, kb.AC)
        it["acexp"] = kb.AC.n

    def stage_BC1(it):
        pq = it["pq"]
        span = it["span"]
        sc.gwait(kb, "vector", ("rs", pq))
        kb.wait("vector", kb.AC, it["acexp"])
        kb.op("vector", lambda E, pq=pq:
              E.reciprocal(sc.rs[pq].ap(), sc.sm[pq].ap()), inc=kb.DV)
        sc.gset(kb, ("sm", pq), kb.DV)
        sc.gwait(kb, "scalar", ("P2", p2k(pq)))
        kb.wait("scalar", kb.DV)
        kb.op("scalar", lambda E, pq=pq, span=span:
              E.activation(sc.P2[p2k(pq)].ap()[:, 0:span], sc.P[pq].ap()[:, 0:span],
                           AF.Copy, scale=sc.rs[pq].ap()), inc=kb.AC)
        sc.gset(kb, ("rs", pq), kb.AC)
        sc.gset(kb, ("P", pq), kb.AC)
        it["acp2"] = kb.AC.n

    def stage_BC2(it):
        hl, qt, pq = it["hl"], it["qt"], it["pq"]
        span = it["span"]
        nkc = span // 128
        acp2 = it["acp2"]
        tb_ = 4 + pq
        ob = 6 + pq
        eng = "vector" if pq == 0 else "scalar"
        esem = kb.DV if pq == 0 else kb.AC
        kb.use_bank(ob, "tensor")
        kb.wait("tensor", kb.AC, acp2)
        n_ = 0
        for grpb in range(0, nkc, 4):
            kb.use_bank(tb_, "tensor")
            gn = min(4, nkc - grpb)
            for ii in range(gn):
                kb.op("tensor", lambda E, grpb=grpb, ii=ii, tb_=tb_:
                      E.transpose(kb.ps_view(128, 0, 512 * tb_ + ii * 64, 64, BF16),
                                  sc.P2[p2k(pq)].ap()[:, (grpb + ii) * 128:
                                                      (grpb + ii + 1) * 128],
                                  idn.ap()),
                      inc=(kb.PE if ii == gn - 1 else None))
            kb.wait(eng, kb.PE)
            for ii in range(gn):
                kb.op(eng, lambda E, ii=ii, tb_=tb_:
                      (E.tensor_copy if pq == 0 else
                       (lambda o, i_: E.activation(o, i_, AF.Copy)))(
                          sc.PT.ap()[:, ii * 128:(ii + 1) * 128],
                          kb.ps_view(128, 0, 512 * tb_ + ii * 64, 64, BF16)),
                      inc=(esem if ii == gn - 1 else None))
            kb.guard_bank(tb_, esem)
            kb.wait("tensor", esem)
            for ii in range(gn):
                n_ += 1
                kb.op("tensor", lambda E, kc=grpb + ii, ii=ii, hl=hl, ob=ob,
                      st_=(n_ == 1), en=(n_ == nkc):
                      E.matmul(kb.ps_view(64, 0, 512 * ob, 128),
                               sc.v_tm[kc].ap()[:, hl * 64:(hl + 1) * 64],
                               sc.PT.ap()[:, ii * 128:(ii + 1) * 128],
                               start=st_, stop=en),
                      inc=(kb.PE if n_ == nkc else None))
        sc.gset(kb, ("P2", p2k(pq)), kb.PE)
        kb.wait("scalar", kb.PE)
        kb.op("scalar", lambda E, hl=hl, qt=qt, ob=ob:
              E.activation(out.ap()[hl * 64:(hl + 1) * 64,
                                    qt * 128:(qt + 1) * 128],
                           kb.ps_view(64, 0, 512 * ob, 128), AF.Copy), inc=kb.AC)
        kb.guard_bank(ob, kb.AC)

    prev = None
    for hl in range(2):
        for qt in range(16):
            it = {"hl": hl, "qt": qt, "pq": qt % 2}
            stage_A(it)
            if len(sc.P2) == 2:
                stage_BC1(it)
                if prev is not None:
                    stage_BC2(prev)
            else:
                if prev is not None:
                    stage_BC1(prev)
                    stage_BC2(prev)
            prev = it
    if len(sc.P2) != 2:
        stage_BC1(prev)
    stage_BC2(prev)
    sc.gset(kb, ("qk", "q"), kb.PE)
    sc.gset(kb, ("qk", "k"), kb.PE)
    sc.gset(kb, ("vtm",), kb.PE)


def l0_attn_slot(kb, st, d):
    onesrow = kb.const("c_ones_rowD")
    klib_emitln(kb, st, onesrow, None)
    m = kb.mark()
    kb.load_pconsts(["c_dmaster", "c_dmask"])
    sc = DenseScratch(kb, alibi=True)
    ps = kb.take_sems(1, "sync")[0]
    bp = load_cols_f32(kb, d["L0_attn_b_proj"], D, sem=ps)
    kb.wait("vector", ps, ps.n)
    kb.phase_fence()
    for pair in range(NCH):
        pp = pair % 2
        s = sc.wsem[pp]
        for which, wd in (("q", d["L0_attn_W_query"]), ("k", d["L0_attn_W_key"]),
                          ("v", d["L0_attn_W_value"])):
            kb.wait_precast("sync", {"q": "L0_attn_W_query", "k": "L0_attn_W_key",
                                     "v": "L0_attn_W_value"}[which])
            sc.gwait(kb, "sync", ("wb", which, 0))
            sc.gwait(kb, "sync", ("wb", which, 1))
            kb.dma("sync",
                   sc.w_b[which][pp].ap().rearrange("p (kc m) -> p kc m", kc=NCH),
                   wd[:, pair * 128:(pair + 1) * 128]
                   .rearrange("(kc kp) m -> kp kc m", kp=128), inc=s)
        kb.wait("tensor", s, s.n)
        qkv_dense(kb, sc, lambda c: st.zbs(c), pair, pp, "q")
        qkv_dense(kb, sc, lambda c: st.zbs(c), pair, pp, "k")
        qkv_dense(kb, sc, lambda c: st.zbs(c), pair, pp, "v")
        for name in ("q", "k", "v"):
            sc.gd[("wb", name, pp)] = (kb.PE, kb.PE.n)
        dense_attn_pair(kb, sc, pair, alibi=True)
    kb.phase_fence()

    def evac(mi, tb, psu, mrows):
        kb.wait("vector", kb.PE)
        kb.wait("vector", kb.DV)
        kb.op("vector", lambda E, mi=mi, tb=tb, psu=psu:
              E.scalar_tensor_tensor(st.zs(mi)[:, tb * TB:(tb + 1) * TB], psu,
                                     bp.ap()[:, mi:mi + 1],
                                     st.zs(mi)[:, tb * TB:(tb + 1) * TB],
                                     ALU.add, ALU.add), inc=kb.DV)
    kb.dbg = {"sc": sc}
    emit_proj_fm(kb, d["L0_attn_W_proj"], D, D,
                 srcb=lambda c: sc.out_all[c].ap()[:, 0:T],
                 evac=evac, evac_sem=kb.DV, mw=256, banks=(0, 1, 2, 3),
                 pre="L0_attn_W_proj")
    kb.phase_fence()
    kb.clear_pconsts(["c_dmaster", "c_dmask"])
    kb.reset(m)


def klib_emitln(kb, st, grow, bcol, dstf32=False):
    emit_ln(kb, src=lambda c: st.zs(c), srcb=None,
            dst=(lambda c: st.zs(c)) if dstf32 else (lambda c: st.zbs(c)),
            grow=grow, bcol=bcol)
    if dstf32:
        refresh_zb(kb, st)
    kb.phase_fence()


def l1_attn_slot(kb, st, d):
    onesrow = kb.const("c_ones_rowD")
    klib_emitln(kb, st, onesrow, None)
    m = kb.mark()
    yb = [kb.tile(128, T, BF16) for _ in range(NCH)]
    ps = kb.take_sems(1, "sync")[0]
    pg = kb.take_sems(1, "gp")[0]
    lw = load_row_bf16(kb, d["L1_attn_ln_w"], D, sem=pg)
    lb = load_cols_f32(kb, d["L1_attn_ln_b"], D, sem=ps)
    for e in ("tensor", "vector", "scalar", "gpsimd"):
        kb.wait(e, ps, ps.n)
        kb.wait(e, pg, pg.n)
    emit_ln(kb, src=None, srcb=lambda c: st.zbs(c),
            dst=lambda c: yb[c].ap()[:, 0:T], grow=lw, bcol=lb, src_is_bf16=True)
    kb.phase_fence()
    sc = DenseScratch(kb, alibi=False, ptbufs=1)
    kb.phase_fence()
    for pair in range(NCH):
        pp = pair % 2
        s = sc.wsem[pp]
        for which, wd in (("q", d["L1_attn_wq"]), ("k", d["L1_attn_wk"]),
                          ("v", d["L1_attn_wv"])):
            kb.wait_precast("sync", {"q": "L1_attn_wq", "k": "L1_attn_wk",
                                     "v": "L1_attn_wv"}[which])
            sc.gwait(kb, "sync", ("wb", which, 0))
            sc.gwait(kb, "sync", ("wb", which, 1))
            for hl in range(2):
                kb.dma("sync",
                       sc.w_b[which][pp].ap()
                       .rearrange("p (kc m) -> p kc m", kc=NCH)[:, :, hl * 64:(hl + 1) * 64],
                       wd[2 * pair + hl].rearrange("(kc kp) m -> kp kc m", kp=128),
                       inc=s)
        kb.wait("tensor", s, s.n)
        qkv_dense(kb, sc, lambda c: yb[c].ap()[:, 0:T], pair, pp, "q")
        qkv_dense(kb, sc, lambda c: yb[c].ap()[:, 0:T], pair, pp, "k")
        qkv_dense(kb, sc, lambda c: yb[c].ap()[:, 0:T], pair, pp, "v")
        for name in ("q", "k", "v"):
            sc.gd[("wb", name, pp)] = (kb.PE, kb.PE.n)
        dense_attn_pair(kb, sc, pair, alibi=False)
    kb.phase_fence()

    def evac(mi, tb, psu, mrows):
        kb.wait("vector", kb.PE)
        kb.wait("vector", kb.DV)
        kb.op("vector", lambda E, mi=mi, tb=tb, psu=psu:
              E.scalar_tensor_tensor(st.zs(mi)[:, tb * TB:(tb + 1) * TB], psu, 0.0,
                                     st.zs(mi)[:, tb * TB:(tb + 1) * TB],
                                     ALU.add, ALU.add), inc=kb.DV)
        kb.wait("vector", kb.DV)
        kb.op("vector", lambda E, mi=mi, tb=tb:
              E.scalar_tensor_tensor(st.zs(mi)[:, tb * TB:(tb + 1) * TB],
                                     st.zbs(mi)[:, tb * TB:(tb + 1) * TB], 1.0,
                                     st.zs(mi)[:, tb * TB:(tb + 1) * TB],
                                     ALU.mult, ALU.add), inc=kb.DV)
    kb.dbg = {"yb": yb, "sc": sc}
    emit_proj_fm(kb, d["L1_attn_wo"], D, D,
                 srcb=lambda c: sc.out_all[c].ap()[:, 0:T],
                 evac=evac, evac_sem=kb.DV, mw=256, banks=(0, 1, 2, 3),
                 pre="L1_attn_wo")
    kb.phase_fence()
    kb.reset(m)


def moe_slot(kb, st, d):
    onesrow = kb.const("c_ones_rowD")
    ones128 = kb.const("c_ones_row")
    ones512 = kb.const("c_ones512")
    idn = kb.const("c_idn")
    klib_emitln(kb, st, onesrow, None)
    m0 = kb.mark()
    ps_ = kb.take_sems(1, "sync")[0]
    pg_ = kb.take_sems(1, "gp")[0]
    rw = kb.tile(128, 8 * 16, BF16)
    kb.dma("gpsimd", rw.ap().rearrange("p (kc m) -> p kc m", kc=NCH)[:, :, 0:15],
           d["L1_ffn_router_W"].rearrange("(kc kp) m -> kp kc m", kp=128), inc=pg_)
    rb = load_row_bf16(kb, d["L1_ffn_router_b"], 15, sem=pg_)
    sb1r = load_row_bf16(kb, d["L1_ffn_sb1"], D, sem=pg_)
    sb2r = load_row_bf16(kb, d["L1_ffn_sb2"], D, sem=pg_)
    eb2t = kb.tile(15, D, BF16)
    kb.dma("gpsimd", eb2t.ap(), d["L1_ffn_eb2"], inc=pg_)
    for e in ("tensor", "vector", "scalar", "gpsimd"):
        kb.wait(e, pg_, pg_.n)
        kb.wait(e, ps_, ps_.n)
    cw_fm = kb.tile(16, T, BF16)
    cw_tm = kb.tile(128, 128, BF16)
    l_sb = kb.tile(128, 15, F32)
    m1 = kb.tile(128, 1, F32)
    ee = kb.tile(128, 15, F32)
    ssum = kb.tile(128, 1, F32)
    rsr = kb.tile(128, 1, F32)
    tt_ = kb.tile(128, 15, F32)
    gi = kb.tile(128, 15, F32)
    msum = kb.tile(128, 15, F32)
    kb.op("vector", lambda E: E.memset(cw_tm.ap(), 0.0), inc=kb.DV)
    gdr = {}
    for tt in range(16):
        b = tt % 2
        psl = kb.bank(b, cols=15)
        kb.use_bank(b, "tensor")
        for kc in range(NCH):
            kb.op("tensor", lambda E, kc=kc, tt=tt, psl=psl:
                  E.matmul(psl, st.zbs(kc)[:, tt * 128:(tt + 1) * 128],
                           rw.ap().rearrange("p (kc m) -> p kc m", kc=NCH)[:, kc, 0:15],
                           start=(kc == 0), stop=False))
        kb.op("tensor", lambda E, psl=psl:
              E.matmul(psl, ones128.ap(), rb.ap(), start=False, stop=True), inc=kb.PE)
        kb.wait("vector", kb.PE)
        kb.wait("vector", kb.DV)
        kb.op("vector", lambda E, psl=psl: E.tensor_copy(l_sb.ap(), psl), inc=kb.DV)
        kb.guard_bank(b, kb.DV)
        kb.wait("vector", kb.DV)
        kb.op("vector", lambda E: E.reduce_max(m1.ap(), l_sb.ap(), axis=AX.X,
                                               negate=True), inc=kb.DV)
        kb.wait("scalar", kb.DV)
        kb.op("scalar", lambda E: E.activation(ee.ap(), l_sb.ap(), AF.Exp,
                                               bias=m1.ap(), accum_out=ssum.ap()),
              inc=kb.AC)
        kb.wait("vector", kb.AC)
        kb.op("vector", lambda E: E.reciprocal(rsr.ap(), ssum.ap()), inc=kb.DV)
        kb.wait("vector", kb.DV)
        kb.op("vector", lambda E: E.tensor_copy(tt_.ap(), ee.ap()), inc=kb.DV)
        for it in range(3):
            kb.wait("vector", kb.DV)
            kb.op("vector", lambda E: E.reduce_max(m1.ap(), tt_.ap(), axis=AX.X),
                  inc=kb.DV)
            kb.wait("vector", kb.DV)
            kb.op("vector", lambda E: E.tensor_single_scalar(gi.ap(), tt_.ap(),
                                                             m1.ap(), ALU.is_ge),
                  inc=kb.DV)
            kb.wait("vector", kb.DV)
            if it == 0:
                kb.op("vector", lambda E: E.tensor_copy(msum.ap(), gi.ap()),
                      inc=kb.DV)
            else:
                kb.op("vector", lambda E: E.tensor_add(msum.ap(), msum.ap(), gi.ap()),
                      inc=kb.DV)
            if it < 2:
                kb.wait("vector", kb.DV)
                kb.op("vector", lambda E: E.scalar_tensor_tensor(
                    tt_.ap(), gi.ap(), -1e30, tt_.ap(), ALU.mult, ALU.add), inc=kb.DV)
        kb.wait("vector", kb.DV)
        g = gdr.get("cwtm")
        if g:
            kb.wait("vector", g[0], g[1])
        kb.op("vector", lambda E: E.scalar_tensor_tensor(
            cw_tm.ap()[:, 0:15], ee.ap(), rsr.ap(), msum.ap(), ALU.mult, ALU.mult),
            inc=kb.DV)
        kb.use_bank(2, "tensor")
        kb.wait("tensor", kb.DV)
        kb.op("tensor", lambda E, tt=tt:
              E.transpose(kb.ps_view(128, 0, 512 * 2, 64, BF16), cw_tm.ap(), idn.ap()),
              inc=kb.PE)
        gdr["cwtm"] = (kb.PE, kb.PE.n)
        kb.wait("vector", kb.PE)
        kb.op("vector", lambda E, tt=tt:
              E.tensor_copy(cw_fm.ap()[0:16, tt * 128:(tt + 1) * 128],
                            kb.ps_view(16, 0, 512 * 2, 64, BF16)), inc=kb.DV)
        kb.guard_bank(2, kb.DV)
    kb.phase_fence()
    he = [[kb.tile(128, T, BF16) for _ in range(NCH)] for _ in range(2)]
    hs = he[0]
    cwb = [kb.tile(128, T, BF16) for _ in range(2)]
    crow = [kb.tile(1, TB, BF16) for _ in range(2)]
    w2buf = [[kb.tile(128, 8 * 128, BF16) for _ in range(2)] for _ in range(2)]
    csem = kb.take_sems(2, "gp")
    w2sem = kb.take_sems(4, "syncs")
    gd = {}
    def gwait(eng, key):
        g = gd.get(key)
        if g:
            kb.wait(eng, g[0], g[1])
    def gset(key, sem):
        gd[key] = (sem, sem.n)

    def w1_evac_shared(mi, tb, psu, mrows):
        kb.wait("scalar", kb.PE)
        kb.op("scalar", lambda E, mi=mi, tb=tb, psu=psu:
              E.activation(hs[mi].ap()[:, tb * TB:(tb + 1) * TB], psu, AF.Relu),
              inc=kb.AC)
    emit_proj_fm(kb, d["L1_ffn_sW1"], D, D, srcb=lambda c: st.zbs(c),
                 evac=w1_evac_shared, evac_sem=kb.AC, mw=256, banks=(0, 1, 2, 3),
                 aug=(lambda mi: sb1r.ap()[0:1, mi * 128:(mi + 1) * 128],
                      lambda mi, tb: ones512.ap()), pre="L1_ffn_sW1")
    kb.fence_pe()
    def w2_evac(mi, tb, psu, mrows):
        kb.wait("vector", kb.PE)
        kb.wait("vector", kb.DV)
        kb.op("vector", lambda E, mi=mi, tb=tb, psu=psu:
              E.scalar_tensor_tensor(st.zs(mi)[:, tb * TB:(tb + 1) * TB], psu, 0.5,
                                     st.zs(mi)[:, tb * TB:(tb + 1) * TB],
                                     ALU.mult, ALU.add), inc=kb.DV)
    emit_proj_fm(kb, d["L1_ffn_sW2"], D, D, srcb=lambda c: hs[c].ap()[:, 0:T],
                 evac=w2_evac, evac_sem=kb.DV, mw=256, banks=(4, 5, 6, 7),
                 aug=(lambda mi: sb2r.ap()[0:1, mi * 128:(mi + 1) * 128],
                      lambda mi, tb: ones512.ap()), pre="L1_ffn_sW2")
    kb.phase_fence()
    eb1r = [kb.tile(1, D, BF16) for _ in range(2)]
    ebsem = kb.take_sems(2, "gp")
    groups = [(0, 1), (2, 3), (4, 5), (6, 7), (8, 9), (10, 11), (12, 13), (14,)]
    for gidx, grp in enumerate(groups):
        for ei, e in enumerate(grp):
            pp = ei
            gwait("gpsimd", ("eb1", pp))
            kb.dma("gpsimd", eb1r[pp].ap(),
                   d["L1_ffn_eb1"][e].rearrange("(o n) -> o n", o=1), inc=ebsem[pp])
            en_ = ebsem[pp].n
            for tb in range(NTB):
                gwait("gpsimd", ("crow", pp)) if tb == 0 else None
                kb.wait("gpsimd", kb.PE)
                kb.dma("gpsimd", crow[pp].ap(),
                       cw_fm.ap()[e:e + 1, tb * TB:(tb + 1) * TB], inc=csem[pp])
                cn = csem[pp].n
                bb = 2 + tb % 2
                kb.use_bank(bb, "tensor")
                kb.wait("tensor", csem[pp], cn)
                kb.op("tensor", lambda E, tb=tb, pp=pp, bb=bb:
                      E.matmul(kb.bank(bb), ones128.ap(),
                               crow[pp].ap(), start=True, stop=True), inc=kb.PE)
                gwait("scalar", ("cwb", pp)) if tb == 0 else None
                kb.wait("scalar", kb.PE)
                kb.op("scalar", lambda E, tb=tb, pp=pp, bb=bb:
                      E.activation(cwb[pp].ap()[:, tb * TB:(tb + 1) * TB],
                                   kb.bank(bb), AF.Copy), inc=kb.AC)
                kb.guard_bank(bb, kb.AC)
            gset(("crow", pp), kb.AC)
            kb.wait("tensor", ebsem[pp], en_)
            def w1_evac_e(mi, tb, psu, mrows, pp=pp):
                kb.wait("vector", kb.PE)
                kb.wait("vector", kb.AC)
                kb.wait("vector", kb.DV)
                kb.op("vector", lambda E, mi=mi, tb=tb, psu=psu, pp=pp:
                      E.scalar_tensor_tensor(he[pp][mi].ap()[:, tb * TB:(tb + 1) * TB],
                                             psu, 0.0,
                                             cwb[pp].ap()[:, tb * TB:(tb + 1) * TB],
                                             ALU.max, ALU.mult), inc=kb.DV)
            emit_proj_fm(kb, d["L1_ffn_eW1"][e], D, D, srcb=lambda c: st.zbs(c),
                         evac=w1_evac_e, evac_sem=kb.DV, mw=256, banks=(0, 1, 2, 3),
                         aug=(lambda mi, pp=pp: eb1r[pp].ap()[0:1, mi * 128:(mi + 1) * 128],
                              lambda mi, tb: ones512.ap()), pre="L1_ffn_eW1")
            gset(("eb1", pp), kb.PE)
            gset(("cwb", pp), kb.DV)
        kb.fence_pe()
        w2n = [0, 0]
        for fc in range(NCH):
            fp = fc % 2
            for ei, e in enumerate(grp):
                gwait("sync", ("w2", ei, fp))
                kb.wait_precast("sync", "L1_ffn_eW2")
                kb.dma("sync",
                       w2buf[ei][fp].ap().rearrange("p (kc m) -> p kc m", kc=NCH),
                       d["L1_ffn_eW2"][e][:, fc * 128:(fc + 1) * 128]
                       .rearrange("(kc kp) m -> kp kc m", kp=128),
                       inc=w2sem[ei * 2 + fp])
            wn = [w2sem[ei * 2 + fp].n for ei in range(len(grp))]
            for tb in range(NTB):
                bb = 4 + (fc * NTB + tb) % 4
                kb.use_bank(bb, "tensor")
                if tb == 0:
                    for ei in range(len(grp)):
                        kb.wait("tensor", w2sem[ei * 2 + fp], wn[ei])
                nmm = len(grp) * NCH + (1 if gidx == 0 else 0)
                n_ = 0
                for ei, e in enumerate(grp):
                    for kc in range(NCH):
                        n_ += 1
                        en_f = (n_ == nmm)
                        kb.op("tensor", lambda E, ei=ei, kc=kc, tb=tb, fp=fp, bb=bb,
                              st_=(n_ == 1), en=en_f:
                              E.matmul(kb.bank(bb),
                                       w2buf[ei][fp].ap()[:, kc * 128:(kc + 1) * 128],
                                       he[ei][kc].ap()[:, tb * TB:(tb + 1) * TB],
                                       start=st_, stop=en),
                              inc=(kb.PE if en_f else None))
                if gidx == 0:
                    n_ += 1
                    kb.op("tensor", lambda E, fc=fc, tb=tb, bb=bb:
                          E.matmul(kb.bank(bb), eb2t.ap()[0:15, fc * 128:(fc + 1) * 128],
                                   cw_fm.ap()[0:15, tb * TB:(tb + 1) * TB],
                                   start=False, stop=True), inc=kb.PE)
                w2_evac(fc, tb, kb.bank(bb), 128)
                kb.guard_bank(bb, kb.DV)
            for ei in range(len(grp)):
                gset(("w2", ei, fp), kb.PE)
        kb.phase_fence()
    kb.reset(m0)


import concourse.bass as bass
from concourse.bass_utils import run_bass_kernel_spmd

_hostc().update(dense_consts())

FFN_CONSTS = ["c_idn", "c_tri", "c_tri2", "c_negI", "c_ones_col", "c_ones_row",
              "c_ones512", "c_ones_rowD", "c_blk2", "c_tiny", "c_eps"]
BASE_CONSTS = FFN_CONSTS
L0A_CONSTS = ["c_idn", "c_tri", "c_negI", "c_ones_col", "c_ones_row", "c_ones_rowD",
              "c_eps"]
L1A_CONSTS = ["c_idn", "c_tri", "c_negI", "c_ones_col", "c_ones_rowD", "c_eps"]
MOE_CONSTS = ["c_idn", "c_ones_col", "c_ones_row", "c_ones_rowD", "c_ones512",
              "c_eps"]

FFN_W = ["L0_ffn_Wqkv", "L0_ffn_Wo", "L0_ffn_W1", "L0_ffn_W2", "L0_ffn_q_scale",
         "L0_ffn_k_scale", "L0_ffn_ln_g", "L0_ffn_ln_b", "L0_ffn_b1", "L0_ffn_b2"]
L0A_W = ["L0_attn_W_query", "L0_attn_W_key", "L0_attn_W_value", "L0_attn_W_proj",
         "L0_attn_b_proj"]
L1A_W = ["L1_attn_ln_w", "L1_attn_ln_b", "L1_attn_wq", "L1_attn_wk", "L1_attn_wv",
         "L1_attn_wo"]
MOE_W = ["L1_ffn_router_W", "L1_ffn_router_b", "L1_ffn_sW1", "L1_ffn_sb1",
         "L1_ffn_sW2", "L1_ffn_sb2", "L1_ffn_eW1", "L1_ffn_eb1", "L1_ffn_eW2",
         "L1_ffn_eb2"]

SHAPES = {
    "L0_ffn_Wqkv": [6, D, 3072], "L0_ffn_Wo": [6, D, D], "L0_ffn_W1": [6, D, 5460],
    "L0_ffn_W2": [6, 2730, D], "L0_ffn_q_scale": [6, 64], "L0_ffn_k_scale": [6, 64],
    "L0_ffn_ln_g": [6, D], "L0_ffn_ln_b": [6, D], "L0_ffn_b1": [6, 5460],
    "L0_ffn_b2": [6, D],
    "L0_attn_W_query": [D, D], "L0_attn_W_key": [D, D], "L0_attn_W_value": [D, D],
    "L0_attn_W_proj": [D, D], "L0_attn_b_proj": [D],
    "L1_attn_ln_w": [D], "L1_attn_ln_b": [D], "L1_attn_wq": [16, D, 64],
    "L1_attn_wk": [16, D, 64], "L1_attn_wv": [16, D, 64], "L1_attn_wo": [D, D],
    "L1_ffn_router_W": [D, 15], "L1_ffn_router_b": [15], "L1_ffn_sW1": [D, D],
    "L1_ffn_sb1": [D], "L1_ffn_sW2": [D, D], "L1_ffn_sb2": [D],
    "L1_ffn_eW1": [15, D, D], "L1_ffn_eb1": [15, D], "L1_ffn_eW2": [15, D, D],
    "L1_ffn_eb2": [15, D],
}


BIG_W = ["L0_ffn_Wqkv", "L0_ffn_W1", "L0_ffn_W2", "L0_ffn_Wo",
         "L0_attn_W_query", "L0_attn_W_key", "L0_attn_W_value", "L0_attn_W_proj",
         "L1_ffn_sW1", "L1_ffn_sW2", "L1_ffn_eW1", "L1_ffn_eW2",
         "L1_attn_wq", "L1_attn_wk", "L1_attn_wv", "L1_attn_wo"]


def _prog_common(wnames, consts):
    nc = bass.Bass("TRN2", target_bir_lowering=False)
    kb = KB(nc)
    d = {n: kb.din(n, SHAPES[n], F32) for n in wnames}
    xd = kb.din("x", [D, T], F32)
    outz = kb.dout("outz", [D, T], F32)
    st = St(kb)
    for c in range(NCH):
        kb.dma("sync", st.zs(c), xd[c * 128:(c + 1) * 128, :])
    kb.preload_consts(consts)
    kb.barrier_loads()
    kb.precast_weights(d, [n for n in BIG_W if n in wnames], SHAPES)
    return nc, kb, d, st, outz


def _prog_finish(nc, kb, st, outz):
    STm = kb.sem("STOUT")
    for c in range(NCH):
        kb.wait("sync", kb.DV)
        kb.wait("sync", kb.GP)
        kb.dma("sync", outz[c * 128:(c + 1) * 128, :], st.zs(c), inc=STm)
    kb.wait("sync", STm)
    kb.finalize()
    return nc, kb


def build_ffn_prog():
    nc, kb, d, st, outz = _prog_common(FFN_W, FFN_CONSTS)
    xspill = kb.dtmp("xspill", [D, T], F32)
    l0ffn_invocation(kb, st, d, xspill, 6)
    return _prog_finish(nc, kb, st, outz)


def build_l0attn_prog():
    nc, kb, d, st, outz = _prog_common(L0A_W, L0A_CONSTS)
    l0_attn_slot(kb, st, d)
    return _prog_finish(nc, kb, st, outz)


def build_l1attn_prog():
    nc, kb, d, st, outz = _prog_common(L1A_W, L1A_CONSTS)
    l1_attn_slot(kb, st, d)
    return _prog_finish(nc, kb, st, outz)


def build_moe_prog():
    nc, kb, d, st, outz = _prog_common(MOE_W, MOE_CONSTS)
    moe_slot(kb, st, d)
    return _prog_finish(nc, kb, st, outz)


def build_fused_prog():
    allw = FFN_W + L0A_W + L1A_W + MOE_W
    nc, kb, d, st, outz = _prog_common(allw, BASE_CONSTS)
    xspill = kb.dtmp("xspill", [D, T], F32)
    l0ffn_invocation(kb, st, d, xspill, 6)
    l0_attn_slot(kb, st, d)
    l0ffn_invocation(kb, st, d, xspill, 6)
    moe_slot(kb, st, d)
    l1_attn_slot(kb, st, d)
    moe_slot(kb, st, d)
    return _prog_finish(nc, kb, st, outz)


def kernel(**inputs):
    xin = inputs["L0_attn_x"]
    ncores = 8
    xs = [np.ascontiguousarray(xin[i].T).astype(np.float32, copy=False)
          for i in range(ncores)]
    nc, kb = build_fused_prog()
    base = dict(kb.inputs)
    for n in FFN_W + L0A_W + L1A_W + MOE_W:
        base[n] = inputs[n]
    in_maps = [dict(base, x=xs[i]) for i in range(ncores)]
    res = run_bass_kernel_spmd(nc, in_maps, core_ids=list(range(ncores)))
    out = np.stack([np.asarray(res.results[i]["outz"]).T for i in range(ncores)],
                   axis=0).astype(np.float32)
    return out


def kernel_unfused(**inputs):
    xin = inputs["L0_attn_x"]
    ncores = 8
    xs = [np.ascontiguousarray(xin[i].T).astype(np.float32, copy=False)
          for i in range(ncores)]
    progs = {}
    progs["ffn"] = build_ffn_prog()
    progs["l0a"] = build_l0attn_prog()
    progs["moe"] = build_moe_prog()
    progs["l1a"] = build_l1attn_prog()
    wsets = {"ffn": FFN_W, "l0a": L0A_W, "moe": MOE_W, "l1a": L1A_W}
    for step in ("ffn", "l0a", "ffn", "moe", "l1a", "moe"):
        nc, kb = progs[step]
        base = dict(kb.inputs)
        for n in wsets[step]:
            base[n] = inputs[n]
        in_maps = [dict(base, x=xs[i]) for i in range(ncores)]
        res = run_bass_kernel_spmd(nc, in_maps, core_ids=list(range(ncores)))
        xs = [np.asarray(res.results[i]["outz"]) for i in range(ncores)]
    out = np.stack([xs[i].T for i in range(ncores)], axis=0).astype(np.float32)
    return out
